# version 10
# baseline (speedup 1.0000x reference)
import numpy as np
import concourse.bass as bass
import concourse.mybir as mybir
from concourse.bass_utils import run_bass_kernel_spmd

F32 = mybir.dt.float32
BF16 = mybir.dt.bfloat16
I32 = mybir.dt.int32
AF = mybir.ActivationFunctionType
ALU = mybir.AluOpType
AX = mybir.AxisListType

NCORES = 8
T = 2048
BPC = 2
TC = T * BPC
D = 1024
NB = D // 128
LNEPS = 1e-5


class Sem:
    def __init__(self, h):
        self.h = h
        self.v = 0


class Prog:
    ENGINES = ("sync", "tensor", "vector", "scalar", "gpsimd")

    def __init__(self, nc):
        self.nc = nc
        self.q = {e: [] for e in self.ENGINES}
        self._sems = {}
        self.plan = []
        self._semname = {}
        globals()["_LAST_PROG"] = self

    def sem(self, name):
        if name not in self._sems:
            self._sems[name] = Sem(self.nc.alloc_semaphore(name))
            self._semname[id(self._sems[name])] = name
        return self._sems[name]

    def do(self, eng, meth, *a, inc=None, n=1, **kw):
        rec = {"kind": "op", "eng": eng, "meth": meth, "args": a, "kw": dict(kw),
               "incs": [], "qi": len(self.q[eng])}
        if inc is not None:
            inc.v += n
            rec["incs"].append((self._semname[id(inc)], n, inc.v))
            def f(e, a=a, kw=kw, inc=inc, n=n, meth=meth):
                ins = getattr(e, meth)(*a, **kw)
                ins.then_inc(inc.h, n)
                return ins
        else:
            def f(e, a=a, kw=kw, meth=meth):
                return getattr(e, meth)(*a, **kw)
        self.q[eng].append(f)
        self.plan.append(rec)

    def lastinc(self, eng, sem, n=1):
        sem.v += n
        f0 = self.q[eng][-1]
        for rec in reversed(self.plan):
            if rec["kind"] == "op" and rec["eng"] == eng and rec["qi"] == len(self.q[eng]) - 1:
                rec["incs"].append((self._semname[id(sem)], n, sem.v))
                break
        def f(e, f0=f0, sem=sem, n=n):
            ins = f0(e)
            ins.then_inc(sem.h, n)
            return ins
        self.q[eng][-1] = f

    def wait(self, eng, sem, v=None):
        vv = sem.v if v is None else v
        self.plan.append({"kind": "wait", "eng": eng, "sem": self._semname[id(sem)],
                          "v": vv, "full": (v is None)})
        self.q[eng].append(lambda e, h=sem.h, vv=vv: e.wait_ge(h, vv))

    def sbuf(self, stack, name, shape, dtype):
        self._tn = getattr(self, "_tn", 0) + 1
        return stack.enter_context(self.nc.sbuf_tensor(f"{name}_t{self._tn}", shape, dtype))

    def psum(self, stack, name, shape, dtype):
        self._tn = getattr(self, "_tn", 0) + 1
        return stack.enter_context(self.nc.psum_tensor(f"{name}_t{self._tn}", shape, dtype))

    def barrier(self):
        sb = self.sem("bar")
        for e in self.ENGINES:
            self.do(e, "nop", inc=sb)
        for e in self.ENGINES:
            self.wait(e, sb)

    def emit(self):
        with self.nc.Block() as block:
            for e in self.ENGINES:
                if self.q[e]:
                    def body(eng, e=e):
                        for f in self.q[e]:
                            f(eng)
                    getattr(block, e)(body)


def _mkconsts(nc, p):
    ident = nc.alloc_sbuf_tensor("c_ident", [128, 128], BF16)
    ident32 = nc.alloc_sbuf_tensor("c_ident32", [128, 128], F32)
    onest = nc.alloc_sbuf_tensor("c_onest", [128, 128], BF16)
    onesc = nc.alloc_sbuf_tensor("c_onesc", [128, 1], BF16)
    onesr = nc.alloc_sbuf_tensor("c_onesr", [1, 128], F32)
    epsc = nc.alloc_sbuf_tensor("c_epsc", [128, 1], F32)
    eps6c = nc.alloc_sbuf_tensor("c_eps6c", [128, 1], F32)
    tinyc = nc.alloc_sbuf_tensor("c_tiny", [128, 1], F32)
    s = p.sem("init")
    p.do("gpsimd", "memset", ident[:, :], 1.0, inc=s)
    p.do("gpsimd", "memset", ident32[:, :], 1.0, inc=s)
    p.do("gpsimd", "memset", onest[:, :], 1.0, inc=s)
    p.do("gpsimd", "memset", onesc[:, :], 1.0, inc=s)
    p.do("gpsimd", "memset", onesr[:, :], 1.0, inc=s)
    p.do("gpsimd", "memset", epsc[:, :], LNEPS, inc=s)
    p.do("gpsimd", "memset", eps6c[:, :], 1e-6, inc=s)
    p.do("gpsimd", "memset", tinyc[:, :], 1e-30, inc=s)
    p.wait("gpsimd", s)
    p.do("gpsimd", "affine_select", ident[:, :], ident[:, :], [[1, 128]],
         ALU.is_equal, 0.0, base=0, channel_multiplier=-1, inc=s)
    p.wait("gpsimd", s)
    p.do("gpsimd", "affine_select", ident32[:, :], ident32[:, :], [[1, 128]],
         ALU.is_equal, 0.0, base=0, channel_multiplier=-1, inc=s)
    return dict(ident=ident, ident32=ident32, onest=onest, onesc=onesc,
                onesr=onesr, epsc=epsc, eps6c=eps6c, tinyc=tinyc)


def _ln_stats_chunk(p, onest, xap3, ps_sum, ps_ssq, xsq, mean, alpha, s_mm, s_ve, s_se, epsc=None):
    L = xap3.shape[2]
    p.do("vector", "tensor_mul", xsq[:, :, :], xap3, xap3, inc=s_ve)
    for cb in range(NB):
        p.do("tensor", "matmul", ps_sum, onest[:, :], xap3[:, cb, :],
             start=(cb == 0), stop=(cb == NB - 1), inc=(s_mm if cb == NB - 1 else None))
    p.wait("tensor", s_ve)
    for cb in range(NB):
        p.do("tensor", "matmul", ps_ssq, onest[:, :], xsq[:, cb, :],
             start=(cb == 0), stop=(cb == NB - 1), inc=(s_mm if cb == NB - 1 else None))
    p.wait("vector", s_mm)
    p.do("vector", "tensor_scalar", mean, ps_sum, 1.0 / D, None, ALU.mult, inc=s_ve)
    p.wait("vector", s_ve)
    p.do("vector", "tensor_mul", alpha, mean, mean, inc=s_ve)
    p.wait("vector", s_ve)
    p.do("vector", "scalar_tensor_tensor", alpha, ps_ssq, 1.0 / D, alpha,
         ALU.mult, ALU.subtract, inc=s_ve)
    p.wait("scalar", s_ve)
    p.do("scalar", "activation", alpha, alpha, AF.Ln, bias=epsc[:, :])
    p.do("scalar", "activation", alpha, alpha, AF.Exp, scale=-0.5, inc=s_se)


def _ln_apply_chunk(p, xap3, mean, alpha, out3, tmp, s_ve, s_se):
    L = xap3.shape[2]
    mb = mean.unsqueeze(1).broadcast_to([128, NB, L])
    ab = alpha.unsqueeze(1).broadcast_to([128, NB, L])
    p.wait("vector", s_se)
    p.do("vector", "tensor_tensor", tmp[:, :, :], xap3, mb, ALU.subtract)
    p.do("vector", "tensor_tensor", out3, tmp[:, :, :], ab, ALU.mult, inc=s_ve)


def emit_stage_a(nc, p, cn, x_d, wqkv_d, wout_d, bout_d, ox_d, debug=False):
    import contextlib
    stack = contextlib.ExitStack()
    _dbg_outs = []
    def dbg(p, name, ap, when):
        if not debug:
            return
        d = nc.dram_tensor("dbg_" + name, list(ap.shape), ap.dtype, kind="ExternalOutput").ap()
        _dbg_outs.append("dbg_" + name)
        sd = p.sem("dbgst"); sb = p.sem("blk")
        for s in when:
            p.wait("sync", s)
        p.do("sync", "dma_start", out=d, in_=ap, inc=sd, n=16)
        p.wait("sync", sd)
        p.do("sync", "sem_inc", sb.h, 1)
        sb.v += 1
        for e in ("tensor", "vector", "scalar", "gpsimd"):
            p.wait(e, sb)
    ident, onest, onesc, epsc, eps6c = cn["ident"], cn["onest"], cn["onesc"], cn["epsc"], cn["eps6c"]

    xsb = p.sbuf(stack, "xsb", [128, NB * TC], BF16)
    wq = p.sbuf(stack, "wq", [128, NB * 3 * D], BF16)
    wo = p.sbuf(stack, "wo", [128, NB * D], BF16)
    bo = p.sbuf(stack, "bo", [128, NB], F32)
    xln = p.sbuf(stack, "xln", [128, NB * T], BF16)
    qfm = p.sbuf(stack, "qfm", [128, NB * 512], BF16)
    ktm = p.sbuf(stack, "ktm", [128, D], BF16)
    VW = 130
    vtm = p.sbuf(stack, "vtm", [128, NB * VW], BF16)
    kvd = p.sbuf(stack, "kvd", [128, NB * 128], BF16)
    ks2 = p.sbuf(stack, "ks2", [128, NB * 2], BF16)
    att = ktm
    afm = p.sbuf(stack, "afm", [128, NB * 512], BF16)
    rden = p.sbuf(stack, "rden", [128, 16], F32)

    mean = p.sbuf(stack, "mean", [128, 512], F32)
    alpha = p.sbuf(stack, "alpha", [128, 512], F32)
    tA = p.sbuf(stack, "tA", [128, NB * 512], F32)
    tB = p.sbuf(stack, "tB", [128, 1024], F32)
    xsq = afm

    s_ld = p.sem("ld"); s_mm = p.sem("mm"); s_ve = p.sem("ve")
    s_se = p.sem("se"); s_gp = p.sem("gp"); s_st = p.sem("st")

    ps_big = p.psum(stack, "ps_big", [128, 2048], F32)
    ps_kv = p.psum(stack, "ps_kv", [128, NB * 128], F32)
    ps_ks = p.psum(stack, "ps_ks", [128, 16], F32)
    ps_tr = p.psum(stack, "ps_tr", [128, NB * 128], BF16)

    for cb in range(NB):
        p.do("gpsimd", "dma_start", out=xsb[:, cb * TC:(cb + 1) * TC],
             in_=x_d[cb * 128:(cb + 1) * 128, :], inc=s_ld, n=16)
        p.do("gpsimd", "dma_start", out=wq[:, cb * 3 * D:(cb + 1) * 3 * D],
             in_=wqkv_d[cb * 128:(cb + 1) * 128, :], inc=s_ld, n=16)
        p.do("gpsimd", "dma_start", out=wo[:, cb * D:(cb + 1) * D],
             in_=wout_d[cb * 128:(cb + 1) * 128, :], inc=s_ld, n=16)
    s_ldh = p.sem("ldh")
    p.do("sync", "dma_start", out=bo[:, :], in_=bout_d[:, :], inc=s_ldh, n=16)
    p.wait("vector", s_ld); p.wait("vector", s_ldh)
    p.wait("tensor", s_ld); p.wait("tensor", s_ldh)
    p.wait("scalar", s_ld); p.wait("scalar", s_ldh)
    p.wait("tensor", p.sem("init"))
    for cb in range(NB):
        p.do("vector", "memset", vtm[:, cb * VW + 128: cb * VW + 130], 1.0)
    p.lastinc("vector", s_ve)

    qfm_f32 = qfm[:, :].bitcast(F32)
    kvacc = qfm_f32[:, 0:1024]
    ksacc = qfm_f32[:, 1024:1032]
    x3 = lambda t0, L: xsb[:, :].rearrange("p (c t) -> p c t", c=NB)[:, :, t0:t0 + L]
    xln3 = lambda t0, L: xln[:, :].rearrange("p (c t) -> p c t", c=NB)[:, :, t0:t0 + L]

    def elu_from_psum(ps_ap, out_ap, w, off=0):
        a = tA[:, off:off + w]; b = tB[:, off:off + w]
        p.do("vector", "tensor_scalar", b, ps_ap, 0.0, None, ALU.max)
        p.do("vector", "tensor_sub", a, b, ps_ap, inc=s_ve)
        p.wait("scalar", s_ve)
        p.do("scalar", "activation", a, a, AF.Exp, scale=-1.0, inc=s_se)
        p.wait("vector", s_se)
        p.do("vector", "tensor_add", out_ap, b, a, inc=s_ve)

    NCH = T // 512

    for b in range(BPC):
        for c in range(NCH):
            t0 = b * T + c * 512
            tl0 = c * 512
            p.wait("tensor", s_ve)
            p.wait("tensor", s_se)
            p.wait("vector", s_st)
            p.wait("scalar", s_st)
            _ln_stats_chunk(p, onest, x3(t0, 512), ps_big[:, 0:512], ps_big[:, 512:1024],
                            xsq[:, :].rearrange("p (c t) -> p c t", c=NB), mean[:, :], alpha[:, :],
                            s_mm, s_ve, s_se, epsc=epsc)
            _ln_apply_chunk(p, x3(t0, 512), mean[:, :], alpha[:, :],
                            xln3(tl0, 512), tA[:, :].rearrange("p (c t) -> p c t", c=NB), s_ve, s_se)
            if b == 0 and c == 0:
                dbg(p, "xln0", xln[:, 0:512], [s_ve])
                dbg(p, "xln0b", xln[:, T:T + 512], [s_ve])
            for tt in range(4):
                tx = tl0 + tt * 128
                if b == 0 and c == 0 and tt == 0:
                    _asnap_ve, _asnap_se, _ait = [], [], [0]
                if tt == 0:
                    p.wait("tensor", s_ve)
                for item in range(4):
                    slot = _ait[0] % 4
                    sl = ps_big[:, slot * 512:(slot + 1) * 512]
                    if _ait[0] >= 4:
                        p.wait("tensor", s_ve, _asnap_ve[_ait[0] - 4])
                        p.wait("tensor", s_se, _asnap_se[_ait[0] - 4])
                    half = item % 2
                    cof = D + half * 512 if item < 2 else 2 * D + half * 512
                    for cb in range(NB):
                        p.do("tensor", "matmul", sl,
                             xln[:, cb * T + tx: cb * T + tx + 128],
                             wq[:, cb * 3 * D + cof: cb * 3 * D + cof + 512],
                             start=(cb == 0), stop=(cb == NB - 1))
                    p.lastinc("tensor", s_mm)
                    p.wait("vector", s_mm)
                    if item < 2:
                        elu_from_psum(sl, ktm[:, half * 512:(half + 1) * 512], 512, off=half * 512)
                    else:
                        v3 = vtm[:, :].rearrange("p (c w) -> p c w", c=NB)[:, half * 4:(half + 1) * 4, 0:128]
                        p.do("vector", "tensor_copy", v3,
                             sl.rearrange("p (c w) -> p c w", c=4), inc=s_ve)
                    _asnap_ve.append(s_ve.v)
                    _asnap_se.append(s_se.v)
                    _ait[0] += 1
                p.wait("tensor", s_ve)
                for cb in range(NB):
                    p.do("tensor", "matmul", ps_kv[:, cb * 128:(cb + 1) * 128],
                         ktm[:, cb * 128:(cb + 1) * 128],
                         vtm[:, cb * VW: cb * VW + 128],
                         start=True, stop=True)
                for cb in range(NB):
                    p.do("tensor", "matmul", ps_ks[:, cb: cb + 1],
                         ktm[:, cb * 128:(cb + 1) * 128], onesc[:, :],
                         start=True, stop=True, inc=s_mm)
                p.wait("vector", s_mm)
                if tt == 0 and c == 0:
                    p.do("vector", "tensor_copy", kvacc, ps_kv[:, :])
                    p.do("vector", "tensor_copy", ksacc, ps_ks[:, 0:8], inc=s_ve)
                else:
                    p.do("vector", "tensor_add", kvacc, kvacc, ps_kv[:, :])
                    p.do("vector", "tensor_add", ksacc, ksacc, ps_ks[:, 0:8], inc=s_ve)
                if b == 0 and c == 0 and tt == 0:
                    dbg(p, "ktm0", ktm[:, :], [s_ve])
                    dbg(p, "vtm0", vtm[:, :], [s_ve])
        p.do("vector", "memset", kvd[:, :], 0.0)
        p.do("vector", "memset", ks2[:, :], 0.0, inc=s_ve)
        p.wait("vector", s_ve)
        for cb in range(NB):
            p.do("vector", "tensor_copy", kvd[0:64, cb * 128: cb * 128 + 64],
                 kvacc[0:64, cb * 128: cb * 128 + 64])
            p.do("vector", "tensor_copy", kvd[64:128, cb * 128 + 64: cb * 128 + 128],
                 kvacc[64:128, cb * 128 + 64: cb * 128 + 128])
            p.do("vector", "tensor_copy", ks2[0:64, cb * 2: cb * 2 + 1], ksacc[0:64, cb: cb + 1])
            p.do("vector", "tensor_copy", ks2[64:128, cb * 2 + 1: cb * 2 + 2], ksacc[64:128, cb: cb + 1])
        p.lastinc("vector", s_ve)
        if b == 0:
            dbg(p, "kvacc", kvacc, [s_ve])
            dbg(p, "ksacc", ksacc, [s_ve])
            dbg(p, "kvd", kvd[:, :], [s_ve])
            dbg(p, "ks2", ks2[:, :], [s_ve])

        for c in range(NCH):
            t0 = b * T + c * 512
            tl0 = c * 512
            p.wait("vector", s_st)
            p.wait("scalar", s_st)
            if b == 0 and c == 0:
                _qsnap_ve, _qsnap_se, _qit = [], [], [0]
            for ob in range(NB):
                q = _qit[0] % 2
                if ob == 0:
                    p.wait("tensor", s_ve)
                    p.wait("tensor", s_se)
                if _qit[0] >= 2:
                    p.wait("tensor", s_ve, _qsnap_ve[_qit[0] - 2])
                    p.wait("tensor", s_se, _qsnap_se[_qit[0] - 2])
                for cb in range(NB):
                    p.do("tensor", "matmul", ps_big[:, q * 512:(q + 1) * 512],
                         wq[:, cb * 3 * D + ob * 128: cb * 3 * D + ob * 128 + 128],
                         xln[:, cb * T + tl0: cb * T + tl0 + 512],
                         start=(cb == 0), stop=(cb == NB - 1))
                p.lastinc("tensor", s_mm)
                p.wait("vector", s_mm)
                elu_from_psum(ps_big[:, q * 512:(q + 1) * 512], qfm[:, ob * 512:(ob + 1) * 512],
                              512, off=q * 512)
                _qsnap_ve.append(s_ve.v)
                _qsnap_se.append(s_se.v)
                _qit[0] += 1
            if b == 0 and c == 0:
                dbg(p, "qfm0", qfm[:, :], [s_ve])
            for tt in range(4):
                q3 = qfm[:, :].rearrange("p (c t) -> p c t", c=NB)[:, :, tt * 128:(tt + 1) * 128]
                p.wait("tensor", s_ve)
                for cb in range(NB):
                    p.do("tensor", "matmul", ps_big[:, 1024 + cb * 128: 1024 + (cb + 1) * 128],
                         q3[:, cb, :], kvd[:, cb * 128:(cb + 1) * 128],
                         start=True, stop=True)
                for cb in range(NB):
                    p.do("tensor", "matmul", ps_ks[:, cb * 2:(cb + 1) * 2],
                         q3[:, cb, :], ks2[:, cb * 2:(cb + 1) * 2],
                         start=True, stop=True, inc=s_mm)
                p.wait("vector", s_mm)
                if debug and b == 0 and c == 0 and tt == 0:
                    p.do("vector", "tensor_copy", tB[:, 0:16], ps_ks[:, :])
                    p.do("vector", "tensor_copy", tB[:, 16:144], ps_big[:, 1024:1152], inc=s_ve)
                    dbg(p, "qksps", tB[:, 0:16], [s_ve])
                    dbg(p, "qkvps", tB[:, 16:144], [s_ve])
                p.wait("scalar", s_mm)
                p.do("scalar", "activation", rden[:, :], ps_ks[:, :], AF.Ln, bias=eps6c[:, :])
                p.do("scalar", "activation", rden[:, :], rden[:, :], AF.Exp, scale=-1.0, inc=s_se)
                p.wait("vector", s_se)
                rb = rden[:, :].unsqueeze(2).broadcast_to([128, 16, 64])
                p.do("vector", "tensor_tensor", att[:, :].rearrange("p (h f) -> p h f", h=16),
                     ps_big[:, 1024:2048].rearrange("p (h f) -> p h f", h=16), rb, ALU.mult, inc=s_ve)
                p.wait("tensor", s_ve)
                p.wait("tensor", s_se)
                for cb in range(NB):
                    p.do("tensor", "transpose", ps_tr[:, cb * 128:(cb + 1) * 128],
                         att[:, cb * 128:(cb + 1) * 128], ident[:, :])
                p.lastinc("tensor", s_mm)
                p.wait("scalar", s_mm)
                p.do("scalar", "copy",
                     afm[:, :].rearrange("p (c t) -> p c t", c=NB)[:, :, tt * 128:(tt + 1) * 128],
                     ps_tr[:, :].rearrange("p (c t) -> p c t", c=NB), inc=s_se)
                if b == 0 and c == 0 and tt == 0:
                    dbg(p, "att0", att[:, :], [s_ve])
                    dbg(p, "rden0", rden[:, :], [s_ve])
                    dbg(p, "afm0", afm[:, 0:128], [s_se])
            p.wait("tensor", s_se)
            p.wait("tensor", s_ve)
            if b == 0 and c == 0:
                _wsnap2, _wit2 = [], [0]
            for ob in range(NB):
                if _wit2[0] >= 2:
                    p.wait("tensor", s_ve, _wsnap2[_wit2[0] - 2])
                for cb in range(NB):
                    p.do("tensor", "matmul", ps_big[:, (ob % 2) * 512:(ob % 2 + 1) * 512],
                         wo[:, cb * D + ob * 128: cb * D + ob * 128 + 128],
                         afm[:, cb * 512:(cb + 1) * 512],
                         start=(cb == 0), stop=(cb == NB - 1))
                p.lastinc("tensor", s_mm)
                p.wait("vector", s_mm)
                p.do("vector", "tensor_scalar", tB[:, (ob % 2) * 512:(ob % 2) * 512 + 512],
                     ps_big[:, (ob % 2) * 512:(ob % 2 + 1) * 512],
                     bo[:, ob:ob + 1], None, ALU.add)
                p.do("vector", "tensor_add", tA[:, ob * 512:(ob + 1) * 512],
                     tB[:, (ob % 2) * 512:(ob % 2) * 512 + 512],
                     xsb[:, ob * TC + t0: ob * TC + t0 + 512], inc=s_ve)
                _wsnap2.append(s_ve.v)
                _wit2[0] += 1
            p.wait("sync", s_ve)
            for ob in range(NB):
                p.do("sync", "dma_start", out=ox_d[ob * 128:(ob + 1) * 128, t0:t0 + 512],
                     in_=tA[:, ob * 512:(ob + 1) * 512], inc=s_st, n=16)

    p.wait("sync", s_st)
    stack.close()
    p.barrier()


def build_stage_a(debug=False):
    nc = bass.Bass("TRN2", target_bir_lowering=False, detect_race_conditions=False)
    p = Prog(nc)
    cn = _mkconsts(nc, p)
    x_d = nc.dram_tensor("x", [D, TC], F32, kind="ExternalInput").ap()
    wqkv_d = nc.dram_tensor("wqkv", [D, 3 * D], F32, kind="ExternalInput").ap()
    wout_d = nc.dram_tensor("wout", [D, D], F32, kind="ExternalInput").ap()
    bout_d = nc.dram_tensor("boutc", [128, NB], F32, kind="ExternalInput").ap()
    ox_d = nc.dram_tensor("ox", [D, TC], F32, kind="ExternalOutput").ap()
    emit_stage_a(nc, p, cn, x_d, wqkv_d, wout_d, bout_d, ox_d, debug=debug)
    p.emit()
    return nc



def _prep_stage_a(inputs, core):
    x = inputs["L0_attn_x"]
    xs = x[2 * core: 2 * core + 2].reshape(TC, D)
    return {
        "x": np.ascontiguousarray(xs.T),
        "wqkv": np.ascontiguousarray(inputs["L0_attn_Wqkv"]),
        "wout": np.ascontiguousarray(inputs["L0_attn_Wout"]),
        "boutc": np.ascontiguousarray(inputs["L0_attn_bout"].reshape(NB, 128).T),
    }


def run_stage_a(inputs, cores=None):
    cores = list(range(NCORES)) if cores is None else cores
    nc = build_stage_a()
    in_maps = [_prep_stage_a(inputs, c) for c in cores]
    res = run_bass_kernel_spmd(nc, in_maps, core_ids=list(range(len(cores))))
    return [r["ox"] for r in res.results]


def emit_stage_b(nc, p, cn, x_d, wg_d, wu_d, wd_d, ox_d, prep_args=None):
    import contextlib
    stack = contextlib.ExitStack()
    ident, onest, onesc, epsc, eps6c = cn["ident"], cn["onest"], cn["onesc"], cn["epsc"], cn["eps6c"]
    HID = 3072
    HH = HID // 2
    NIB = HH // 128
    S2 = float(1.0 / (127.0 * 127.0))

    xsb = p.sbuf(stack, "xsb", [128, NB * TC], BF16)
    wgh = p.sbuf(stack, "wgh", [128, NB * HH], BF16)
    wuh = p.sbuf(stack, "wuh", [128, NB * HH], BF16)
    wdh = p.sbuf(stack, "wdh", [128, NIB * D], BF16)
    hbuf = p.sbuf(stack, "hbuf", [128, NIB * 512], BF16)
    xln = p.sbuf(stack, "xlnb", [128, NB * 512], BF16)
    sg = p.sbuf(stack, "sg", [128, 512], BF16)
    sg2 = p.sbuf(stack, "sg2", [128, 512], BF16)
    mean = p.sbuf(stack, "mean", [128, 512], F32)
    alpha = p.sbuf(stack, "alpha", [128, 512], F32)
    tA = p.sbuf(stack, "tA", [128, NB * 512], F32)
    stg = p.sbuf(stack, "stg", [128, NB * 512], F32)
    xsq = hbuf[:, 0:4096].bitcast(BF16)

    s_ld = p.sem("ld"); s_mm = p.sem("mm"); s_ve = p.sem("ve")
    s_se = p.sem("se"); s_st = p.sem("st")

    prep_gen = None
    if prep_args is not None:
        ringb = p.sbuf(stack, "prep_ring", [128, 6144], BF16)
        prep_gen = gen_prep_moe(nc, p, ringb, *prep_args)
    ps_st = p.psum(stack, "ps_st", [128, 1024], F32)
    ps_g = [p.psum(stack, f"ps_g{i}", [128, 512], F32) for i in range(2)]
    ps_u = [p.psum(stack, f"ps_u{i}", [128, 512], F32) for i in range(2)]
    ps_y = p.psum(stack, "ps_y", [128, 1024], F32)
    sgq = [sg, sg2]

    for cb in range(NB):
        p.do("gpsimd", "dma_start", out=xsb[:, cb * TC:(cb + 1) * TC],
             in_=x_d[cb * 128:(cb + 1) * 128, :], inc=s_ld, n=16)

    x3 = lambda t0, L: xsb[:, :].rearrange("p (c t) -> p c t", c=NB)[:, :, t0:t0 + L]
    xln3 = xln[:, :].rearrange("p (c t) -> p c t", c=NB)

    for half in range(2):
        p.wait("gpsimd", s_mm)
        for cb in range(NB):
            p.do("gpsimd", "dma_start", out=wgh[:, cb * HH:(cb + 1) * HH],
                 in_=wg_d[cb * 128:(cb + 1) * 128, half * HH:(half + 1) * HH], inc=s_ld, n=16)
            p.do("gpsimd", "dma_start", out=wuh[:, cb * HH:(cb + 1) * HH],
                 in_=wu_d[cb * 128:(cb + 1) * 128, half * HH:(half + 1) * HH], inc=s_ld, n=16)
        for ib in range(NIB):
            p.do("gpsimd", "dma_start", out=wdh[:, ib * D:(ib + 1) * D],
                 in_=wd_d[half * HH + ib * 128: half * HH + (ib + 1) * 128, :], inc=s_ld, n=16)
        p.wait("tensor", s_ld)
        p.wait("vector", s_ld)

        for c in range(TC // 512):
            t0 = c * 512
            p.wait("vector", s_mm)
            p.wait("tensor", s_ve)
            p.wait("tensor", s_se)
            _ln_stats_chunk(p, onest, x3(t0, 512), ps_st[:, 0:512], ps_st[:, 512:1024],
                            xsq.rearrange("p (c t) -> p c t", c=NB), mean[:, :], alpha[:, :],
                            s_mm, s_ve, s_se, epsc=epsc)
            _ln_apply_chunk(p, x3(t0, 512), mean[:, :], alpha[:, :],
                            xln3[:, :, :], tA[:, :].rearrange("p (c t) -> p c t", c=NB), s_ve, s_se)
            if c == 0 and half == 0:
                _bsnap_se, _bsnap_ve, _bit = [], [], [0]
            for ib in range(NIB):
                q = _bit[0] % 2
                if ib == 0:
                    p.wait("tensor", s_ve)
                    p.wait("tensor", s_se)
                if _bit[0] >= 2:
                    p.wait("tensor", s_se, _bsnap_se[_bit[0] - 2])
                    p.wait("tensor", s_ve, _bsnap_ve[_bit[0] - 2])
                for cb in range(NB):
                    p.do("tensor", "matmul", ps_g[q][:, :],
                         wgh[:, cb * HH + ib * 128: cb * HH + (ib + 1) * 128],
                         xln[:, cb * 512:(cb + 1) * 512],
                         start=(cb == 0), stop=(cb == NB - 1))
                p.lastinc("tensor", s_mm)
                for cb in range(NB):
                    p.do("tensor", "matmul", ps_u[q][:, :],
                         wuh[:, cb * HH + ib * 128: cb * HH + (ib + 1) * 128],
                         xln[:, cb * 512:(cb + 1) * 512],
                         start=(cb == 0), stop=(cb == NB - 1))
                p.lastinc("tensor", s_mm)
                p.wait("scalar", s_mm, s_mm.v - 1)
                if _bit[0] >= 2:
                    p.wait("scalar", s_ve, _bsnap_ve[_bit[0] - 2])
                p.do("scalar", "activation", sgq[q][:, :], ps_g[q][:, :], AF.Silu,
                     scale=float(1.0 / 127.0), inc=s_se)
                p.wait("vector", s_mm)
                p.wait("vector", s_se)
                p.do("vector", "tensor_mul", hbuf[:, ib * 512:(ib + 1) * 512],
                     sgq[q][:, :], ps_u[q][:, :], inc=s_ve)
                _bsnap_se.append(s_se.v)
                _bsnap_ve.append(s_ve.v)
                _bit[0] += 1
            p.wait("vector", s_st)
            p.wait("tensor", s_ve)
            if c == 0 and half == 0:
                _dsnap, _dit = [], [0]
            for ob in range(NB):
                bank = ob % 2
                if _dit[0] >= 2:
                    p.wait("tensor", s_ve, _dsnap[_dit[0] - 2])
                for ib in range(NIB):
                    p.do("tensor", "matmul", ps_y[:, bank * 512:(bank + 1) * 512],
                         wdh[:, ib * D + ob * 128: ib * D + ob * 128 + 128],
                         hbuf[:, ib * 512:(ib + 1) * 512],
                         start=(ib == 0), stop=(ib == NIB - 1))
                p.lastinc("tensor", s_mm)
                p.wait("vector", s_mm)
                if half == 0:
                    p.do("vector", "scalar_tensor_tensor", stg[:, ob * 512:(ob + 1) * 512],
                         ps_y[:, bank * 512:(bank + 1) * 512], S2,
                         xsb[:, ob * TC + t0: ob * TC + t0 + 512],
                         ALU.mult, ALU.add, inc=s_ve)
                else:
                    p.do("vector", "tensor_scalar", stg[:, ob * 512:(ob + 1) * 512],
                         ps_y[:, bank * 512:(bank + 1) * 512], S2, None,
                         ALU.mult, inc=s_ve)
                _dsnap.append(s_ve.v)
                _dit[0] += 1
            p.wait("gpsimd", s_ve)
            for ob in range(NB):
                if half == 0:
                    p.do("gpsimd", "dma_start", out=ox_d[ob * 128:(ob + 1) * 128, t0:t0 + 512],
                         in_=stg[:, ob * 512:(ob + 1) * 512], inc=s_st, n=16)
                else:
                    p.do("gpsimd", "dma_start", out=ox_d[ob * 128:(ob + 1) * 128, t0:t0 + 512],
                         in_=stg[:, ob * 512:(ob + 1) * 512], accum_op=ALU.add, inc=s_st, n=16)
            if prep_gen is not None:
                for _ in range(4):
                    next(prep_gen, None)

    p.wait("gpsimd", s_st)
    if prep_gen is not None:
        for _ in prep_gen:
            pass
    stack.close()
    p.barrier()


def build_stage_b():
    nc = bass.Bass("TRN2", target_bir_lowering=False, detect_race_conditions=False)
    p = Prog(nc)
    cn = _mkconsts(nc, p)
    HID = 3072
    x_d = nc.dram_tensor("x", [D, TC], F32, kind="ExternalInput").ap()
    wg_d = nc.dram_tensor("wgt", [D, HID], I32, kind="ExternalInput").ap()
    wu_d = nc.dram_tensor("wut", [D, HID], I32, kind="ExternalInput").ap()
    wd_d = nc.dram_tensor("wdt", [HID, D], I32, kind="ExternalInput").ap()
    ox_d = nc.dram_tensor("ox", [D, TC], F32, kind="ExternalOutput").ap()
    emit_stage_b(nc, p, cn, x_d, wg_d, wu_d, wd_d, ox_d)
    p.emit()
    return nc


def _prep_stage_b(inputs, core, x_full):
    xs = x_full[2 * core: 2 * core + 2].reshape(TC, D)
    return {
        "x": np.ascontiguousarray(xs.T),
        "wgt": np.ascontiguousarray(inputs["L0_ffn_w_gate"].T),
        "wut": np.ascontiguousarray(inputs["L0_ffn_w_up"].T),
        "wdt": np.ascontiguousarray(inputs["L0_ffn_w_down"].T),
    }


def run_stage_b(inputs, x_full, cores=None):
    cores = list(range(NCORES)) if cores is None else cores
    nc = build_stage_b()
    in_maps = [_prep_stage_b(inputs, c, x_full) for c in cores]
    res = run_bass_kernel_spmd(nc, in_maps, core_ids=list(range(len(cores))))
    return [r["ox"] for r in res.results]


def _ln_stats_generic(p, onest, xap3, ps_sum, ps_ssq, xsq3, mean, alpha, s_mm, s_ve, s_se, epsc):
    _ln_stats_chunk(p, onest, xap3, ps_sum, ps_ssq, xsq3, mean, alpha, s_mm, s_ve, s_se, epsc=epsc)


def emit_stage_c(nc, p, cn, x_d, wtk_d, wtv_d, wtr_d, wck_d, wcv_d, wcr_d, vc_d, ox_d,
                 trunc='full', debug=False, prep_gen=None):
    import contextlib
    stack = contextlib.ExitStack()
    L = 256
    NCH_W = T // L
    LC = 512
    CH = 4096
    HH = CH // 2
    NHB = HH // 128

    _dbg_names = []
    def dbg(name, ap, when):
        if not debug:
            return
        d = nc.dram_tensor("dbg_" + name, list(ap.shape), ap.dtype, kind="ExternalOutput").ap()
        _dbg_names.append("dbg_" + name)
        sd = p.sem("dbgst"); sb = p.sem("blk")
        for s in when:
            p.wait("sync", s)
        p.do("sync", "dma_start", out=d, in_=ap, inc=sd, n=16)
        p.wait("sync", sd)
        p.do("sync", "sem_inc", sb.h, 1)
        sb.v += 1
        for e in ("tensor", "vector", "scalar", "gpsimd"):
            p.wait(e, sb)
    ident, onest, onesc, epsc, eps6c = cn["ident"], cn["onest"], cn["onesc"], cn["epsc"], cn["eps6c"]
    tinyc = cn["tinyc"]

    z = p.sbuf(stack, "z", [128, NB * TC], BF16)
    vc = p.sbuf(stack, "vc", [128, 88], F32)
    dcol = p.sbuf(stack, "dcol", [128, 24], F32)
    mean = p.sbuf(stack, "mean", [128, LC], F32)
    alpha = p.sbuf(stack, "alpha", [128, LC], F32)

    s_ld = p.sem("ld"); s_ldh = p.sem("ldh"); s_mm = p.sem("mm"); s_ve = p.sem("ve")
    s_se = p.sem("se"); s_st = p.sem("st"); s_bar = p.sem("bar")

    def barrier():
        for e in Prog.ENGINES:
            p.do(e, "nop", inc=s_bar)
        for e in Prog.ENGINES:
            p.wait(e, s_bar)

    VI = {"td": 0, "tf": 1, "g1": 2, "b1": 3, "g2": 4, "b2": 5,
          "mk": 6, "mv": 7, "mr": 8, "cmk": 9, "cmr": 10}
    def vcol(name, cb):
        return vc[:, VI[name] * 8 + cb: VI[name] * 8 + cb + 1]
    def vcol8(name):
        return vc[:, VI[name] * 8: VI[name] * 8 + 8]

    p.do("sync", "dma_start", out=vc[:, :], in_=vc_d[:, :], inc=s_ldh, n=16)
    p.wait("scalar", s_ldh)
    p.wait("vector", s_ldh)
    p.do("scalar", "activation", dcol[:, 0:8], vcol8("td"), AF.Exp, inc=s_se)
    p.do("vector", "tensor_scalar", dcol[:, 8:16], vcol8("td"), 0.0, float(L), ALU.max, ALU.mult, inc=s_ve)
    EW = dcol[:, 0:8]; WRL = dcol[:, 8:16]

    zz3 = z[:, :].rearrange("p (c t) -> p c t", c=NB)

    def ln_apply_bf16(xap3, out3, gname=None, bname=None, Lx=LC):
        mb = mean[:, 0:Lx].unsqueeze(1).broadcast_to([128, NB, Lx])
        ab = alpha[:, 0:Lx].unsqueeze(1).broadcast_to([128, NB, Lx])
        p.wait("vector", s_se)
        p.do("vector", "tensor_tensor", out3, xap3, mb, ALU.subtract)
        p.do("vector", "tensor_tensor", out3, out3, ab, ALU.mult, inc=s_ve)
        if gname is not None:
            for cb in range(NB):
                p.do("vector", "tensor_scalar", out3[:, cb, :], out3[:, cb, :],
                     vcol(gname, cb), vcol(bname, cb), ALU.mult, ALU.add,
                     inc=(s_ve if cb == NB - 1 else None))

    def dump_z_and_finish():
        p.wait("gpsimd", s_ve)
        for cb in range(NB):
            p.do("gpsimd", "dma_start", out=ox_d[cb * 128:(cb + 1) * 128, :],
                 in_=z[:, cb * TC:(cb + 1) * TC], inc=s_st, n=16)
        p.wait("gpsimd", s_st)
        stack.close()
        p.barrier()

    with contextlib.ExitStack() as st0:
        xch = p.sbuf(st0, "xch", [128, NB * LC], BF16)
        xsq0 = p.sbuf(st0, "xsq0", [128, NB * LC], BF16)
        ps_st0 = p.psum(st0, "ps_st0", [128, 1024], F32)
        for c in range(TC // LC):
            t0 = c * LC
            p.wait("gpsimd", s_ve)
            for cb in range(NB):
                p.do("gpsimd", "dma_start", out=xch[:, cb * LC:(cb + 1) * LC],
                     in_=x_d[cb * 128:(cb + 1) * 128, t0:t0 + LC], inc=s_ld, n=16)
            p.wait("vector", s_ld)
            p.wait("tensor", s_ld)
            p.wait("tensor", s_ve)
            p.wait("tensor", s_se)
            x3c = xch[:, :].rearrange("p (c t) -> p c t", c=NB)
            _ln_stats_chunk(p, onest, x3c, ps_st0[:, 0:512], ps_st0[:, 512:1024],
                            xsq0[:, :].rearrange("p (c t) -> p c t", c=NB),
                            mean[:, :], alpha[:, :], s_mm, s_ve, s_se, epsc=epsc)
            ln_apply_bf16(x3c, zz3[:, :, t0:t0 + LC])
    barrier()
    if trunc == 'p0':
        dump_z_and_finish()
        return

    with contextlib.ExitStack() as stw:
        wtk = p.sbuf(stw, "wtk_s", [128, NB * D], BF16)
        wtv = p.sbuf(stw, "wtv_s", [128, NB * D], BF16)
        wtr = p.sbuf(stw, "wtr_s", [128, NB * D], BF16)
        hb = p.sbuf(stw, "hb", [128, NB * L], BF16)
        hpb = p.sbuf(stw, "hpb", [128, NB * L], BF16)
        mx = p.sbuf(stw, "mx", [128, NB * L], BF16)
        kf = p.sbuf(stw, "kf", [128, NB * L], F32)
        vb = p.sbuf(stw, "vb", [128, NB * L], BF16)
        rb = p.sbuf(stw, "rb", [128, NB * L], BF16)
        kexp = p.sbuf(stw, "kexp", [128, NB * L], BF16)
        zbb = p.sbuf(stw, "zbb", [128, NB * L], BF16)
        Bbuf = p.sbuf(stw, "Bbuf", [128, NB * (L + 2)], BF16)
        Abuf = p.sbuf(stw, "Abuf", [128, NB * (L + 2)], BF16)
        ppp = p.sbuf(stw, "ppp", [128, NB * L], F32)
        t1f = p.sbuf(stw, "t1f", [128, NB * L], F32)
        t2f = p.sbuf(stw, "t2f", [128, NB * L], F32)

        e1u = hpb
        mxr = p.sbuf(stw, "mxr", [128, NB * L], BF16)
        rdn = t2f
        e1b = kexp
        e2b = mx
        dnb = zbb
        xsqw = kexp
        sm = p.sbuf(stw, "sm", [128, 64], F32)
        psA = p.psum(stw, "ps_kA", [128, NB * L], F32)
        psB = p.psum(stw, "ps_kB", [128, NB * L], F32)

        for cb in range(NB):
            p.do("gpsimd", "dma_start", out=wtk[:, cb * D:(cb + 1) * D],
                 in_=wtk_d[cb * 128:(cb + 1) * 128, :], inc=s_ld, n=16)
            p.do("gpsimd", "dma_start", out=wtv[:, cb * D:(cb + 1) * D],
                 in_=wtv_d[cb * 128:(cb + 1) * 128, :], inc=s_ld, n=16)
            p.do("gpsimd", "dma_start", out=wtr[:, cb * D:(cb + 1) * D],
                 in_=wtr_d[cb * 128:(cb + 1) * 128, :], inc=s_ld, n=16)
        p.wait("tensor", s_ld)

        h3 = hb[:, :].rearrange("p (c t) -> p c t", c=NB)
        hp3 = hpb[:, :].rearrange("p (c t) -> p c t", c=NB)
        mx3 = mx[:, :].rearrange("p (c t) -> p c t", c=NB)
        k3 = kf[:, :].rearrange("p (c t) -> p c t", c=NB)
        B3 = Bbuf[:, :].rearrange("p (c t) -> p c t", c=NB)
        A3 = Abuf[:, :].rearrange("p (c t) -> p c t", c=NB)
        aac = sm[:, 0:8]; btc = sm[:, 8:16]; mprev = sm[:, 16:24]
        kmax = sm[:, 24:32]; m2 = sm[:, 32:40]; negm2 = sm[:, 40:48]
        sc1 = sm[:, 48:56]; sc2 = sm[:, 56:64]

        def fm_matmul(wsb, rhs_sb, out_ps, Lx, evac):
            p.wait("tensor", s_ve)
            p.wait("tensor", s_se)
            for ob in range(NB):
                for cb in range(NB):
                    p.do("tensor", "matmul", out_ps[:, ob * Lx:(ob + 1) * Lx],
                         wsb[:, cb * D + ob * 128: cb * D + ob * 128 + 128],
                         rhs_sb[:, cb * Lx:(cb + 1) * Lx],
                         start=(cb == 0), stop=(cb == NB - 1))
            p.lastinc("tensor", s_mm)

        kf2 = p.sbuf(stw, "kf2", [128, NB * L], F32)
        vb2 = p.sbuf(stw, "vb2", [128, NB * L], BF16)
        mxv = p.sbuf(stw, "mxv", [128, NB * L], BF16)
        e2uv = mxv
        pcol = p.sbuf(stw, "pcol", [128, 8], BF16)
        kfq = [kf, kf2]; vbq = [vb, vb2]
        NCHT = BPC * NCH_W

        def front(gi):
            b, c = divmod(gi, NCH_W)
            t0 = b * T + c * L
            q = gi % 2
            p.wait("vector", s_mm)
            p.wait("tensor", s_ve)
            p.wait("tensor", s_se)
            _ln_stats_chunk(p, onest, zz3[:, :, t0:t0 + L], psA[:, 0:L], psA[:, L:2 * L],
                            xsqw[:, 0:NB * L].rearrange("p (c t) -> p c t", c=NB),
                            mean[:, 0:L], alpha[:, 0:L], s_mm, s_ve, s_se, epsc=epsc)
            _ssqsnap[gi] = p.sem("mm").v
            ln_apply_bf16(zz3[:, :, t0:t0 + L], h3, "g1", "b1", Lx=L)
            if c == 0:
                p.do("vector", "memset", pcol[:, :], 0.0, inc=s_ve)
                p.wait("vector", s_ve)
            p.do("vector", "tensor_copy", hp3[:, :, 0], pcol[:, :])
            p.do("vector", "tensor_copy", hp3[:, :, 1:L], h3[:, :, 0:L - 1], inc=s_ve)
            p.wait("vector", s_ve)
            p.do("vector", "tensor_copy", pcol[:, :], h3[:, :, L - 1], inc=s_ve)
            p.wait("vector", s_ve)
            p.do("vector", "tensor_tensor", mx3[:, :, :], h3, hp3, ALU.subtract, inc=s_ve)
            for mixn, mb_ in (("mr", mxr), ("mv", mxv), ("mk", mx)):
                mb3 = mb_[:, :].rearrange("p (c t) -> p c t", c=NB)
                for cb in range(NB):
                    p.do("vector", "scalar_tensor_tensor", mb3[:, cb, :], mx3[:, cb, :],
                         vcol(mixn, cb), hp3[:, cb, :], ALU.mult, ALU.add,
                         inc=(s_ve if cb == NB - 1 else None))
            for mat, (wsb, rhs_sb, pso) in enumerate(((wtk, mx, psA), (wtv, mxv, psB))):
                p.wait("tensor", s_ve)
                p.wait("tensor", s_se)
                for ob in range(NB):
                    for cb in range(NB):
                        p.do("tensor", "matmul", pso[:, ob * L:(ob + 1) * L],
                             wsb[:, cb * D + ob * 128: cb * D + ob * 128 + 128],
                             rhs_sb[:, cb * L:(cb + 1) * L],
                             start=(cb == 0), stop=(cb == NB - 1))
                p.lastinc("tensor", s_mm)
                if mat == 0:
                    _kmmsnap[gi] = p.sem("mm").v

        def wkv(gi):
            b, c = divmod(gi, NCH_W)
            t0 = b * T + c * L
            q = gi % 2
            kfc = kfq[q]; vbc = vbq[q]; rbc = rb
            k3c = kfc[:, :].rearrange("p (c t) -> p c t", c=NB)
            if c == 0:
                p.wait("vector", s_se)
                p.do("vector", "memset", aac, 0.0)
                p.do("vector", "memset", btc, 0.0)
                p.do("vector", "memset", mprev, -1e30, inc=s_ve)
                p.wait("vector", s_ve)
            p.wait("vector", s_se)
            p.do("vector", "reduce_max", kmax, k3c, axis=AX.X, inc=s_ve)
            p.wait("vector", s_ve)
            p.do("vector", "tensor_max", m2, kmax, mprev, inc=s_ve)
            p.wait("vector", s_ve)
            p.do("vector", "tensor_add", m2, m2, WRL, inc=s_ve)
            p.wait("vector", s_ve)
            p.do("vector", "tensor_scalar", negm2, m2, -1.0, None, ALU.mult)
            p.do("vector", "tensor_sub", sc1, mprev, m2, inc=s_ve)
            p.wait("scalar", s_ve)
            p.do("scalar", "activation", sc1, sc1, AF.Exp, inc=s_se)
            p.wait("vector", s_se)
            p.do("vector", "tensor_mul", sc2, btc, sc1, inc=s_ve)
            p.wait("vector", s_ve)
            p.do("vector", "tensor_copy", B3[:, :, L], sc2, inc=s_ve)
            p.wait("vector", s_ve)
            p.do("vector", "tensor_copy", A3[:, :, L], aac, inc=s_ve)
            p.wait("vector", s_ve)
            p.wait("scalar", s_mm, _ssqsnap.get(gi + 1, 0))
            p.wait("scalar", s_ve)
            for cb in range(NB):
                p.do("scalar", "activation", kexp[:, cb * L:(cb + 1) * L], k3c[:, cb, :],
                     AF.Exp, bias=negm2[:, cb:cb + 1],
                     inc=(s_se if cb == NB - 1 else None))
            p.wait("vector", s_se)
            for cb in range(NB):
                p.do("vector", "tensor_scalar", zbb[:, cb * L:(cb + 1) * L],
                     kexp[:, cb * L:(cb + 1) * L], EW[:, cb:cb + 1], None, ALU.mult,
                     inc=(s_ve if cb == NB - 1 else None))
            for cb in range(NB):
                d0 = EW[:, cb:cb + 1].broadcast_to([128, L])
                p.do("vector", "tensor_tensor_scan", B3[:, cb, 0:L], d0,
                     zbb[:, cb * L:(cb + 1) * L], B3[:, cb, L:L + 1],
                     ALU.mult, ALU.add, inc=(s_ve if cb == NB - 1 else None))
            p.wait("scalar", s_ve)
            ppp3 = ppp[:, :].rearrange("p (c t) -> p c t", c=NB)
            p.do("scalar", "activation", ppp3[:, :, 1:L], B3[:, :, 0:L - 1],
                 AF.Ln, bias=tinyc[:, :])
            p.do("scalar", "activation", ppp3[:, :, 0:1], B3[:, :, L:L + 1],
                 AF.Ln, bias=tinyc[:, :], inc=s_se)
            p.wait("vector", s_se)
            for cb in range(NB):
                p.do("vector", "tensor_scalar", ppp[:, cb * L:(cb + 1) * L],
                     ppp[:, cb * L:(cb + 1) * L], m2[:, cb:cb + 1], None, ALU.add,
                     inc=(s_ve if cb == NB - 1 else None))
            p.do("vector", "tensor_max", t1f[:, :], ppp[:, :], kfc[:, :])
            p.do("vector", "tensor_sub", t2f[:, :], ppp[:, :], t1f[:, :], inc=s_ve)
            p.wait("scalar", s_ve)
            p.do("scalar", "activation", e1u[:, :], t2f[:, :], AF.Exp, inc=s_se)
            p.wait("vector", s_se)
            p.do("vector", "tensor_sub", t2f[:, :], kfc[:, :], t1f[:, :], inc=s_ve)
            p.wait("scalar", s_ve)
            p.wait("scalar", s_mm)
            p.do("scalar", "activation", e2uv[:, :], t2f[:, :], AF.Exp, inc=s_se)
            p.wait("vector", s_se)
            p.do("vector", "tensor_mul", e2uv[:, :], e2uv[:, :], vbc[:, :], inc=s_ve)
            for cb in range(NB):
                p.do("vector", "tensor_tensor_scan", A3[:, cb, 0:L],
                     e1u[:, cb * L:(cb + 1) * L], e2uv[:, cb * L:(cb + 1) * L],
                     A3[:, cb, L:L + 1], ALU.mult, ALU.add,
                     inc=(s_ve if cb == NB - 1 else None))
            for cb in range(NB):
                p.do("vector", "tensor_scalar", t1f[:, cb * L:(cb + 1) * L],
                     k3c[:, cb, :], vcol("tf", cb), None, ALU.add,
                     inc=(s_ve if cb == NB - 1 else None))
            p.wait("vector", s_se)
            p.do("vector", "tensor_max", t2f[:, :], ppp[:, :], t1f[:, :])
            p.do("vector", "tensor_sub", ppp[:, :], ppp[:, :], t2f[:, :], inc=s_ve)
            p.wait("scalar", s_ve)
            p.do("scalar", "activation", e1b[:, :], ppp[:, :], AF.Exp, inc=s_se)
            p.wait("vector", s_se)
            p.do("vector", "tensor_sub", t1f[:, :], t1f[:, :], t2f[:, :], inc=s_ve)
            p.wait("scalar", s_ve)
            p.do("scalar", "activation", e2b[:, :], t1f[:, :], AF.Exp, inc=s_se)
            p.wait("vector", s_se)
            p.do("vector", "tensor_add", dnb[:, :], e1b[:, :], e2b[:, :], inc=s_ve)
            p.wait("scalar", s_ve)
            p.do("scalar", "activation", rdn[:, :], dnb[:, :], AF.Ln)
            p.do("scalar", "activation", rdn[:, :], rdn[:, :], AF.Exp, scale=-1.0, inc=s_se)
            e1b3 = e1b[:, :].rearrange("p (c t) -> p c t", c=NB)
            p.do("vector", "tensor_tensor", e1b3[:, :, 1:L], e1b3[:, :, 1:L],
                 A3[:, :, 0:L - 1], ALU.mult)
            p.do("vector", "tensor_tensor", e1b3[:, :, 0:1], e1b3[:, :, 0:1],
                 A3[:, :, L:L + 1], ALU.mult)
            p.do("vector", "tensor_mul", e2b[:, :], e2b[:, :], vbc[:, :])
            p.do("vector", "tensor_add", e1b[:, :], e1b[:, :], e2b[:, :])
            p.wait("vector", s_se)
            p.do("vector", "tensor_mul", e1b[:, :], e1b[:, :], rdn[:, :])
            p.do("vector", "tensor_mul", e1b[:, :], e1b[:, :], rbc[:, :])
            p.do("vector", "tensor_tensor", zz3[:, :, t0:t0 + L], zz3[:, :, t0:t0 + L],
                 e1b[:, :].rearrange("p (c t) -> p c t", c=NB), ALU.add, inc=s_ve)
            p.do("vector", "tensor_copy", aac, A3[:, :, L - 1])
            p.do("vector", "tensor_copy", btc, B3[:, :, L - 1])
            p.do("vector", "tensor_copy", mprev, m2, inc=s_ve)
            p.wait("vector", s_ve)

        def tail(gi):
            q = gi % 2
            b, c = divmod(gi, NCH_W)
            p.wait("scalar", s_mm, _kmmsnap[gi])
            p.do("scalar", "copy", kfq[q][:, :], psA[:, :], inc=s_se)
            p.wait("scalar", s_mm)
            p.do("scalar", "copy", vbq[q][:, :], psB[:, :], inc=s_se)
            kevac_v = p.sem("se").v - 1
            p.wait("tensor", s_se, kevac_v)
            for ob in range(NB):
                for cb in range(NB):
                    p.do("tensor", "matmul", psA[:, ob * L:(ob + 1) * L],
                         wtr[:, cb * D + ob * 128: cb * D + ob * 128 + 128],
                         mxr[:, cb * L:(cb + 1) * L],
                         start=(cb == 0), stop=(cb == NB - 1))
            p.lastinc("tensor", s_mm)
            p.wait("scalar", s_mm)
            p.wait("scalar", s_ve)
            p.do("scalar", "activation", rb[:, :], psA[:, :], AF.Sigmoid, inc=s_se)

        _ssqsnap, _kmmsnap = {}, {}
        for gi in range(NCHT + 1):
            if gi < NCHT:
                front(gi)
            if gi >= 1:
                wkv(gi - 1)
            if gi < NCHT:
                tail(gi)
            if prep_gen is not None:
                for _ in range(4):
                    next(prep_gen, None)
    barrier()
    if trunc == 'w':
        dump_z_and_finish()
        return

    with contextlib.ExitStack() as stc:
        wck = p.sbuf(stc, "wck_s", [128, NB * HH], BF16)
        wcv = p.sbuf(stc, "wcv_s", [128, NHB * D], BF16)
        wcr = p.sbuf(stc, "wcr_s", [128, NB * D], BF16)
        h2b = p.sbuf(stc, "h2b", [128, NB * (LC + 1)], BF16)
        mxc = p.sbuf(stc, "mxc", [128, NB * LC], BF16)
        rrb = p.sbuf(stc, "rrb", [128, NB * LC], BF16)
        kkb = p.sbuf(stc, "kkb", [128, NHB * LC], BF16)
        sgc = p.sbuf(stc, "sgc", [128, LC], BF16)
        sgc2 = p.sbuf(stc, "sgc2", [128, LC], BF16)
        sgcq = [sgc, sgc2]
        stg = p.sbuf(stc, "stg", [128, 2 * LC], F32)
        mxc2 = p.sbuf(stc, "mxc2", [128, NB * LC], BF16)
        xsqc = mxc
        ps_st = p.psum(stc, "ps_stc", [128, 1024], F32)
        ps_g = p.psum(stc, "ps_gc", [128, 1024], F32)
        ps_y = p.psum(stc, "ps_yc", [128, 1024], F32)

        h2e = h2b[:, :].rearrange("p (c t) -> p c t", c=NB)
        h23 = h2e[:, :, 1:LC + 1]
        hp23 = h2e[:, :, 0:LC]
        mxc3 = mxc[:, :].rearrange("p (c t) -> p c t", c=NB)
        mxc23 = mxc2[:, :].rearrange("p (c t) -> p c t", c=NB)

        for cb in range(NB):
            p.do("gpsimd", "dma_start", out=wcr[:, cb * D:(cb + 1) * D],
                 in_=wcr_d[cb * 128:(cb + 1) * 128, :], inc=s_ld, n=16)
        def cm_front(b, c):
            t0f = b * T + c * LC
            if c == 0:
                p.do("vector", "memset", h2e[:, :, 0:1], 0.0, inc=s_ve)
            else:
                p.do("vector", "tensor_copy", h2e[:, :, 0:1], h2e[:, :, LC:LC + 1], inc=s_ve)
            yield
            p.wait("vector", s_mm)
            p.do("vector", "tensor_mul",
                 xsqc[:, 0:NB * LC].rearrange("p (c t) -> p c t", c=NB),
                 zz3[:, :, t0f:t0f + LC], zz3[:, :, t0f:t0f + LC], inc=s_ve)
            yield
            p.wait("tensor", s_ve)
            for cb in range(NB):
                p.do("tensor", "matmul", ps_st[:, 0:512], onest[:, :],
                     zz3[:, cb, t0f:t0f + LC], start=(cb == 0), stop=(cb == NB - 1),
                     inc=(s_mm if cb == NB - 1 else None))
            for cb in range(NB):
                p.do("tensor", "matmul", ps_st[:, 512:1024], onest[:, :],
                     xsqc[:, cb * LC:(cb + 1) * LC],
                     start=(cb == 0), stop=(cb == NB - 1),
                     inc=(s_mm if cb == NB - 1 else None))
            smsnap = p.sem("mm").v
            yield
            p.wait("vector", s_mm, smsnap)
            p.do("vector", "tensor_scalar", mean[:, :], ps_st[:, 0:512], 1.0 / D,
                 None, ALU.mult, inc=s_ve)
            p.wait("vector", s_ve)
            p.do("vector", "tensor_mul", alpha[:, :], mean[:, :], mean[:, :], inc=s_ve)
            p.wait("vector", s_ve)
            p.do("vector", "scalar_tensor_tensor", alpha[:, :], ps_st[:, 512:1024],
                 1.0 / D, alpha[:, :], ALU.mult, ALU.subtract, inc=s_ve)
            p.wait("scalar", s_ve)
            p.do("scalar", "activation", alpha[:, :], alpha[:, :], AF.Ln, bias=epsc[:, :])
            p.do("scalar", "activation", alpha[:, :], alpha[:, :], AF.Exp, scale=-0.5, inc=s_se)
            yield
            ln_apply_bf16(zz3[:, :, t0f:t0f + LC], h23, "g2", "b2", Lx=LC)
            yield
            p.wait("vector", s_mm)
            p.do("vector", "tensor_tensor", mxc3[:, :, :], h23, hp23, ALU.subtract)
            for cb in range(NB):
                p.do("vector", "scalar_tensor_tensor", mxc3[:, cb, :], mxc3[:, cb, :],
                     vcol("cmr", cb), hp23[:, cb, :], ALU.mult, ALU.add,
                     inc=(s_ve if cb == NB - 1 else None))
            yield
            p.do("vector", "tensor_tensor", mxc23[:, :, :], h23, hp23, ALU.subtract)
            for cb in range(NB):
                p.do("vector", "scalar_tensor_tensor", mxc23[:, cb, :], mxc23[:, cb, :],
                     vcol("cmk", cb), hp23[:, cb, :], ALU.mult, ALU.add,
                     inc=(s_ve if cb == NB - 1 else None))
            yield

        weave = lambda fg: (next(fg, None) if fg is not None else None)
        chunks = [(half, b, c) for half in range(2) for b in range(BPC)
                  for c in range(T // LC)]
        for gi, (half, b, c) in enumerate(chunks):
            t0 = b * T + c * LC
            if gi == 0 or chunks[gi - 1][0] != half:
                p.wait("gpsimd", s_mm)
                for cb in range(NB):
                    p.do("gpsimd", "dma_start", out=wck[:, cb * HH:(cb + 1) * HH],
                         in_=wck_d[cb * 128:(cb + 1) * 128, half * HH:(half + 1) * HH], inc=s_ld, n=16)
                for ib in range(NHB):
                    p.do("gpsimd", "dma_start", out=wcv[:, ib * D:(ib + 1) * D],
                         in_=wcv_d[half * HH + ib * 128: half * HH + (ib + 1) * 128, :], inc=s_ld, n=16)
                p.wait("tensor", s_ld)
            if gi == 0:
                for _ in cm_front(b, c):
                    pass
            fg = None
            if gi + 1 < len(chunks):
                nh, nb_, nch = chunks[gi + 1]
                fg = cm_front(nb_, nch)
            p.wait("tensor", s_ve)
            p.wait("tensor", s_se)
            p.wait("scalar", s_ve)
            if gi == 0:
                _rsnap, _rit = [], [0]
            for ob in range(NB):
                bank = _rit[0] % 2
                if _rit[0] >= 2:
                    p.wait("tensor", s_se, _rsnap[_rit[0] - 2])
                for cb in range(NB):
                    p.do("tensor", "matmul", ps_g[:, bank * 512:(bank + 1) * 512],
                         wcr[:, cb * D + ob * 128: cb * D + ob * 128 + 128],
                         mxc[:, cb * LC:(cb + 1) * LC],
                         start=(cb == 0), stop=(cb == NB - 1))
                p.lastinc("tensor", s_mm)
                p.wait("scalar", s_mm)
                p.do("scalar", "activation", rrb[:, ob * LC:(ob + 1) * LC],
                     ps_g[:, bank * 512:(bank + 1) * 512], AF.Sigmoid, inc=s_se)
                _rsnap.append(s_se.v)
                _rit[0] += 1
            p.wait("tensor", s_ve)
            p.wait("tensor", s_se)
            if gi == 0:
                _ksnap_ve, _ksnap_se, _kit = [], [], [0]
            for hbk in range(NHB):
                bank = _kit[0] % 2
                if _kit[0] >= 2:
                    p.wait("tensor", s_ve, _ksnap_ve[_kit[0] - 2])
                    p.wait("tensor", s_se, _ksnap_se[_kit[0] - 2])
                for cb in range(NB):
                    p.do("tensor", "matmul", ps_g[:, bank * 512:(bank + 1) * 512],
                         wck[:, cb * HH + hbk * 128: cb * HH + hbk * 128 + 128],
                         mxc2[:, cb * LC:(cb + 1) * LC],
                         start=(cb == 0), stop=(cb == NB - 1))
                p.lastinc("tensor", s_mm)
                p.wait("scalar", s_mm)
                if _kit[0] >= 2:
                    p.wait("scalar", s_ve, _ksnap_ve[_kit[0] - 2])
                p.do("scalar", "activation", sgcq[bank][:, :], ps_g[:, bank * 512:(bank + 1) * 512],
                     AF.Relu, inc=s_se)
                p.wait("vector", s_se)
                p.do("vector", "tensor_mul", kkb[:, hbk * LC:(hbk + 1) * LC],
                     sgcq[bank][:, :], sgcq[bank][:, :], inc=s_ve)
                _ksnap_ve.append(s_ve.v)
                _ksnap_se.append(s_se.v)
                _kit[0] += 1
                if hbk in (1, 3, 9, 12):
                    weave(fg)
            p.wait("tensor", s_ve)
            if gi == 0:
                _vsnap, _vit = [], [0]
            for ob in range(NB):
                bank = _vit[0] % 2
                if _vit[0] >= 2:
                    p.wait("tensor", s_ve, _vsnap[_vit[0] - 2])
                for ib in range(NHB):
                    p.do("tensor", "matmul", ps_y[:, bank * 512:(bank + 1) * 512],
                         wcv[:, ib * D + ob * 128: ib * D + ob * 128 + 128],
                         kkb[:, ib * LC:(ib + 1) * LC],
                         start=(ib == 0), stop=(ib == NHB - 1))
                p.lastinc("tensor", s_mm)
                p.wait("vector", s_mm)
                p.wait("vector", s_st, max(0, s_st.v - 16))
                if half == 0:
                    p.do("vector", "tensor_mul", stg[:, bank * LC:(bank + 1) * LC],
                         rrb[:, ob * LC:(ob + 1) * LC], ps_y[:, bank * 512:(bank + 1) * 512])
                    p.do("vector", "tensor_add", stg[:, bank * LC:(bank + 1) * LC],
                         stg[:, bank * LC:(bank + 1) * LC],
                         z[:, ob * TC + t0: ob * TC + t0 + LC], inc=s_ve)
                else:
                    p.do("vector", "tensor_mul", stg[:, bank * LC:(bank + 1) * LC],
                         rrb[:, ob * LC:(ob + 1) * LC], ps_y[:, bank * 512:(bank + 1) * 512],
                         inc=s_ve)
                _vsnap.append(s_ve.v)
                _vit[0] += 1
                p.wait("gpsimd", s_ve)
                if half == 0:
                    p.do("gpsimd", "dma_start", out=ox_d[ob * 128:(ob + 1) * 128, t0:t0 + LC],
                         in_=stg[:, bank * LC:(bank + 1) * LC], inc=s_st, n=16)
                else:
                    p.do("gpsimd", "dma_start", out=ox_d[ob * 128:(ob + 1) * 128, t0:t0 + LC],
                         in_=stg[:, bank * LC:(bank + 1) * LC], accum_op=ALU.add, inc=s_st, n=16)
                if ob in (1, 2, 4, 6):
                    weave(fg)
            if fg is not None:
                for _ in fg:
                    pass
        p.wait("gpsimd", s_st)
        for cb in range(NB):
            p.do("gpsimd", "dma_start", out=ox_d[cb * 128:(cb + 1) * 128, :],
                 in_=x_d[cb * 128:(cb + 1) * 128, :], accum_op=ALU.add,
                 max_dma_last_dim=2048, inc=s_st, n=16)
        p.wait("gpsimd", s_st)
    stack.close()
    p.barrier()


def build_stage_c(trunc='full', debug=False):
    import contextlib
    nc = bass.Bass("TRN2", target_bir_lowering=False, detect_race_conditions=False)
    p = Prog(nc)
    cn = _mkconsts(nc, p)
    CH = 4096
    x_d = nc.dram_tensor("x", [D, TC], F32, kind="ExternalInput").ap()
    wtk_d = nc.dram_tensor("wtk", [D, D], F32, kind="ExternalInput").ap()
    wtv_d = nc.dram_tensor("wtv", [D, D], F32, kind="ExternalInput").ap()
    wtr_d = nc.dram_tensor("wtr", [D, D], F32, kind="ExternalInput").ap()
    wck_d = nc.dram_tensor("wck", [D, CH], F32, kind="ExternalInput").ap()
    wcv_d = nc.dram_tensor("wcv", [CH, D], F32, kind="ExternalInput").ap()
    wcr_d = nc.dram_tensor("wcr", [D, D], F32, kind="ExternalInput").ap()
    vc_d = nc.dram_tensor("vcols", [128, 88], F32, kind="ExternalInput").ap()
    ox_d = nc.dram_tensor("ox", [D, TC], F32, kind="ExternalOutput").ap()
    emit_stage_c(nc, p, cn, x_d, wtk_d, wtv_d, wtr_d, wck_d, wcv_d, wcr_d, vc_d, ox_d,
                 trunc=trunc, debug=debug)
    p.emit()
    return nc


def _prep_stage_c(inputs, core, x_full):
    xs = x_full[2 * core: 2 * core + 2].reshape(TC, D)
    def col8(name):
        return inputs[name].astype(np.float32).reshape(NB, 128).T
    vcols = np.concatenate([
        col8("L1_attn_time_decay"), col8("L1_attn_time_first"),
        col8("L1_attn_ln1_g"), col8("L1_attn_ln1_b"),
        col8("L1_attn_ln2_g"), col8("L1_attn_ln2_b"),
        inputs["L1_attn_mix_k"].astype(np.float32).reshape(NB, 128).T,
        inputs["L1_attn_mix_v"].astype(np.float32).reshape(NB, 128).T,
        inputs["L1_attn_mix_r"].astype(np.float32).reshape(NB, 128).T,
        inputs["L1_attn_cmix_k"].astype(np.float32).reshape(NB, 128).T,
        inputs["L1_attn_cmix_r"].astype(np.float32).reshape(NB, 128).T,
    ], axis=1)
    return {
        "x": np.ascontiguousarray(xs.T),
        "wtk": np.ascontiguousarray(inputs["L1_attn_w_tk"]),
        "wtv": np.ascontiguousarray(inputs["L1_attn_w_tv"]),
        "wtr": np.ascontiguousarray(inputs["L1_attn_w_tr"]),
        "wck": np.ascontiguousarray(inputs["L1_attn_w_ck"]),
        "wcv": np.ascontiguousarray(inputs["L1_attn_w_cv"]),
        "wcr": np.ascontiguousarray(inputs["L1_attn_w_cr"]),
        "vcols": np.ascontiguousarray(vcols),
    }


def run_stage_c(inputs, x_full, cores=None):
    cores = list(range(NCORES)) if cores is None else cores
    nc = build_stage_c()
    in_maps = [_prep_stage_c(inputs, c, x_full) for c in cores]
    res = run_bass_kernel_spmd(nc, in_maps, core_ids=list(range(len(cores))))
    return [r["ox"] for r in res.results]


def gen_prep_moe(nc, p, ring, w1_d, w3_d, w2_d, ws1_d, ws3_d, ws2_d,
                 w1b_d, w3b_d, w2b_d):
    E = 16
    s_pi = p.sem("pld"); s_po = p.sem("pst")
    jobs = []
    for e in range(E + 1):
        s1 = ws1_d if e == E else w1_d[e]
        s3 = ws3_d if e == E else w3_d[e]
        s2 = ws2_d if e == E else w2_d[e]
        jobs.append((s1.rearrange("(c p) i -> p c i", p=128), w1b_d[e, :, :], 768))
        jobs.append((s3.rearrange("(c p) i -> p c i", p=128), w3b_d[e, :, :], 768))
        jobs.append((s2.rearrange("(c p) d -> p c d", p=128), w2b_d[e, :, :], 1024))
    for src_ap, dst_ap, w in jobs:
        p.wait("gpsimd", s_po)
        p.do("gpsimd", "dma_start",
             out=ring[:, :].rearrange("p (c i) -> p c i", i=w), in_=src_ap, inc=s_pi, n=16)
        p.wait("sync", s_pi)
        p.do("sync", "dma_start", out=dst_ap, in_=ring[:, :], inc=s_po, n=16)
        yield
    p.wait("sync", s_po)
    p.do("sync", "sem_inc", p.sem("prepdone").h, 1)
    p.sem("prepdone").v += 1


def emit_stage_d(nc, p, cn, x_d, w1_d, w3_d, w2_d, ws1_d, ws3_d, ws2_d, gw_d, gb_d, ox_d,
                 w1b_d=None, w3b_d=None, w2b_d=None, prep_here=True):
    import contextlib
    stack = contextlib.ExitStack()
    E = 16
    I = 768
    NIB = I // 128
    PL = 1024
    NPASS = TC // PL
    NTI = PL // 128
    BIG = 1.0e30
    ident, onest, onesc, epsc, eps6c = cn["ident"], cn["onest"], cn["onesc"], cn["epsc"], cn["eps6c"]
    onesr = cn["onesr"]; ident32 = cn["ident32"]

    s_ld = p.sem("ld"); s_ldh = p.sem("ldh"); s_mm = p.sem("mm"); s_ve = p.sem("ve")
    s_se = p.sem("se"); s_st = p.sem("st"); s_rw = p.sem("rw")

    xln = p.sbuf(stack, "xlnp", [128, NB * PL], BF16)
    xch = p.sbuf(stack, "xch", [128, NB * 512], BF16)
    yab = p.sbuf(stack, "yab", [128, NB * PL], F32)
    w1s = [p.sbuf(stack, f"w1s{i}", [128, NB * I], BF16) for i in range(2)]
    w3s = [p.sbuf(stack, f"w3s{i}", [128, NB * I], BF16) for i in range(2)]
    w2s = [p.sbuf(stack, f"w2s{i}", [128, NIB * D], BF16) for i in range(2)]
    hbuf = p.sbuf(stack, "hbufd", [128, NIB * PL], BF16)
    sgm = [p.sbuf(stack, f"sgmd{i}", [128, 512], BF16) for i in range(2)]
    cmbq = [p.sbuf(stack, f"cmbq{i}", [128, PL], BF16) for i in range(2)]
    SELS = p.sbuf(stack, "SELS", [16, E * 128], F32)
    gws = p.sbuf(stack, "gws", [128, NB * E], BF16)
    gbb = p.sbuf(stack, "gbb", [128, E], F32)
    mean = p.sbuf(stack, "meand", [128, 512], F32)
    alpha = p.sbuf(stack, "alphad", [128, 512], F32)
    LG = p.sbuf(stack, "LG", [128, NTI * E], F32)
    SB = p.sbuf(stack, "SBr", [128, NTI * E], F32)
    SCR = p.sbuf(stack, "SCR", [128, NTI * E], F32)
    GA = p.sbuf(stack, "GA", [128, NTI * 4], F32)
    GB = p.sbuf(stack, "GB", [128, NTI * 4], F32)
    GC = p.sbuf(stack, "GC", [128, NTI * 4], F32)
    GD = p.sbuf(stack, "GD", [128, NTI * 4], F32)
    R1 = p.sbuf(stack, "R1", [128, NTI], F32)
    R2 = p.sbuf(stack, "R2", [128, NTI], F32)
    cmT = p.sbuf(stack, "cmT", [16, NTI * 128], F32)
    ps_misc = p.psum(stack, "ps_miscd", [128, 1024], F32)
    ps_g = [p.psum(stack, f"ps_gd{i}", [128, 512], F32) for i in range(2)]
    ps_u = [p.psum(stack, f"ps_ud{i}", [128, 512], F32) for i in range(2)]
    ps_y = p.psum(stack, "ps_yd", [128, 1024], F32)

    sI = p.sem("init")
    kcol = p.sbuf(stack, "selk", [16, 1], F32)
    p.do("gpsimd", "iota", SELS[:, :].rearrange("p (e m) -> p e m", e=E),
         [[1, E], [0, 128]], channel_multiplier=0,
         allow_small_or_imprecise_dtypes=True, inc=sI)
    p.do("gpsimd", "iota", kcol[:, :], [[0, 1]], channel_multiplier=1,
         allow_small_or_imprecise_dtypes=True, inc=sI)
    p.wait("vector", sI)
    p.do("vector", "tensor_scalar", SELS[:, :], SELS[:, :], kcol[:, :], None,
         ALU.is_equal, inc=sI)

    p.do("sync", "dma_start", out=gbb[0:1, :], in_=gb_d[:, :], inc=s_ldh, n=16)
    _gbv = p.sem("ldh").v
    for cb in range(NB):
        p.do("gpsimd", "dma_start", out=gws[:, cb * E:(cb + 1) * E],
             in_=gw_d[cb * 128:(cb + 1) * 128, :], inc=s_ld, n=16)
    _gwv = p.sem("ld").v

    if prep_here:
        ring = p.sbuf(stack, "prep_ring", [128, NB * I], BF16)
        for _ in gen_prep_moe(nc, p, ring, w1_d, w3_d, w2_d, ws1_d, ws3_d, ws2_d,
                              w1b_d, w3b_d, w2b_d):
            pass
    p.wait("sync", p.sem("prepdone"))

    def wload(e, buf):
        p.do("sync", "dma_start", out=w1s[buf][:, :], in_=w1b_d[e, :, :], inc=s_ldh, n=16)
        p.do("sync", "dma_start", out=w3s[buf][:, :], in_=w3b_d[e, :, :], inc=s_ldh, n=16)
        p.do("sync", "dma_start", out=w2s[buf][:, :], in_=w2b_d[e, :, :], inc=s_ldh, n=16)

    vsync = lambda: (p.lastinc("vector", s_ve), p.wait("vector", s_ve))

    _wsnap = {}
    _gsnap_se, _gsnap_ve, _ysnap = [], [], []
    _bcsnap, _bcit = [], [0]
    _git, _yit = [0], [0]
    wload(0, 0)
    _wsnap[0] = p.sem("ldh").v
    xlnB = p.sbuf(stack, "xlnpB", [128, NB * PL], BF16)
    xln2 = [xln, xlnB]

    def head(pi):
        xl = xln2[pi % 2]
        pt0h = pi * PL
        x3c = xch[:, :].rearrange("p (c t) -> p c t", c=NB)
        for c in range(PL // 512):
            t0 = pt0h + c * 512
            p.wait("gpsimd", s_ve)
            p.wait("gpsimd", s_mm)
            for cb in range(NB):
                p.do("gpsimd", "dma_start", out=xch[:, cb * 512:(cb + 1) * 512],
                     in_=x_d[cb * 128:(cb + 1) * 128, t0:t0 + 512], inc=s_ld, n=16)
            yield
            xsq3 = xl[:, :].rearrange("p (c t) -> p c t", c=NB)[:, :, c * 512:(c + 1) * 512]
            p.wait("vector", s_ld)
            p.wait("vector", s_mm)
            p.do("vector", "tensor_mul", xsq3, x3c, x3c, inc=s_ve)
            yield
            p.wait("tensor", s_ld)
            p.wait("tensor", s_ve)
            p.wait("tensor", s_se)
            for cb in range(NB):
                p.do("tensor", "matmul", ps_misc[:, 0:512], onest[:, :], x3c[:, cb, :],
                     start=(cb == 0), stop=(cb == NB - 1), inc=(s_mm if cb == NB - 1 else None))
            for cb in range(NB):
                p.do("tensor", "matmul", ps_misc[:, 512:1024], onest[:, :], xsq3[:, cb, :],
                     start=(cb == 0), stop=(cb == NB - 1), inc=(s_mm if cb == NB - 1 else None))
            smsnap = p.sem("mm").v
            yield
            p.wait("vector", s_mm, smsnap)
            p.do("vector", "tensor_scalar", mean[:, :], ps_misc[:, 0:512], 1.0 / D,
                 None, ALU.mult, inc=s_ve)
            p.wait("vector", s_ve)
            p.do("vector", "tensor_mul", alpha[:, :], mean[:, :], mean[:, :], inc=s_ve)
            p.wait("vector", s_ve)
            p.do("vector", "scalar_tensor_tensor", alpha[:, :], ps_misc[:, 512:1024], 1.0 / D,
                 alpha[:, :], ALU.mult, ALU.subtract, inc=s_ve)
            p.wait("scalar", s_ve)
            p.do("scalar", "activation", alpha[:, :], alpha[:, :], AF.Ln, bias=epsc[:, :])
            p.do("scalar", "activation", alpha[:, :], alpha[:, :], AF.Exp, scale=-0.5, inc=s_se)
            yield
            mb = mean[:, :].unsqueeze(1).broadcast_to([128, NB, 512])
            ab = alpha[:, :].unsqueeze(1).broadcast_to([128, NB, 512])
            o3 = xl[:, :].rearrange("p (c t) -> p c t", c=NB)[:, :, c * 512:(c + 1) * 512]
            p.wait("vector", s_se)
            p.do("vector", "tensor_tensor", o3, x3c, mb, ALU.subtract)
            p.do("vector", "tensor_tensor", o3, o3, ab, ALU.mult, inc=s_ve)
            yield
        p.wait("tensor", s_ve)
        p.wait("tensor", s_ld, _gwv)
        p.wait("tensor", s_ldh, _gbv)
        for ti in range(NTI):
            for cb in range(NB):
                p.do("tensor", "matmul", ps_misc[:, ti * E:(ti + 1) * E],
                     xl[:, cb * PL + ti * 128: cb * PL + (ti + 1) * 128],
                     gws[:, cb * E:(cb + 1) * E],
                     start=(cb == 0), stop=(cb == NB - 1), inc=(s_mm if cb == NB - 1 else None))
        p.do("tensor", "matmul", ps_misc[:, 512:512 + E], onesr[:, :], gbb[0:1, :],
             start=True, stop=True, inc=s_mm)
        p.wait("scalar", s_mm)
        p.do("scalar", "activation", LG[:, :], ps_misc[:, 0:NTI * E], AF.Sigmoid, inc=s_se)
        yield
        p.wait("vector", s_se)
        gb3 = ps_misc[:, 512:512 + E].unsqueeze(1).broadcast_to([128, NTI, E])
        LG3 = LG[:, :].rearrange("p (t e) -> p t e", t=NTI)
        SB3 = SB[:, :].rearrange("p (t e) -> p t e", t=NTI)
        SC3 = SCR[:, :].rearrange("p (t e) -> p t e", t=NTI)
        p.do("vector", "tensor_tensor", SB3, LG3, gb3, ALU.add, inc=s_ve)
        p.wait("vector", s_ve)
        SBg = SB[:, :].rearrange("p (t g e) -> p (t g) e", g=4, e=4)
        a0, a1, a2, a3 = (SBg[:, :, j] for j in range(4))
        g = lambda t_: t_[:, :]
        p.do("vector", "tensor_max", GA[:, :], a0, a1)
        p.do("vector", "tensor_tensor", GB[:, :], a0, a1, ALU.min, inc=s_ve)
        p.wait("vector", s_ve)
        p.do("vector", "tensor_max", GC[:, :], a2, a3)
        p.do("vector", "tensor_tensor", GD[:, :], a2, a3, ALU.min, inc=s_ve)
        p.wait("vector", s_ve)
        yield
        p.do("vector", "tensor_sub", GB[:, :], GB[:, :], GD[:, :], inc=s_ve)
        p.wait("vector", s_ve)
        p.do("vector", "tensor_tensor", R1[:, 0:NTI * 4].bitcast(F32) if False else GB[:, :],
             GB[:, :], GB[:, :], ALU.bypass) if False else None
        p.do("vector", "tensor_tensor", SCR[:, 0:NTI * 4], GA[:, :], GC[:, :], ALU.is_gt, inc=s_ve)
        p.wait("vector", s_ve)
        p.do("vector", "tensor_mul", GB[:, :], GB[:, :], SCR[:, 0:NTI * 4], inc=s_ve)
        p.wait("vector", s_ve)
        yield
        p.do("vector", "tensor_add", GB[:, :], GB[:, :], GD[:, :], inc=s_ve)
        p.wait("vector", s_ve)
        p.do("vector", "tensor_tensor", GD[:, :], GA[:, :], GC[:, :], ALU.min, inc=s_ve)
        p.wait("vector", s_ve)
        p.do("vector", "tensor_max", GB[:, :], GB[:, :], GD[:, :], inc=s_ve)
        p.wait("vector", s_ve)
        yield
        p.do("vector", "tensor_max", GA[:, :], GA[:, :], GC[:, :], inc=s_ve)
        p.wait("vector", s_ve)
        p.do("vector", "tensor_add", GA[:, :], GA[:, :], GB[:, :], inc=s_ve)
        p.wait("vector", s_ve)
        GA4 = GA[:, :].rearrange("p (t g) -> p t g", t=NTI)
        p.do("vector", "reduce_max", R1[:, :], GA4, axis=AX.X, inc=s_ve)
        p.wait("vector", s_ve)
        yield
        r1b = R1[:, :].unsqueeze(2).broadcast_to([128, NTI, 4])
        p.do("vector", "tensor_tensor", GB[:, :].rearrange("p (t g) -> p t g", t=NTI), GA4, r1b,
             ALU.is_ge, inc=s_ve)
        p.wait("vector", s_ve)
        p.do("vector", "scalar_tensor_tensor", GC[:, :], GB[:, :], -BIG, GA[:, :],
             ALU.mult, ALU.add, inc=s_ve)
        p.wait("vector", s_ve)
        p.do("vector", "reduce_max", R2[:, :], GC[:, :].rearrange("p (t g) -> p t g", t=NTI),
             axis=AX.X, inc=s_ve)
        p.wait("vector", s_ve)
        yield
        r2b = R2[:, :].unsqueeze(2).broadcast_to([128, NTI, 4])
        p.do("vector", "tensor_tensor", GB[:, :].rearrange("p (t g) -> p t g", t=NTI), GA4, r2b,
             ALU.is_ge, inc=s_ve)
        p.wait("vector", s_ve)
        gm4 = GB[:, :].rearrange("p (t g) -> p t g 1", t=NTI) if False else None
        gmr = GB[:, :].rearrange("p (t g) -> p t g", t=NTI).unsqueeze(3).broadcast_to([128, NTI, 4, 4])
        SB4 = SB[:, :].rearrange("p (t g e) -> p t g e", g=4, e=4)
        SC4 = SCR[:, :].rearrange("p (t g e) -> p t g e", g=4, e=4)
        p.do("vector", "tensor_scalar", GD[:, :], GB[:, :], BIG, -BIG, ALU.mult, ALU.add, inc=s_ve)
        p.wait("vector", s_ve)
        gdr = GD[:, :].rearrange("p (t g) -> p t g", t=NTI).unsqueeze(3).broadcast_to([128, NTI, 4, 4])
        p.do("vector", "tensor_tensor", SC4, SB4, gmr, ALU.mult, inc=s_ve)
        p.wait("vector", s_ve)
        yield
        p.do("vector", "tensor_tensor", SC4, SC4, gdr, ALU.add, inc=s_ve)
        p.wait("vector", s_ve)
        for it in range(3):
            p.do("vector", "reduce_max", R1[:, :], SC3, axis=AX.X, inc=s_ve)
            p.wait("vector", s_ve)
            rb = R1[:, :].unsqueeze(2).broadcast_to([128, NTI, E])
            p.do("vector", "tensor_tensor", SB3 if False else LG3, SC3, rb, ALU.is_ge) if False else None
            p.do("vector", "tensor_tensor", GC[:, 0:NTI].rearrange("p t -> p t 1") if False else SC3,
                 SC3, rb, ALU.is_ge) if False else None
            p.do("vector", "tensor_tensor", GD[:, 0:NTI * 4], GD[:, 0:NTI * 4], GD[:, 0:NTI * 4], ALU.bypass) if False else None
            p.do("vector", "tensor_tensor", SC4, SC4, R1[:, :].unsqueeze(2).unsqueeze(3).broadcast_to([128, NTI, 4, 4]), ALU.is_ge, inc=s_ve) if False else None
            p.do("vector", "tensor_tensor", SB3, SC3, rb, ALU.is_ge, inc=s_ve)
            p.wait("vector", s_ve)
            yield
            p.do("vector", "scalar_tensor_tensor", SCR[:, :], SB[:, :], -BIG, SCR[:, :],
                 ALU.mult, ALU.add, inc=s_ve)
            p.wait("vector", s_ve)
        p.do("vector", "reduce_max", R1[:, :], SC3, axis=AX.X, inc=s_ve)
        p.wait("vector", s_ve)
        p.do("vector", "tensor_tensor", SB3, LG3, gb3, ALU.add, inc=s_ve)
        p.wait("vector", s_ve)
        yield
        p.do("vector", "tensor_tensor", SB4, SB4, gmr, ALU.mult, inc=s_ve)
        p.wait("vector", s_ve)
        p.do("vector", "tensor_tensor", SB4, SB4, gdr, ALU.add, inc=s_ve)
        p.wait("vector", s_ve)
        rb = R1[:, :].unsqueeze(2).broadcast_to([128, NTI, E])
        p.do("vector", "tensor_tensor", SB3, SB3, rb, ALU.is_ge, inc=s_ve)
        p.wait("vector", s_ve)
        yield
        p.do("vector", "tensor_mul", SB[:, :], SB[:, :], LG[:, :], inc=s_ve)
        p.wait("vector", s_ve)
        p.do("vector", "reduce_sum", R1[:, :], SB3, axis=AX.X, inc=s_ve)
        p.wait("vector", s_ve)
        p.do("vector", "tensor_scalar", R2[:, :], R1[:, :], 1e-20, None, ALU.add, inc=s_ve)
        p.wait("scalar", s_ve)
        p.do("scalar", "activation", R2[:, :], R2[:, :], AF.Ln)
        p.do("scalar", "activation", R2[:, :], R2[:, :], AF.Exp, scale=-1.0, inc=s_se)
        p.wait("vector", s_se)
        rwb = R2[:, :].unsqueeze(2).broadcast_to([128, NTI, E])
        p.do("vector", "tensor_tensor", SB3, SB3, rwb, ALU.mult, inc=s_ve)
        p.wait("vector", s_ve)
        yield
        p.do("vector", "tensor_scalar", SB[:, :], SB[:, :], 2.5, None, ALU.mult, inc=s_ve)
        p.wait("tensor", s_ve)
        for ti in range(NTI):
            p.do("tensor", "transpose", ps_misc[0:16, ti * 128:(ti + 1) * 128],
                 SB[:, ti * E:(ti + 1) * E], ident32[:, :], inc=s_mm)
        p.wait("vector", s_mm)
        p.do("vector", "tensor_copy", cmT[:, :], ps_misc[0:16, 0:NTI * 128], inc=s_ve)
    for _ in head(0):
        pass
    for pi in range(NPASS):
        pt0 = pi * PL
        xln_p = xln2[pi % 2]
        hg = head(pi + 1) if pi + 1 < NPASS else None
        for ei in range(E + 1):
            gctr = pi * (E + 1) + ei
            buf = gctr % 2
            if ei < E or pi + 1 < NPASS:
                p.wait("sync", s_mm)
                wload((ei + 1) % (E + 1), 1 - buf)
                _wsnap[(gctr + 1)] = p.sem("ldh").v
            p.wait("tensor", s_ldh, _wsnap.get(gctr, 0))
            if ei < E:
                cq = cmbq[ei % 2]
                p.wait("scalar", s_ve)
                p.wait("tensor", s_ve)
                p.wait("tensor", s_se)
                for h2 in range(2):
                    bank = ps_misc[:, (_bcit[0] % 2) * 512:(_bcit[0] % 2) * 512 + 512]
                    if _bcit[0] >= 2:
                        p.wait("tensor", s_se, _bcsnap[_bcit[0] - 2])
                    p.do("tensor", "matmul", bank, SELS[:, ei * 128:(ei + 1) * 128],
                         cmT[:, h2 * 512:(h2 + 1) * 512], start=True, stop=True, inc=s_mm)
                    p.wait("scalar", s_mm)
                    p.do("scalar", "copy", cq[:, h2 * 512:(h2 + 1) * 512], bank, inc=s_se)
                    _bcsnap.append(p.sem("se").v)
                    _bcit[0] += 1
            for ib in range(NIB):
                for h2 in range(2):
                    tc0 = h2 * 512
                    q = _git[0] % 2
                    if _git[0] >= 2:
                        p.wait("tensor", s_se, _gsnap_se[_git[0] - 2])
                        p.wait("tensor", s_ve, _gsnap_ve[_git[0] - 2])
                    for cb in range(NB):
                        p.do("tensor", "matmul", ps_g[q][:, :],
                             w1s[buf][:, cb * I + ib * 128: cb * I + (ib + 1) * 128],
                             xln_p[:, cb * PL + tc0: cb * PL + tc0 + 512],
                             start=(cb == 0), stop=(cb == NB - 1))
                    p.lastinc("tensor", s_mm)
                    for cb in range(NB):
                        p.do("tensor", "matmul", ps_u[q][:, :],
                             w3s[buf][:, cb * I + ib * 128: cb * I + (ib + 1) * 128],
                             xln_p[:, cb * PL + tc0: cb * PL + tc0 + 512],
                             start=(cb == 0), stop=(cb == NB - 1))
                    p.lastinc("tensor", s_mm)
                    p.wait("scalar", s_mm, s_mm.v - 1)
                    if _git[0] >= 2:
                        p.wait("scalar", s_ve, _gsnap_ve[_git[0] - 2])
                    p.do("scalar", "activation", sgm[q][:, :], ps_g[q][:, :], AF.Silu, inc=s_se)
                    p.wait("vector", s_mm)
                    p.wait("vector", s_se)
                    hslice = hbuf[:, ib * PL + tc0: ib * PL + tc0 + 512]
                    p.do("vector", "tensor_mul", hslice, sgm[q][:, :], ps_u[q][:, :], inc=s_ve)
                    if ei < E:
                        p.do("vector", "tensor_mul", hslice, hslice,
                             cmbq[ei % 2][:, tc0:tc0 + 512], inc=s_ve)
                    _gsnap_se.append(s_se.v)
                    _gsnap_ve.append(s_ve.v)
                    _git[0] += 1
                    if ei >= E - 2 and hg is not None:
                        next(hg, None)
            p.wait("tensor", s_ve)
            for ob in range(NB):
                for h2 in range(2):
                    bank = _yit[0] % 2
                    tc0 = h2 * 512
                    if _yit[0] >= 2:
                        p.wait("tensor", s_ve, _ysnap[_yit[0] - 2])
                    for ib in range(NIB):
                        p.do("tensor", "matmul", ps_y[:, bank * 512:(bank + 1) * 512],
                             w2s[buf][:, ib * D + ob * 128: ib * D + ob * 128 + 128],
                             hbuf[:, ib * PL + tc0: ib * PL + tc0 + 512],
                             start=(ib == 0), stop=(ib == NIB - 1))
                    p.lastinc("tensor", s_mm)
                    p.wait("vector", s_mm)
                    ysl = yab[:, ob * PL + tc0: ob * PL + tc0 + 512]
                    if ei == 0:
                        p.do("vector", "tensor_copy", ysl, ps_y[:, bank * 512:(bank + 1) * 512], inc=s_ve)
                    else:
                        p.do("vector", "tensor_add", ysl, ysl, ps_y[:, bank * 512:(bank + 1) * 512], inc=s_ve)
                    _ysnap.append(s_ve.v)
                    _yit[0] += 1
        if hg is not None:
            for _ in hg:
                pass
        p.wait("gpsimd", s_ve)
        for ob in range(NB):
            p.do("gpsimd", "dma_start", out=ox_d[ob * 128:(ob + 1) * 128, pt0:pt0 + PL],
                 in_=yab[:, ob * PL:(ob + 1) * PL], inc=s_st, n=16)
        p.wait("vector", s_st)
    p.wait("gpsimd", s_st)
    for cb in range(NB):
        p.do("gpsimd", "dma_start", out=ox_d[cb * 128:(cb + 1) * 128, :],
             in_=x_d[cb * 128:(cb + 1) * 128, :], accum_op=ALU.add,
             max_dma_last_dim=2048, inc=s_st, n=16)
    p.wait("gpsimd", s_st)
    stack.close()
    p.barrier()


def build_stage_d():
    nc = bass.Bass("TRN2", target_bir_lowering=False, detect_race_conditions=False)
    p = Prog(nc)
    cn = _mkconsts(nc, p)
    E, I = 16, 768
    x_d = nc.dram_tensor("x", [D, TC], F32, kind="ExternalInput").ap()
    w1_d = nc.dram_tensor("w1t", [E, D, I], F32, kind="ExternalInput").ap()
    w3_d = nc.dram_tensor("w3t", [E, D, I], F32, kind="ExternalInput").ap()
    w2_d = nc.dram_tensor("w2t", [E, I, D], F32, kind="ExternalInput").ap()
    ws1_d = nc.dram_tensor("ws1t", [D, I], F32, kind="ExternalInput").ap()
    ws3_d = nc.dram_tensor("ws3t", [D, I], F32, kind="ExternalInput").ap()
    ws2_d = nc.dram_tensor("ws2t", [I, D], F32, kind="ExternalInput").ap()
    gw_d = nc.dram_tensor("gwt", [D, E], F32, kind="ExternalInput").ap()
    gb_d = nc.dram_tensor("gbias", [1, E], F32, kind="ExternalInput").ap()
    ox_d = nc.dram_tensor("ox", [D, TC], F32, kind="ExternalOutput").ap()
    w1b_d = nc.dram_tensor("w1b", [E + 1, 128, NB * I], BF16).ap()
    w3b_d = nc.dram_tensor("w3b", [E + 1, 128, NB * I], BF16).ap()
    w2b_d = nc.dram_tensor("w2b", [E + 1, 128, (I // 128) * D], BF16).ap()
    emit_stage_d(nc, p, cn, x_d, w1_d, w3_d, w2_d, ws1_d, ws3_d, ws2_d, gw_d, gb_d, ox_d,
                 w1b_d=w1b_d, w3b_d=w3b_d, w2b_d=w2b_d, prep_here=True)
    p.emit()
    return nc


def _prep_stage_d(inputs, core, x_full):
    xs = x_full[2 * core: 2 * core + 2].reshape(TC, D)
    return {
        "x": np.ascontiguousarray(xs.T),
        "w1t": np.ascontiguousarray(inputs["L1_ffn_w1"].transpose(0, 2, 1)),
        "w3t": np.ascontiguousarray(inputs["L1_ffn_w3"].transpose(0, 2, 1)),
        "w2t": np.ascontiguousarray(inputs["L1_ffn_w2"].transpose(0, 2, 1)),
        "ws1t": np.ascontiguousarray(inputs["L1_ffn_ws1"].T),
        "ws3t": np.ascontiguousarray(inputs["L1_ffn_ws3"].T),
        "ws2t": np.ascontiguousarray(inputs["L1_ffn_ws2"].T),
        "gwt": np.ascontiguousarray(inputs["L1_ffn_gate_w"].T),
        "gbias": np.ascontiguousarray(inputs["L1_ffn_gate_bias"].reshape(1, 16)),
    }


def run_stage_d(inputs, x_full, cores=None):
    cores = list(range(NCORES)) if cores is None else cores
    nc = build_stage_d()
    in_maps = [_prep_stage_d(inputs, c, x_full) for c in cores]
    res = run_bass_kernel_spmd(nc, in_maps, core_ids=list(range(len(cores))))
    return [r["ox"] for r in res.results]



def build_fused():
    nc = bass.Bass("TRN2", target_bir_lowering=False, detect_race_conditions=False)
    p = Prog(nc)
    cn = _mkconsts(nc, p)
    E, I, CH, HID = 16, 768, 4096, 3072
    x_d = nc.dram_tensor("x", [D, TC], F32, kind="ExternalInput").ap()
    out_d = nc.dram_tensor("out", [D, TC], F32, kind="ExternalOutput").ap()
    act1 = nc.dram_tensor("act1", [D, TC], F32).ap()
    act2 = nc.dram_tensor("act2", [D, TC], F32).ap()
    wqkv_d = nc.dram_tensor("wqkv", [D, 3 * D], F32, kind="ExternalInput").ap()
    wout_d = nc.dram_tensor("wout", [D, D], F32, kind="ExternalInput").ap()
    bout_d = nc.dram_tensor("boutc", [128, NB], F32, kind="ExternalInput").ap()
    wg_d = nc.dram_tensor("wgt", [D, HID], I32, kind="ExternalInput").ap()
    wu_d = nc.dram_tensor("wut", [D, HID], I32, kind="ExternalInput").ap()
    wd_d = nc.dram_tensor("wdt", [HID, D], I32, kind="ExternalInput").ap()
    wtk_d = nc.dram_tensor("wtk", [D, D], F32, kind="ExternalInput").ap()
    wtv_d = nc.dram_tensor("wtv", [D, D], F32, kind="ExternalInput").ap()
    wtr_d = nc.dram_tensor("wtr", [D, D], F32, kind="ExternalInput").ap()
    wck_d = nc.dram_tensor("wck", [D, CH], F32, kind="ExternalInput").ap()
    wcv_d = nc.dram_tensor("wcv", [CH, D], F32, kind="ExternalInput").ap()
    wcr_d = nc.dram_tensor("wcr", [D, D], F32, kind="ExternalInput").ap()
    vc_d = nc.dram_tensor("vcols", [128, 88], F32, kind="ExternalInput").ap()
    w1_d = nc.dram_tensor("w1t", [E, D, I], F32, kind="ExternalInput").ap()
    w3_d = nc.dram_tensor("w3t", [E, D, I], F32, kind="ExternalInput").ap()
    w2_d = nc.dram_tensor("w2t", [E, I, D], F32, kind="ExternalInput").ap()
    ws1_d = nc.dram_tensor("ws1t", [D, I], F32, kind="ExternalInput").ap()
    ws3_d = nc.dram_tensor("ws3t", [D, I], F32, kind="ExternalInput").ap()
    ws2_d = nc.dram_tensor("ws2t", [I, D], F32, kind="ExternalInput").ap()
    gw_d = nc.dram_tensor("gwt", [D, E], F32, kind="ExternalInput").ap()
    gb_d = nc.dram_tensor("gbias", [1, E], F32, kind="ExternalInput").ap()
    w1b_d = nc.dram_tensor("w1b", [E + 1, 128, NB * I], BF16).ap()
    w3b_d = nc.dram_tensor("w3b", [E + 1, 128, NB * I], BF16).ap()
    w2b_d = nc.dram_tensor("w2b", [E + 1, 128, (I // 128) * D], BF16).ap()
    emit_stage_a(nc, p, cn, x_d, wqkv_d, wout_d, bout_d, act1)
    emit_stage_b(nc, p, cn, act1, wg_d, wu_d, wd_d, act2,
                 prep_args=(w1_d, w3_d, w2_d, ws1_d, ws3_d, ws2_d, w1b_d, w3b_d, w2b_d))
    emit_stage_c(nc, p, cn, act2, wtk_d, wtv_d, wtr_d, wck_d, wcv_d, wcr_d, vc_d, act1)
    emit_stage_d(nc, p, cn, act1, w1_d, w3_d, w2_d, ws1_d, ws3_d, ws2_d, gw_d, gb_d, out_d,
                 w1b_d=w1b_d, w3b_d=w3b_d, w2b_d=w2b_d, prep_here=False)
    p.emit()
    return nc


def _prep_fused(inputs, core):
    xs = inputs["L0_attn_x"][2 * core: 2 * core + 2].reshape(TC, D)
    m = {"x": np.ascontiguousarray(xs.T)}
    m.update({
        "wqkv": np.ascontiguousarray(inputs["L0_attn_Wqkv"]),
        "wout": np.ascontiguousarray(inputs["L0_attn_Wout"]),
        "boutc": np.ascontiguousarray(inputs["L0_attn_bout"].reshape(NB, 128).T),
        "wgt": np.ascontiguousarray(inputs["L0_ffn_w_gate"].T),
        "wut": np.ascontiguousarray(inputs["L0_ffn_w_up"].T),
        "wdt": np.ascontiguousarray(inputs["L0_ffn_w_down"].T),
    })
    pc = _prep_stage_c(inputs, 0, None) if False else None
    def col8(name):
        return inputs[name].astype(np.float32).reshape(NB, 128).T
    vcols = np.concatenate([
        col8("L1_attn_time_decay"), col8("L1_attn_time_first"),
        col8("L1_attn_ln1_g"), col8("L1_attn_ln1_b"),
        col8("L1_attn_ln2_g"), col8("L1_attn_ln2_b"),
        inputs["L1_attn_mix_k"].astype(np.float32).reshape(NB, 128).T,
        inputs["L1_attn_mix_v"].astype(np.float32).reshape(NB, 128).T,
        inputs["L1_attn_mix_r"].astype(np.float32).reshape(NB, 128).T,
        inputs["L1_attn_cmix_k"].astype(np.float32).reshape(NB, 128).T,
        inputs["L1_attn_cmix_r"].astype(np.float32).reshape(NB, 128).T,
    ], axis=1)
    m.update({
        "wtk": np.ascontiguousarray(inputs["L1_attn_w_tk"]),
        "wtv": np.ascontiguousarray(inputs["L1_attn_w_tv"]),
        "wtr": np.ascontiguousarray(inputs["L1_attn_w_tr"]),
        "wck": np.ascontiguousarray(inputs["L1_attn_w_ck"]),
        "wcv": np.ascontiguousarray(inputs["L1_attn_w_cv"]),
        "wcr": np.ascontiguousarray(inputs["L1_attn_w_cr"]),
        "vcols": np.ascontiguousarray(vcols),
        "w1t": np.ascontiguousarray(inputs["L1_ffn_w1"].transpose(0, 2, 1)),
        "w3t": np.ascontiguousarray(inputs["L1_ffn_w3"].transpose(0, 2, 1)),
        "w2t": np.ascontiguousarray(inputs["L1_ffn_w2"].transpose(0, 2, 1)),
        "ws1t": np.ascontiguousarray(inputs["L1_ffn_ws1"].T),
        "ws3t": np.ascontiguousarray(inputs["L1_ffn_ws3"].T),
        "ws2t": np.ascontiguousarray(inputs["L1_ffn_ws2"].T),
        "gwt": np.ascontiguousarray(inputs["L1_ffn_gate_w"].T),
        "gbias": np.ascontiguousarray(inputs["L1_ffn_gate_bias"].reshape(1, 16)),
    })
    return m


def kernel(**inputs):
    cores = list(range(NCORES))
    nc = build_fused()
    in_maps = [_prep_fused(inputs, c) for c in cores]
    res = run_bass_kernel_spmd(nc, in_maps, core_ids=cores)
    out = np.empty((16, T, D), dtype=np.float32)
    for c in cores:
        out[2 * c: 2 * c + 2] = np.asarray(res.results[c]["out"]).T.reshape(BPC, T, D)
    return out


# version 11
# speedup vs baseline: 1.0711x; 1.0711x over previous
import numpy as np
import concourse.bass as bass
import concourse.mybir as mybir
from concourse.bass_utils import run_bass_kernel_spmd

F32 = mybir.dt.float32
BF16 = mybir.dt.bfloat16
I32 = mybir.dt.int32
AF = mybir.ActivationFunctionType
ALU = mybir.AluOpType
AX = mybir.AxisListType

NCORES = 8
T = 2048
BPC = 2
TC = T * BPC
D = 1024
NB = D // 128
LNEPS = 1e-5


class Sem:
    def __init__(self, h):
        self.h = h
        self.v = 0


class Prog:
    ENGINES = ("sync", "tensor", "vector", "scalar", "gpsimd")

    def __init__(self, nc):
        self.nc = nc
        self.q = {e: [] for e in self.ENGINES}
        self._sems = {}
        self.plan = []
        self._semname = {}
        globals()["_LAST_PROG"] = self

    def sem(self, name):
        if name not in self._sems:
            self._sems[name] = Sem(self.nc.alloc_semaphore(name))
            self._semname[id(self._sems[name])] = name
        return self._sems[name]

    def do(self, eng, meth, *a, inc=None, n=1, **kw):
        rec = {"kind": "op", "eng": eng, "meth": meth, "args": a, "kw": dict(kw),
               "incs": [], "qi": len(self.q[eng])}
        if inc is not None:
            inc.v += n
            rec["incs"].append((self._semname[id(inc)], n, inc.v))
            def f(e, a=a, kw=kw, inc=inc, n=n, meth=meth):
                ins = getattr(e, meth)(*a, **kw)
                ins.then_inc(inc.h, n)
                return ins
        else:
            def f(e, a=a, kw=kw, meth=meth):
                return getattr(e, meth)(*a, **kw)
        self.q[eng].append(f)
        self.plan.append(rec)

    def lastinc(self, eng, sem, n=1):
        sem.v += n
        f0 = self.q[eng][-1]
        for rec in reversed(self.plan):
            if rec["kind"] == "op" and rec["eng"] == eng and rec["qi"] == len(self.q[eng]) - 1:
                rec["incs"].append((self._semname[id(sem)], n, sem.v))
                break
        def f(e, f0=f0, sem=sem, n=n):
            ins = f0(e)
            ins.then_inc(sem.h, n)
            return ins
        self.q[eng][-1] = f

    def wait(self, eng, sem, v=None):
        vv = sem.v if v is None else v
        self.plan.append({"kind": "wait", "eng": eng, "sem": self._semname[id(sem)],
                          "v": vv, "full": (v is None)})
        self.q[eng].append(lambda e, h=sem.h, vv=vv: e.wait_ge(h, vv))

    def sbuf(self, stack, name, shape, dtype):
        self._tn = getattr(self, "_tn", 0) + 1
        return stack.enter_context(self.nc.sbuf_tensor(f"{name}_t{self._tn}", shape, dtype))

    def psum(self, stack, name, shape, dtype):
        self._tn = getattr(self, "_tn", 0) + 1
        return stack.enter_context(self.nc.psum_tensor(f"{name}_t{self._tn}", shape, dtype))

    def barrier(self):
        sb = self.sem("bar")
        for e in self.ENGINES:
            self.do(e, "nop", inc=sb)
        for e in self.ENGINES:
            self.wait(e, sb)

    def emit(self):
        with self.nc.Block() as block:
            for e in self.ENGINES:
                if self.q[e]:
                    def body(eng, e=e):
                        for f in self.q[e]:
                            f(eng)
                    getattr(block, e)(body)


def _mkconsts(nc, p):
    ident = nc.alloc_sbuf_tensor("c_ident", [128, 128], BF16)
    ident32 = nc.alloc_sbuf_tensor("c_ident32", [128, 128], F32)
    onest = nc.alloc_sbuf_tensor("c_onest", [128, 128], BF16)
    onesc = nc.alloc_sbuf_tensor("c_onesc", [128, 1], BF16)
    onesr = nc.alloc_sbuf_tensor("c_onesr", [1, 128], F32)
    epsc = nc.alloc_sbuf_tensor("c_epsc", [128, 1], F32)
    eps6c = nc.alloc_sbuf_tensor("c_eps6c", [128, 1], F32)
    tinyc = nc.alloc_sbuf_tensor("c_tiny", [128, 1], F32)
    s = p.sem("init")
    p.do("gpsimd", "memset", ident[:, :], 1.0, inc=s)
    p.do("gpsimd", "memset", ident32[:, :], 1.0, inc=s)
    p.do("gpsimd", "memset", onest[:, :], 1.0, inc=s)
    p.do("gpsimd", "memset", onesc[:, :], 1.0, inc=s)
    p.do("gpsimd", "memset", onesr[:, :], 1.0, inc=s)
    p.do("gpsimd", "memset", epsc[:, :], LNEPS, inc=s)
    p.do("gpsimd", "memset", eps6c[:, :], 1e-6, inc=s)
    p.do("gpsimd", "memset", tinyc[:, :], 1e-30, inc=s)
    p.wait("gpsimd", s)
    p.do("gpsimd", "affine_select", ident[:, :], ident[:, :], [[1, 128]],
         ALU.is_equal, 0.0, base=0, channel_multiplier=-1, inc=s)
    p.wait("gpsimd", s)
    p.do("gpsimd", "affine_select", ident32[:, :], ident32[:, :], [[1, 128]],
         ALU.is_equal, 0.0, base=0, channel_multiplier=-1, inc=s)
    return dict(ident=ident, ident32=ident32, onest=onest, onesc=onesc,
                onesr=onesr, epsc=epsc, eps6c=eps6c, tinyc=tinyc)


def _ln_stats_chunk(p, onest, xap3, ps_sum, ps_ssq, xsq, mean, alpha, s_mm, s_ve, s_se, epsc=None):
    L = xap3.shape[2]
    p.do("vector", "tensor_mul", xsq[:, :, :], xap3, xap3, inc=s_ve)
    for cb in range(NB):
        p.do("tensor", "matmul", ps_sum, onest[:, :], xap3[:, cb, :],
             start=(cb == 0), stop=(cb == NB - 1), inc=(s_mm if cb == NB - 1 else None))
    p.wait("tensor", s_ve)
    for cb in range(NB):
        p.do("tensor", "matmul", ps_ssq, onest[:, :], xsq[:, cb, :],
             start=(cb == 0), stop=(cb == NB - 1), inc=(s_mm if cb == NB - 1 else None))
    p.wait("vector", s_mm)
    p.do("vector", "tensor_scalar", mean, ps_sum, 1.0 / D, None, ALU.mult, inc=s_ve)
    p.wait("vector", s_ve)
    p.do("vector", "tensor_mul", alpha, mean, mean, inc=s_ve)
    p.wait("vector", s_ve)
    p.do("vector", "scalar_tensor_tensor", alpha, ps_ssq, 1.0 / D, alpha,
         ALU.mult, ALU.subtract, inc=s_ve)
    p.wait("scalar", s_ve)
    p.do("scalar", "activation", alpha, alpha, AF.Ln, bias=epsc[:, :])
    p.do("scalar", "activation", alpha, alpha, AF.Exp, scale=-0.5, inc=s_se)


def _ln_apply_chunk(p, xap3, mean, alpha, out3, tmp, s_ve, s_se):
    L = xap3.shape[2]
    mb = mean.unsqueeze(1).broadcast_to([128, NB, L])
    ab = alpha.unsqueeze(1).broadcast_to([128, NB, L])
    p.wait("vector", s_se)
    p.do("vector", "tensor_tensor", tmp[:, :, :], xap3, mb, ALU.subtract)
    p.do("vector", "tensor_tensor", out3, tmp[:, :, :], ab, ALU.mult, inc=s_ve)


def emit_stage_a(nc, p, cn, x_d, wqkv_d, wout_d, bout_d, ox_d, debug=False):
    import contextlib
    stack = contextlib.ExitStack()
    _dbg_outs = []
    def dbg(p, name, ap, when):
        if not debug:
            return
        d = nc.dram_tensor("dbg_" + name, list(ap.shape), ap.dtype, kind="ExternalOutput").ap()
        _dbg_outs.append("dbg_" + name)
        sd = p.sem("dbgst"); sb = p.sem("blk")
        for s in when:
            p.wait("sync", s)
        p.do("sync", "dma_start", out=d, in_=ap, inc=sd, n=16)
        p.wait("sync", sd)
        p.do("sync", "sem_inc", sb.h, 1)
        sb.v += 1
        for e in ("tensor", "vector", "scalar", "gpsimd"):
            p.wait(e, sb)
    ident, onest, onesc, epsc, eps6c = cn["ident"], cn["onest"], cn["onesc"], cn["epsc"], cn["eps6c"]

    xsb = p.sbuf(stack, "xsb", [128, NB * TC], BF16)
    wq = p.sbuf(stack, "wq", [128, NB * 3 * D], BF16)
    wo = p.sbuf(stack, "wo", [128, NB * D], BF16)
    bo = p.sbuf(stack, "bo", [128, NB], F32)
    xln = p.sbuf(stack, "xln", [128, NB * T], BF16)
    qfm = p.sbuf(stack, "qfm", [128, NB * 512], BF16)
    ktm = p.sbuf(stack, "ktm", [128, D], BF16)
    VW = 130
    vtm = p.sbuf(stack, "vtm", [128, NB * VW], BF16)
    kvd = p.sbuf(stack, "kvd", [128, NB * 128], BF16)
    ks2 = p.sbuf(stack, "ks2", [128, NB * 2], BF16)
    att = ktm
    afm = p.sbuf(stack, "afm", [128, NB * 512], BF16)
    rden = p.sbuf(stack, "rden", [128, 16], F32)

    mean = p.sbuf(stack, "mean", [128, 512], F32)
    alpha = p.sbuf(stack, "alpha", [128, 512], F32)
    tA = p.sbuf(stack, "tA", [128, NB * 512], F32)
    tB = p.sbuf(stack, "tB", [128, 1024], F32)
    xsq = afm

    s_ld = p.sem("ld"); s_mm = p.sem("mm"); s_ve = p.sem("ve")
    s_se = p.sem("se"); s_gp = p.sem("gp"); s_st = p.sem("st")

    ps_big = p.psum(stack, "ps_big", [128, 2048], F32)
    ps_kv = p.psum(stack, "ps_kv", [128, NB * 128], F32)
    ps_ks = p.psum(stack, "ps_ks", [128, 16], F32)
    ps_tr = p.psum(stack, "ps_tr", [128, NB * 128], BF16)

    for cb in range(NB):
        p.do("gpsimd", "dma_start", out=xsb[:, cb * TC:(cb + 1) * TC],
             in_=x_d[cb * 128:(cb + 1) * 128, :], inc=s_ld, n=16)
        p.do("gpsimd", "dma_start", out=wq[:, cb * 3 * D:(cb + 1) * 3 * D],
             in_=wqkv_d[cb * 128:(cb + 1) * 128, :], inc=s_ld, n=16)
        p.do("gpsimd", "dma_start", out=wo[:, cb * D:(cb + 1) * D],
             in_=wout_d[cb * 128:(cb + 1) * 128, :], inc=s_ld, n=16)
    s_ldh = p.sem("ldh")
    p.do("sync", "dma_start", out=bo[:, :], in_=bout_d[:, :], inc=s_ldh, n=16)
    p.wait("vector", s_ld); p.wait("vector", s_ldh)
    p.wait("tensor", s_ld); p.wait("tensor", s_ldh)
    p.wait("scalar", s_ld); p.wait("scalar", s_ldh)
    p.wait("tensor", p.sem("init"))
    for cb in range(NB):
        p.do("vector", "memset", vtm[:, cb * VW + 128: cb * VW + 130], 1.0)
    p.lastinc("vector", s_ve)

    qfm_f32 = qfm[:, :].bitcast(F32)
    kvacc = qfm_f32[:, 0:1024]
    ksacc = qfm_f32[:, 1024:1032]
    x3 = lambda t0, L: xsb[:, :].rearrange("p (c t) -> p c t", c=NB)[:, :, t0:t0 + L]
    xln3 = lambda t0, L: xln[:, :].rearrange("p (c t) -> p c t", c=NB)[:, :, t0:t0 + L]

    def elu_from_psum(ps_ap, out_ap, w, off=0):
        a = tA[:, off:off + w]; b = tB[:, off:off + w]
        p.do("vector", "tensor_scalar", b, ps_ap, 0.0, None, ALU.max)
        p.do("vector", "tensor_sub", a, b, ps_ap, inc=s_ve)
        p.wait("scalar", s_ve)
        p.do("scalar", "activation", a, a, AF.Exp, scale=-1.0, inc=s_se)
        p.wait("vector", s_se)
        p.do("vector", "tensor_add", out_ap, b, a, inc=s_ve)

    NCH = T // 512

    for b in range(BPC):
        for c in range(NCH):
            t0 = b * T + c * 512
            tl0 = c * 512
            p.wait("tensor", s_ve)
            p.wait("tensor", s_se)
            p.wait("vector", s_st)
            p.wait("scalar", s_st)
            _ln_stats_chunk(p, onest, x3(t0, 512), ps_big[:, 0:512], ps_big[:, 512:1024],
                            xsq[:, :].rearrange("p (c t) -> p c t", c=NB), mean[:, :], alpha[:, :],
                            s_mm, s_ve, s_se, epsc=epsc)
            _ln_apply_chunk(p, x3(t0, 512), mean[:, :], alpha[:, :],
                            xln3(tl0, 512), tA[:, :].rearrange("p (c t) -> p c t", c=NB), s_ve, s_se)
            if b == 0 and c == 0:
                dbg(p, "xln0", xln[:, 0:512], [s_ve])
                dbg(p, "xln0b", xln[:, T:T + 512], [s_ve])
            for tt in range(4):
                tx = tl0 + tt * 128
                if b == 0 and c == 0 and tt == 0:
                    _asnap_ve, _asnap_se, _ait = [], [], [0]
                if tt == 0:
                    p.wait("tensor", s_ve)
                for item in range(4):
                    slot = _ait[0] % 4
                    sl = ps_big[:, slot * 512:(slot + 1) * 512]
                    if _ait[0] >= 4:
                        p.wait("tensor", s_ve, _asnap_ve[_ait[0] - 4])
                        p.wait("tensor", s_se, _asnap_se[_ait[0] - 4])
                    half = item % 2
                    cof = D + half * 512 if item < 2 else 2 * D + half * 512
                    for cb in range(NB):
                        p.do("tensor", "matmul", sl,
                             xln[:, cb * T + tx: cb * T + tx + 128],
                             wq[:, cb * 3 * D + cof: cb * 3 * D + cof + 512],
                             start=(cb == 0), stop=(cb == NB - 1))
                    p.lastinc("tensor", s_mm)
                    p.wait("vector", s_mm)
                    if item < 2:
                        elu_from_psum(sl, ktm[:, half * 512:(half + 1) * 512], 512, off=half * 512)
                    else:
                        v3 = vtm[:, :].rearrange("p (c w) -> p c w", c=NB)[:, half * 4:(half + 1) * 4, 0:128]
                        p.do("vector", "tensor_copy", v3,
                             sl.rearrange("p (c w) -> p c w", c=4), inc=s_ve)
                    _asnap_ve.append(s_ve.v)
                    _asnap_se.append(s_se.v)
                    _ait[0] += 1
                p.wait("tensor", s_ve)
                for cb in range(NB):
                    p.do("tensor", "matmul", ps_kv[:, cb * 128:(cb + 1) * 128],
                         ktm[:, cb * 128:(cb + 1) * 128],
                         vtm[:, cb * VW: cb * VW + 128],
                         start=True, stop=True)
                for cb in range(NB):
                    p.do("tensor", "matmul", ps_ks[:, cb: cb + 1],
                         ktm[:, cb * 128:(cb + 1) * 128], onesc[:, :],
                         start=True, stop=True, inc=s_mm)
                p.wait("vector", s_mm)
                if tt == 0 and c == 0:
                    p.do("vector", "tensor_copy", kvacc, ps_kv[:, :])
                    p.do("vector", "tensor_copy", ksacc, ps_ks[:, 0:8], inc=s_ve)
                else:
                    p.do("vector", "tensor_add", kvacc, kvacc, ps_kv[:, :])
                    p.do("vector", "tensor_add", ksacc, ksacc, ps_ks[:, 0:8], inc=s_ve)
                if b == 0 and c == 0 and tt == 0:
                    dbg(p, "ktm0", ktm[:, :], [s_ve])
                    dbg(p, "vtm0", vtm[:, :], [s_ve])
        p.do("vector", "memset", kvd[:, :], 0.0)
        p.do("vector", "memset", ks2[:, :], 0.0, inc=s_ve)
        p.wait("vector", s_ve)
        for cb in range(NB):
            p.do("vector", "tensor_copy", kvd[0:64, cb * 128: cb * 128 + 64],
                 kvacc[0:64, cb * 128: cb * 128 + 64])
            p.do("vector", "tensor_copy", kvd[64:128, cb * 128 + 64: cb * 128 + 128],
                 kvacc[64:128, cb * 128 + 64: cb * 128 + 128])
            p.do("vector", "tensor_copy", ks2[0:64, cb * 2: cb * 2 + 1], ksacc[0:64, cb: cb + 1])
            p.do("vector", "tensor_copy", ks2[64:128, cb * 2 + 1: cb * 2 + 2], ksacc[64:128, cb: cb + 1])
        p.lastinc("vector", s_ve)
        if b == 0:
            dbg(p, "kvacc", kvacc, [s_ve])
            dbg(p, "ksacc", ksacc, [s_ve])
            dbg(p, "kvd", kvd[:, :], [s_ve])
            dbg(p, "ks2", ks2[:, :], [s_ve])

        for c in range(NCH):
            t0 = b * T + c * 512
            tl0 = c * 512
            p.wait("vector", s_st)
            p.wait("scalar", s_st)
            if b == 0 and c == 0:
                _qsnap_ve, _qsnap_se, _qit = [], [], [0]
            for ob in range(NB):
                q = _qit[0] % 2
                if ob == 0:
                    p.wait("tensor", s_ve)
                    p.wait("tensor", s_se)
                if _qit[0] >= 2:
                    p.wait("tensor", s_ve, _qsnap_ve[_qit[0] - 2])
                    p.wait("tensor", s_se, _qsnap_se[_qit[0] - 2])
                for cb in range(NB):
                    p.do("tensor", "matmul", ps_big[:, q * 512:(q + 1) * 512],
                         wq[:, cb * 3 * D + ob * 128: cb * 3 * D + ob * 128 + 128],
                         xln[:, cb * T + tl0: cb * T + tl0 + 512],
                         start=(cb == 0), stop=(cb == NB - 1))
                p.lastinc("tensor", s_mm)
                p.wait("vector", s_mm)
                elu_from_psum(ps_big[:, q * 512:(q + 1) * 512], qfm[:, ob * 512:(ob + 1) * 512],
                              512, off=q * 512)
                _qsnap_ve.append(s_ve.v)
                _qsnap_se.append(s_se.v)
                _qit[0] += 1
            if b == 0 and c == 0:
                dbg(p, "qfm0", qfm[:, :], [s_ve])
            for tt in range(4):
                q3 = qfm[:, :].rearrange("p (c t) -> p c t", c=NB)[:, :, tt * 128:(tt + 1) * 128]
                p.wait("tensor", s_ve)
                for cb in range(NB):
                    p.do("tensor", "matmul", ps_big[:, 1024 + cb * 128: 1024 + (cb + 1) * 128],
                         q3[:, cb, :], kvd[:, cb * 128:(cb + 1) * 128],
                         start=True, stop=True)
                for cb in range(NB):
                    p.do("tensor", "matmul", ps_ks[:, cb * 2:(cb + 1) * 2],
                         q3[:, cb, :], ks2[:, cb * 2:(cb + 1) * 2],
                         start=True, stop=True, inc=s_mm)
                p.wait("vector", s_mm)
                if debug and b == 0 and c == 0 and tt == 0:
                    p.do("vector", "tensor_copy", tB[:, 0:16], ps_ks[:, :])
                    p.do("vector", "tensor_copy", tB[:, 16:144], ps_big[:, 1024:1152], inc=s_ve)
                    dbg(p, "qksps", tB[:, 0:16], [s_ve])
                    dbg(p, "qkvps", tB[:, 16:144], [s_ve])
                p.wait("scalar", s_mm)
                p.do("scalar", "activation", rden[:, :], ps_ks[:, :], AF.Ln, bias=eps6c[:, :])
                p.do("scalar", "activation", rden[:, :], rden[:, :], AF.Exp, scale=-1.0, inc=s_se)
                p.wait("vector", s_se)
                rb = rden[:, :].unsqueeze(2).broadcast_to([128, 16, 64])
                p.do("vector", "tensor_tensor", att[:, :].rearrange("p (h f) -> p h f", h=16),
                     ps_big[:, 1024:2048].rearrange("p (h f) -> p h f", h=16), rb, ALU.mult, inc=s_ve)
                p.wait("tensor", s_ve)
                p.wait("tensor", s_se)
                for cb in range(NB):
                    p.do("tensor", "transpose", ps_tr[:, cb * 128:(cb + 1) * 128],
                         att[:, cb * 128:(cb + 1) * 128], ident[:, :])
                p.lastinc("tensor", s_mm)
                p.wait("scalar", s_mm)
                p.do("scalar", "copy",
                     afm[:, :].rearrange("p (c t) -> p c t", c=NB)[:, :, tt * 128:(tt + 1) * 128],
                     ps_tr[:, :].rearrange("p (c t) -> p c t", c=NB), inc=s_se)
                if b == 0 and c == 0 and tt == 0:
                    dbg(p, "att0", att[:, :], [s_ve])
                    dbg(p, "rden0", rden[:, :], [s_ve])
                    dbg(p, "afm0", afm[:, 0:128], [s_se])
            p.wait("tensor", s_se)
            p.wait("tensor", s_ve)
            if b == 0 and c == 0:
                _wsnap2, _wit2 = [], [0]
            for ob in range(NB):
                if _wit2[0] >= 2:
                    p.wait("tensor", s_ve, _wsnap2[_wit2[0] - 2])
                for cb in range(NB):
                    p.do("tensor", "matmul", ps_big[:, (ob % 2) * 512:(ob % 2 + 1) * 512],
                         wo[:, cb * D + ob * 128: cb * D + ob * 128 + 128],
                         afm[:, cb * 512:(cb + 1) * 512],
                         start=(cb == 0), stop=(cb == NB - 1))
                p.lastinc("tensor", s_mm)
                p.wait("vector", s_mm)
                p.do("vector", "tensor_scalar", tB[:, (ob % 2) * 512:(ob % 2) * 512 + 512],
                     ps_big[:, (ob % 2) * 512:(ob % 2 + 1) * 512],
                     bo[:, ob:ob + 1], None, ALU.add)
                p.do("vector", "tensor_add", tA[:, ob * 512:(ob + 1) * 512],
                     tB[:, (ob % 2) * 512:(ob % 2) * 512 + 512],
                     xsb[:, ob * TC + t0: ob * TC + t0 + 512], inc=s_ve)
                _wsnap2.append(s_ve.v)
                _wit2[0] += 1
            p.wait("sync", s_ve)
            for ob in range(NB):
                p.do("sync", "dma_start", out=ox_d[ob * 128:(ob + 1) * 128, t0:t0 + 512],
                     in_=tA[:, ob * 512:(ob + 1) * 512], inc=s_st, n=16)

    p.wait("sync", s_st)
    stack.close()
    p.barrier()


def build_stage_a(debug=False):
    nc = bass.Bass("TRN2", target_bir_lowering=False, detect_race_conditions=False)
    p = Prog(nc)
    cn = _mkconsts(nc, p)
    x_d = nc.dram_tensor("x", [D, TC], F32, kind="ExternalInput").ap()
    wqkv_d = nc.dram_tensor("wqkv", [D, 3 * D], F32, kind="ExternalInput").ap()
    wout_d = nc.dram_tensor("wout", [D, D], F32, kind="ExternalInput").ap()
    bout_d = nc.dram_tensor("boutc", [128, NB], F32, kind="ExternalInput").ap()
    ox_d = nc.dram_tensor("ox", [D, TC], F32, kind="ExternalOutput").ap()
    emit_stage_a(nc, p, cn, x_d, wqkv_d, wout_d, bout_d, ox_d, debug=debug)
    p.emit()
    return nc



def _prep_stage_a(inputs, core):
    x = inputs["L0_attn_x"]
    xs = x[2 * core: 2 * core + 2].reshape(TC, D)
    return {
        "x": np.ascontiguousarray(xs.T),
        "wqkv": np.ascontiguousarray(inputs["L0_attn_Wqkv"]),
        "wout": np.ascontiguousarray(inputs["L0_attn_Wout"]),
        "boutc": np.ascontiguousarray(inputs["L0_attn_bout"].reshape(NB, 128).T),
    }


def run_stage_a(inputs, cores=None):
    cores = list(range(NCORES)) if cores is None else cores
    nc = build_stage_a()
    in_maps = [_prep_stage_a(inputs, c) for c in cores]
    res = run_bass_kernel_spmd(nc, in_maps, core_ids=list(range(len(cores))))
    return [r["ox"] for r in res.results]


def emit_stage_b(nc, p, cn, x_d, wg_d, wu_d, wd_d, ox_d, prep_args=None):
    import contextlib
    stack = contextlib.ExitStack()
    ident, onest, onesc, epsc, eps6c = cn["ident"], cn["onest"], cn["onesc"], cn["epsc"], cn["eps6c"]
    HID = 3072
    HH = HID // 2
    NIB = HH // 128
    S2 = float(1.0 / (127.0 * 127.0))

    xsb = p.sbuf(stack, "xsb", [128, NB * TC], BF16)
    wgh = p.sbuf(stack, "wgh", [128, NB * HH], BF16)
    wuh = p.sbuf(stack, "wuh", [128, NB * HH], BF16)
    wdh = p.sbuf(stack, "wdh", [128, NIB * D], BF16)
    hbuf = p.sbuf(stack, "hbuf", [128, NIB * 512], BF16)
    xln = p.sbuf(stack, "xlnb", [128, NB * 512], BF16)
    sg = p.sbuf(stack, "sg", [128, 512], BF16)
    sg2 = p.sbuf(stack, "sg2", [128, 512], BF16)
    mean = p.sbuf(stack, "mean", [128, 512], F32)
    alpha = p.sbuf(stack, "alpha", [128, 512], F32)
    tA = p.sbuf(stack, "tA", [128, NB * 512], F32)
    stg = p.sbuf(stack, "stg", [128, NB * 512], F32)
    xsq = hbuf[:, 0:4096].bitcast(BF16)

    s_ld = p.sem("ld"); s_mm = p.sem("mm"); s_ve = p.sem("ve")
    s_se = p.sem("se"); s_st = p.sem("st")

    prep_gen = None
    if prep_args is not None:
        ringb = p.sbuf(stack, "prep_ring", [128, 6144], BF16)
        prep_gen = gen_prep_moe(nc, p, ringb, *prep_args)
    ps_st = p.psum(stack, "ps_st", [128, 1024], F32)
    ps_g = [p.psum(stack, f"ps_g{i}", [128, 512], F32) for i in range(2)]
    ps_u = [p.psum(stack, f"ps_u{i}", [128, 512], F32) for i in range(2)]
    ps_y = p.psum(stack, "ps_y", [128, 1024], F32)
    sgq = [sg, sg2]

    for cb in range(NB):
        p.do("gpsimd", "dma_start", out=xsb[:, cb * TC:(cb + 1) * TC],
             in_=x_d[cb * 128:(cb + 1) * 128, :], inc=s_ld, n=16)

    x3 = lambda t0, L: xsb[:, :].rearrange("p (c t) -> p c t", c=NB)[:, :, t0:t0 + L]
    xln3 = xln[:, :].rearrange("p (c t) -> p c t", c=NB)

    for half in range(2):
        p.wait("gpsimd", s_mm)
        for cb in range(NB):
            p.do("gpsimd", "dma_start", out=wgh[:, cb * HH:(cb + 1) * HH],
                 in_=wg_d[cb * 128:(cb + 1) * 128, half * HH:(half + 1) * HH], inc=s_ld, n=16)
            p.do("gpsimd", "dma_start", out=wuh[:, cb * HH:(cb + 1) * HH],
                 in_=wu_d[cb * 128:(cb + 1) * 128, half * HH:(half + 1) * HH], inc=s_ld, n=16)
        for ib in range(NIB):
            p.do("gpsimd", "dma_start", out=wdh[:, ib * D:(ib + 1) * D],
                 in_=wd_d[half * HH + ib * 128: half * HH + (ib + 1) * 128, :], inc=s_ld, n=16)
        p.wait("tensor", s_ld)
        p.wait("vector", s_ld)

        for c in range(TC // 512):
            t0 = c * 512
            p.wait("vector", s_mm)
            p.wait("tensor", s_ve)
            p.wait("tensor", s_se)
            _ln_stats_chunk(p, onest, x3(t0, 512), ps_st[:, 0:512], ps_st[:, 512:1024],
                            xsq.rearrange("p (c t) -> p c t", c=NB), mean[:, :], alpha[:, :],
                            s_mm, s_ve, s_se, epsc=epsc)
            _ln_apply_chunk(p, x3(t0, 512), mean[:, :], alpha[:, :],
                            xln3[:, :, :], tA[:, :].rearrange("p (c t) -> p c t", c=NB), s_ve, s_se)
            if c == 0 and half == 0:
                _bsnap_se, _bsnap_ve, _bit = [], [], [0]
            for ib in range(NIB):
                q = _bit[0] % 2
                if ib == 0:
                    p.wait("tensor", s_ve)
                    p.wait("tensor", s_se)
                if _bit[0] >= 2:
                    p.wait("tensor", s_se, _bsnap_se[_bit[0] - 2])
                    p.wait("tensor", s_ve, _bsnap_ve[_bit[0] - 2])
                for cb in range(NB):
                    p.do("tensor", "matmul", ps_g[q][:, :],
                         wgh[:, cb * HH + ib * 128: cb * HH + (ib + 1) * 128],
                         xln[:, cb * 512:(cb + 1) * 512],
                         start=(cb == 0), stop=(cb == NB - 1))
                p.lastinc("tensor", s_mm)
                for cb in range(NB):
                    p.do("tensor", "matmul", ps_u[q][:, :],
                         wuh[:, cb * HH + ib * 128: cb * HH + (ib + 1) * 128],
                         xln[:, cb * 512:(cb + 1) * 512],
                         start=(cb == 0), stop=(cb == NB - 1))
                p.lastinc("tensor", s_mm)
                p.wait("scalar", s_mm, s_mm.v - 1)
                if _bit[0] >= 2:
                    p.wait("scalar", s_ve, _bsnap_ve[_bit[0] - 2])
                p.do("scalar", "activation", sgq[q][:, :], ps_g[q][:, :], AF.Silu,
                     scale=float(1.0 / 127.0), inc=s_se)
                p.wait("vector", s_mm)
                p.wait("vector", s_se)
                p.do("vector", "tensor_mul", hbuf[:, ib * 512:(ib + 1) * 512],
                     sgq[q][:, :], ps_u[q][:, :], inc=s_ve)
                _bsnap_se.append(s_se.v)
                _bsnap_ve.append(s_ve.v)
                _bit[0] += 1
            p.wait("vector", s_st)
            p.wait("tensor", s_ve)
            if c == 0 and half == 0:
                _dsnap, _dit = [], [0]
            for ob in range(NB):
                bank = ob % 2
                if _dit[0] >= 2:
                    p.wait("tensor", s_ve, _dsnap[_dit[0] - 2])
                for ib in range(NIB):
                    p.do("tensor", "matmul", ps_y[:, bank * 512:(bank + 1) * 512],
                         wdh[:, ib * D + ob * 128: ib * D + ob * 128 + 128],
                         hbuf[:, ib * 512:(ib + 1) * 512],
                         start=(ib == 0), stop=(ib == NIB - 1))
                p.lastinc("tensor", s_mm)
                p.wait("vector", s_mm)
                if half == 0:
                    p.do("vector", "scalar_tensor_tensor", stg[:, ob * 512:(ob + 1) * 512],
                         ps_y[:, bank * 512:(bank + 1) * 512], S2,
                         xsb[:, ob * TC + t0: ob * TC + t0 + 512],
                         ALU.mult, ALU.add, inc=s_ve)
                else:
                    p.do("vector", "tensor_scalar", stg[:, ob * 512:(ob + 1) * 512],
                         ps_y[:, bank * 512:(bank + 1) * 512], S2, None,
                         ALU.mult, inc=s_ve)
                _dsnap.append(s_ve.v)
                _dit[0] += 1
            p.wait("gpsimd", s_ve)
            for ob in range(NB):
                if half == 0:
                    p.do("gpsimd", "dma_start", out=ox_d[ob * 128:(ob + 1) * 128, t0:t0 + 512],
                         in_=stg[:, ob * 512:(ob + 1) * 512], inc=s_st, n=16)
                else:
                    p.do("gpsimd", "dma_start", out=ox_d[ob * 128:(ob + 1) * 128, t0:t0 + 512],
                         in_=stg[:, ob * 512:(ob + 1) * 512], accum_op=ALU.add, inc=s_st, n=16)
            if prep_gen is not None:
                for _ in range(4):
                    next(prep_gen, None)

    p.wait("gpsimd", s_st)
    if prep_gen is not None:
        for _ in prep_gen:
            pass
    stack.close()
    p.barrier()


def build_stage_b():
    nc = bass.Bass("TRN2", target_bir_lowering=False, detect_race_conditions=False)
    p = Prog(nc)
    cn = _mkconsts(nc, p)
    HID = 3072
    x_d = nc.dram_tensor("x", [D, TC], F32, kind="ExternalInput").ap()
    wg_d = nc.dram_tensor("wgt", [D, HID], I32, kind="ExternalInput").ap()
    wu_d = nc.dram_tensor("wut", [D, HID], I32, kind="ExternalInput").ap()
    wd_d = nc.dram_tensor("wdt", [HID, D], I32, kind="ExternalInput").ap()
    ox_d = nc.dram_tensor("ox", [D, TC], F32, kind="ExternalOutput").ap()
    emit_stage_b(nc, p, cn, x_d, wg_d, wu_d, wd_d, ox_d)
    p.emit()
    return nc


def _prep_stage_b(inputs, core, x_full):
    xs = x_full[2 * core: 2 * core + 2].reshape(TC, D)
    return {
        "x": np.ascontiguousarray(xs.T),
        "wgt": np.ascontiguousarray(inputs["L0_ffn_w_gate"].T),
        "wut": np.ascontiguousarray(inputs["L0_ffn_w_up"].T),
        "wdt": np.ascontiguousarray(inputs["L0_ffn_w_down"].T),
    }


def run_stage_b(inputs, x_full, cores=None):
    cores = list(range(NCORES)) if cores is None else cores
    nc = build_stage_b()
    in_maps = [_prep_stage_b(inputs, c, x_full) for c in cores]
    res = run_bass_kernel_spmd(nc, in_maps, core_ids=list(range(len(cores))))
    return [r["ox"] for r in res.results]


def _ln_stats_generic(p, onest, xap3, ps_sum, ps_ssq, xsq3, mean, alpha, s_mm, s_ve, s_se, epsc):
    _ln_stats_chunk(p, onest, xap3, ps_sum, ps_ssq, xsq3, mean, alpha, s_mm, s_ve, s_se, epsc=epsc)


def emit_stage_c(nc, p, cn, x_d, wtk_d, wtv_d, wtr_d, wck_d, wcv_d, wcr_d, vc_d, ox_d,
                 trunc='full', debug=False, prep_gen=None):
    import contextlib
    stack = contextlib.ExitStack()
    L = 256
    NCH_W = T // L
    LC = 512
    CH = 4096
    HH = CH // 2
    NHB = HH // 128

    _dbg_names = []
    def dbg(name, ap, when):
        if not debug:
            return
        d = nc.dram_tensor("dbg_" + name, list(ap.shape), ap.dtype, kind="ExternalOutput").ap()
        _dbg_names.append("dbg_" + name)
        sd = p.sem("dbgst"); sb = p.sem("blk")
        for s in when:
            p.wait("sync", s)
        p.do("sync", "dma_start", out=d, in_=ap, inc=sd, n=16)
        p.wait("sync", sd)
        p.do("sync", "sem_inc", sb.h, 1)
        sb.v += 1
        for e in ("tensor", "vector", "scalar", "gpsimd"):
            p.wait(e, sb)
    ident, onest, onesc, epsc, eps6c = cn["ident"], cn["onest"], cn["onesc"], cn["epsc"], cn["eps6c"]
    tinyc = cn["tinyc"]

    z = p.sbuf(stack, "z", [128, NB * TC], BF16)
    vc = p.sbuf(stack, "vc", [128, 88], F32)
    dcol = p.sbuf(stack, "dcol", [128, 24], F32)
    mean = p.sbuf(stack, "mean", [128, LC], F32)
    alpha = p.sbuf(stack, "alpha", [128, LC], F32)

    s_ld = p.sem("ld"); s_ldh = p.sem("ldh"); s_mm = p.sem("mm"); s_ve = p.sem("ve")
    s_se = p.sem("se"); s_st = p.sem("st"); s_bar = p.sem("bar")

    def barrier():
        for e in Prog.ENGINES:
            p.do(e, "nop", inc=s_bar)
        for e in Prog.ENGINES:
            p.wait(e, s_bar)

    VI = {"td": 0, "tf": 1, "g1": 2, "b1": 3, "g2": 4, "b2": 5,
          "mk": 6, "mv": 7, "mr": 8, "cmk": 9, "cmr": 10}
    def vcol(name, cb):
        return vc[:, VI[name] * 8 + cb: VI[name] * 8 + cb + 1]
    def vcol8(name):
        return vc[:, VI[name] * 8: VI[name] * 8 + 8]

    p.do("sync", "dma_start", out=vc[:, :], in_=vc_d[:, :], inc=s_ldh, n=16)
    p.wait("scalar", s_ldh)
    p.wait("vector", s_ldh)
    p.do("scalar", "activation", dcol[:, 0:8], vcol8("td"), AF.Exp, inc=s_se)
    p.do("vector", "tensor_scalar", dcol[:, 8:16], vcol8("td"), 0.0, float(L), ALU.max, ALU.mult, inc=s_ve)
    EW = dcol[:, 0:8]; WRL = dcol[:, 8:16]

    zz3 = z[:, :].rearrange("p (c t) -> p c t", c=NB)

    def ln_apply_bf16(xap3, out3, gname=None, bname=None, Lx=LC):
        mb = mean[:, 0:Lx].unsqueeze(1).broadcast_to([128, NB, Lx])
        ab = alpha[:, 0:Lx].unsqueeze(1).broadcast_to([128, NB, Lx])
        p.wait("vector", s_se)
        p.do("vector", "tensor_tensor", out3, xap3, mb, ALU.subtract)
        p.do("vector", "tensor_tensor", out3, out3, ab, ALU.mult, inc=s_ve)
        if gname is not None:
            for cb in range(NB):
                p.do("vector", "tensor_scalar", out3[:, cb, :], out3[:, cb, :],
                     vcol(gname, cb), vcol(bname, cb), ALU.mult, ALU.add,
                     inc=(s_ve if cb == NB - 1 else None))

    def dump_z_and_finish():
        p.wait("gpsimd", s_ve)
        for cb in range(NB):
            p.do("gpsimd", "dma_start", out=ox_d[cb * 128:(cb + 1) * 128, :],
                 in_=z[:, cb * TC:(cb + 1) * TC], inc=s_st, n=16)
        p.wait("gpsimd", s_st)
        stack.close()
        p.barrier()

    with contextlib.ExitStack() as st0:
        xch = p.sbuf(st0, "xch", [128, NB * LC], BF16)
        xsq0 = p.sbuf(st0, "xsq0", [128, NB * LC], BF16)
        ps_st0 = p.psum(st0, "ps_st0", [128, 1024], F32)
        for c in range(TC // LC):
            t0 = c * LC
            p.wait("gpsimd", s_ve)
            for cb in range(NB):
                p.do("gpsimd", "dma_start", out=xch[:, cb * LC:(cb + 1) * LC],
                     in_=x_d[cb * 128:(cb + 1) * 128, t0:t0 + LC], inc=s_ld, n=16)
            p.wait("vector", s_ld)
            p.wait("tensor", s_ld)
            p.wait("tensor", s_ve)
            p.wait("tensor", s_se)
            x3c = xch[:, :].rearrange("p (c t) -> p c t", c=NB)
            _ln_stats_chunk(p, onest, x3c, ps_st0[:, 0:512], ps_st0[:, 512:1024],
                            xsq0[:, :].rearrange("p (c t) -> p c t", c=NB),
                            mean[:, :], alpha[:, :], s_mm, s_ve, s_se, epsc=epsc)
            ln_apply_bf16(x3c, zz3[:, :, t0:t0 + LC])
    barrier()
    if trunc == 'p0':
        dump_z_and_finish()
        return

    with contextlib.ExitStack() as stw:
        wtk = p.sbuf(stw, "wtk_s", [128, NB * D], BF16)
        wtv = p.sbuf(stw, "wtv_s", [128, NB * D], BF16)
        wtr = p.sbuf(stw, "wtr_s", [128, NB * D], BF16)
        hb = p.sbuf(stw, "hb", [128, NB * L], BF16)
        hpb = p.sbuf(stw, "hpb", [128, NB * L], BF16)
        mx = p.sbuf(stw, "mx", [128, NB * L], BF16)
        kf = p.sbuf(stw, "kf", [128, NB * L], F32)
        vb = p.sbuf(stw, "vb", [128, NB * L], BF16)
        rb = p.sbuf(stw, "rb", [128, NB * L], BF16)
        kexp = p.sbuf(stw, "kexp", [128, NB * L], BF16)
        zbb = p.sbuf(stw, "zbb", [128, NB * L], BF16)
        Bbuf = p.sbuf(stw, "Bbuf", [128, NB * (L + 2)], BF16)
        Abuf = p.sbuf(stw, "Abuf", [128, NB * (L + 2)], BF16)
        ppp = p.sbuf(stw, "ppp", [128, NB * L], F32)
        t1f = p.sbuf(stw, "t1f", [128, NB * L], F32)
        t2f = p.sbuf(stw, "t2f", [128, NB * L], F32)

        e1u = hpb
        mxr = p.sbuf(stw, "mxr", [128, NB * L], BF16)
        rdn = t2f
        e1b = kexp
        e2b = mx
        dnb = zbb
        xsqw = kexp
        sm = p.sbuf(stw, "sm", [128, 64], F32)
        psA = p.psum(stw, "ps_kA", [128, NB * L], F32)
        psB = p.psum(stw, "ps_kB", [128, NB * L], F32)

        for cb in range(NB):
            p.do("gpsimd", "dma_start", out=wtk[:, cb * D:(cb + 1) * D],
                 in_=wtk_d[cb * 128:(cb + 1) * 128, :], inc=s_ld, n=16)
            p.do("gpsimd", "dma_start", out=wtv[:, cb * D:(cb + 1) * D],
                 in_=wtv_d[cb * 128:(cb + 1) * 128, :], inc=s_ld, n=16)
            p.do("gpsimd", "dma_start", out=wtr[:, cb * D:(cb + 1) * D],
                 in_=wtr_d[cb * 128:(cb + 1) * 128, :], inc=s_ld, n=16)
        p.wait("tensor", s_ld)

        h3 = hb[:, :].rearrange("p (c t) -> p c t", c=NB)
        hp3 = hpb[:, :].rearrange("p (c t) -> p c t", c=NB)
        mx3 = mx[:, :].rearrange("p (c t) -> p c t", c=NB)
        k3 = kf[:, :].rearrange("p (c t) -> p c t", c=NB)
        B3 = Bbuf[:, :].rearrange("p (c t) -> p c t", c=NB)
        A3 = Abuf[:, :].rearrange("p (c t) -> p c t", c=NB)
        aac = sm[:, 0:8]; btc = sm[:, 8:16]; mprev = sm[:, 16:24]
        kmax = sm[:, 24:32]; m2 = sm[:, 32:40]; negm2 = sm[:, 40:48]
        sc1 = sm[:, 48:56]; sc2 = sm[:, 56:64]

        def fm_matmul(wsb, rhs_sb, out_ps, Lx, evac):
            p.wait("tensor", s_ve)
            p.wait("tensor", s_se)
            for ob in range(NB):
                for cb in range(NB):
                    p.do("tensor", "matmul", out_ps[:, ob * Lx:(ob + 1) * Lx],
                         wsb[:, cb * D + ob * 128: cb * D + ob * 128 + 128],
                         rhs_sb[:, cb * Lx:(cb + 1) * Lx],
                         start=(cb == 0), stop=(cb == NB - 1))
            p.lastinc("tensor", s_mm)

        kf2 = p.sbuf(stw, "kf2", [128, NB * L], F32)
        vb2 = p.sbuf(stw, "vb2", [128, NB * L], BF16)
        mxv = p.sbuf(stw, "mxv", [128, NB * L], BF16)
        e2uv = mxv
        pcol = p.sbuf(stw, "pcol", [128, 8], BF16)
        kfq = [kf, kf2]; vbq = [vb, vb2]
        NCHT = BPC * NCH_W

        def front(gi):
            b, c = divmod(gi, NCH_W)
            t0 = b * T + c * L
            q = gi % 2
            p.wait("vector", s_mm)
            p.wait("tensor", s_ve)
            p.wait("tensor", s_se)
            _ln_stats_chunk(p, onest, zz3[:, :, t0:t0 + L], psA[:, 0:L], psA[:, L:2 * L],
                            xsqw[:, 0:NB * L].rearrange("p (c t) -> p c t", c=NB),
                            mean[:, 0:L], alpha[:, 0:L], s_mm, s_ve, s_se, epsc=epsc)
            _ssqsnap[gi] = p.sem("mm").v
            ln_apply_bf16(zz3[:, :, t0:t0 + L], h3, "g1", "b1", Lx=L)
            if c == 0:
                p.do("vector", "memset", pcol[:, :], 0.0, inc=s_ve)
                p.wait("vector", s_ve)
            p.do("vector", "tensor_copy", hp3[:, :, 0], pcol[:, :])
            p.do("vector", "tensor_copy", hp3[:, :, 1:L], h3[:, :, 0:L - 1], inc=s_ve)
            p.wait("vector", s_ve)
            p.do("vector", "tensor_copy", pcol[:, :], h3[:, :, L - 1], inc=s_ve)
            p.wait("vector", s_ve)
            p.do("vector", "tensor_tensor", mx3[:, :, :], h3, hp3, ALU.subtract, inc=s_ve)
            for mixn, mb_ in (("mr", mxr), ("mv", mxv), ("mk", mx)):
                mb3 = mb_[:, :].rearrange("p (c t) -> p c t", c=NB)
                for cb in range(NB):
                    p.do("vector", "scalar_tensor_tensor", mb3[:, cb, :], mx3[:, cb, :],
                         vcol(mixn, cb), hp3[:, cb, :], ALU.mult, ALU.add,
                         inc=(s_ve if cb == NB - 1 else None))
            for mat, (wsb, rhs_sb, pso) in enumerate(((wtk, mx, psA), (wtv, mxv, psB))):
                p.wait("tensor", s_ve)
                p.wait("tensor", s_se)
                for ob in range(NB):
                    for cb in range(NB):
                        p.do("tensor", "matmul", pso[:, ob * L:(ob + 1) * L],
                             wsb[:, cb * D + ob * 128: cb * D + ob * 128 + 128],
                             rhs_sb[:, cb * L:(cb + 1) * L],
                             start=(cb == 0), stop=(cb == NB - 1))
                p.lastinc("tensor", s_mm)
                if mat == 0:
                    _kmmsnap[gi] = p.sem("mm").v

        def wkv(gi):
            b, c = divmod(gi, NCH_W)
            t0 = b * T + c * L
            q = gi % 2
            kfc = kfq[q]; vbc = vbq[q]; rbc = rb
            k3c = kfc[:, :].rearrange("p (c t) -> p c t", c=NB)
            if c == 0:
                p.wait("vector", s_se)
                p.do("vector", "memset", aac, 0.0)
                p.do("vector", "memset", btc, 0.0)
                p.do("vector", "memset", mprev, -1e30, inc=s_ve)
                p.wait("vector", s_ve)
            p.wait("vector", s_se)
            p.do("vector", "reduce_max", kmax, k3c, axis=AX.X, inc=s_ve)
            p.wait("vector", s_ve)
            p.do("vector", "tensor_max", m2, kmax, mprev, inc=s_ve)
            p.wait("vector", s_ve)
            p.do("vector", "tensor_add", m2, m2, WRL, inc=s_ve)
            p.wait("vector", s_ve)
            p.do("vector", "tensor_scalar", negm2, m2, -1.0, None, ALU.mult)
            p.do("vector", "tensor_sub", sc1, mprev, m2, inc=s_ve)
            p.wait("scalar", s_ve)
            p.do("scalar", "activation", sc1, sc1, AF.Exp, inc=s_se)
            p.wait("vector", s_se)
            p.do("vector", "tensor_mul", sc2, btc, sc1, inc=s_ve)
            p.wait("vector", s_ve)
            p.do("vector", "tensor_copy", B3[:, :, L], sc2, inc=s_ve)
            p.wait("vector", s_ve)
            p.do("vector", "tensor_copy", A3[:, :, L], aac, inc=s_ve)
            p.wait("vector", s_ve)
            p.wait("scalar", s_mm, _ssqsnap.get(gi + 1, 0))
            p.wait("scalar", s_ve)
            for cb in range(NB):
                p.do("scalar", "activation", kexp[:, cb * L:(cb + 1) * L], k3c[:, cb, :],
                     AF.Exp, bias=negm2[:, cb:cb + 1],
                     inc=(s_se if cb == NB - 1 else None))
            p.wait("vector", s_se)
            for cb in range(NB):
                p.do("vector", "tensor_scalar", zbb[:, cb * L:(cb + 1) * L],
                     kexp[:, cb * L:(cb + 1) * L], EW[:, cb:cb + 1], None, ALU.mult,
                     inc=(s_ve if cb == NB - 1 else None))
            for cb in range(NB):
                d0 = EW[:, cb:cb + 1].broadcast_to([128, L])
                p.do("vector", "tensor_tensor_scan", B3[:, cb, 0:L], d0,
                     zbb[:, cb * L:(cb + 1) * L], B3[:, cb, L:L + 1],
                     ALU.mult, ALU.add, inc=(s_ve if cb == NB - 1 else None))
            p.wait("scalar", s_ve)
            ppp3 = ppp[:, :].rearrange("p (c t) -> p c t", c=NB)
            p.do("scalar", "activation", ppp3[:, :, 1:L], B3[:, :, 0:L - 1],
                 AF.Ln, bias=tinyc[:, :])
            p.do("scalar", "activation", ppp3[:, :, 0:1], B3[:, :, L:L + 1],
                 AF.Ln, bias=tinyc[:, :], inc=s_se)
            p.wait("vector", s_se)
            for cb in range(NB):
                p.do("vector", "tensor_scalar", ppp[:, cb * L:(cb + 1) * L],
                     ppp[:, cb * L:(cb + 1) * L], m2[:, cb:cb + 1], None, ALU.add,
                     inc=(s_ve if cb == NB - 1 else None))
            p.do("vector", "tensor_max", t1f[:, :], ppp[:, :], kfc[:, :])
            p.do("vector", "tensor_sub", t2f[:, :], ppp[:, :], t1f[:, :], inc=s_ve)
            p.wait("scalar", s_ve)
            p.do("scalar", "activation", e1u[:, :], t2f[:, :], AF.Exp, inc=s_se)
            p.wait("vector", s_se)
            p.do("vector", "tensor_sub", t2f[:, :], kfc[:, :], t1f[:, :], inc=s_ve)
            p.wait("scalar", s_ve)
            p.wait("scalar", s_mm)
            p.do("scalar", "activation", e2uv[:, :], t2f[:, :], AF.Exp, inc=s_se)
            p.wait("vector", s_se)
            p.do("vector", "tensor_mul", e2uv[:, :], e2uv[:, :], vbc[:, :], inc=s_ve)
            for cb in range(NB):
                p.do("vector", "tensor_tensor_scan", A3[:, cb, 0:L],
                     e1u[:, cb * L:(cb + 1) * L], e2uv[:, cb * L:(cb + 1) * L],
                     A3[:, cb, L:L + 1], ALU.mult, ALU.add,
                     inc=(s_ve if cb == NB - 1 else None))
            for cb in range(NB):
                p.do("vector", "tensor_scalar", t1f[:, cb * L:(cb + 1) * L],
                     k3c[:, cb, :], vcol("tf", cb), None, ALU.add,
                     inc=(s_ve if cb == NB - 1 else None))
            p.wait("vector", s_se)
            p.do("vector", "tensor_max", t2f[:, :], ppp[:, :], t1f[:, :])
            p.do("vector", "tensor_sub", ppp[:, :], ppp[:, :], t2f[:, :], inc=s_ve)
            p.wait("scalar", s_ve)
            p.do("scalar", "activation", e1b[:, :], ppp[:, :], AF.Exp, inc=s_se)
            p.wait("vector", s_se)
            p.do("vector", "tensor_sub", t1f[:, :], t1f[:, :], t2f[:, :], inc=s_ve)
            p.wait("scalar", s_ve)
            p.do("scalar", "activation", e2b[:, :], t1f[:, :], AF.Exp, inc=s_se)
            p.wait("vector", s_se)
            p.do("vector", "tensor_add", dnb[:, :], e1b[:, :], e2b[:, :], inc=s_ve)
            p.wait("scalar", s_ve)
            p.do("scalar", "activation", rdn[:, :], dnb[:, :], AF.Ln)
            p.do("scalar", "activation", rdn[:, :], rdn[:, :], AF.Exp, scale=-1.0, inc=s_se)
            e1b3 = e1b[:, :].rearrange("p (c t) -> p c t", c=NB)
            p.do("vector", "tensor_tensor", e1b3[:, :, 1:L], e1b3[:, :, 1:L],
                 A3[:, :, 0:L - 1], ALU.mult)
            p.do("vector", "tensor_tensor", e1b3[:, :, 0:1], e1b3[:, :, 0:1],
                 A3[:, :, L:L + 1], ALU.mult)
            p.do("vector", "tensor_mul", e2b[:, :], e2b[:, :], vbc[:, :])
            p.do("vector", "tensor_add", e1b[:, :], e1b[:, :], e2b[:, :])
            p.wait("vector", s_se)
            p.do("vector", "tensor_mul", e1b[:, :], e1b[:, :], rdn[:, :])
            p.do("vector", "tensor_mul", e1b[:, :], e1b[:, :], rbc[:, :])
            p.do("vector", "tensor_tensor", zz3[:, :, t0:t0 + L], zz3[:, :, t0:t0 + L],
                 e1b[:, :].rearrange("p (c t) -> p c t", c=NB), ALU.add, inc=s_ve)
            p.do("vector", "tensor_copy", aac, A3[:, :, L - 1])
            p.do("vector", "tensor_copy", btc, B3[:, :, L - 1])
            p.do("vector", "tensor_copy", mprev, m2, inc=s_ve)
            p.wait("vector", s_ve)

        def tail(gi):
            q = gi % 2
            b, c = divmod(gi, NCH_W)
            p.wait("scalar", s_mm, _kmmsnap[gi])
            p.do("scalar", "copy", kfq[q][:, :], psA[:, :], inc=s_se)
            p.wait("scalar", s_mm)
            p.do("scalar", "copy", vbq[q][:, :], psB[:, :], inc=s_se)
            kevac_v = p.sem("se").v - 1
            p.wait("tensor", s_se, kevac_v)
            for ob in range(NB):
                for cb in range(NB):
                    p.do("tensor", "matmul", psA[:, ob * L:(ob + 1) * L],
                         wtr[:, cb * D + ob * 128: cb * D + ob * 128 + 128],
                         mxr[:, cb * L:(cb + 1) * L],
                         start=(cb == 0), stop=(cb == NB - 1))
            p.lastinc("tensor", s_mm)
            p.wait("scalar", s_mm)
            p.wait("scalar", s_ve)
            p.do("scalar", "activation", rb[:, :], psA[:, :], AF.Sigmoid, inc=s_se)

        _ssqsnap, _kmmsnap = {}, {}
        for gi in range(NCHT + 1):
            if gi < NCHT:
                front(gi)
            if gi >= 1:
                wkv(gi - 1)
            if gi < NCHT:
                tail(gi)
            if prep_gen is not None:
                for _ in range(4):
                    next(prep_gen, None)
    barrier()
    if trunc == 'w':
        dump_z_and_finish()
        return

    with contextlib.ExitStack() as stc:
        wck = p.sbuf(stc, "wck_s", [128, NB * HH], BF16)
        wcv = p.sbuf(stc, "wcv_s", [128, NHB * D], BF16)
        wcr = p.sbuf(stc, "wcr_s", [128, NB * D], BF16)
        h2b = p.sbuf(stc, "h2b", [128, NB * (LC + 1)], BF16)
        mxc = p.sbuf(stc, "mxc", [128, NB * LC], BF16)
        rrb = p.sbuf(stc, "rrb", [128, NB * LC], BF16)
        kkb = p.sbuf(stc, "kkb", [128, NHB * LC], BF16)
        sgc = p.sbuf(stc, "sgc", [128, LC], BF16)
        sgc2 = p.sbuf(stc, "sgc2", [128, LC], BF16)
        sgcq = [sgc, sgc2]
        stg = p.sbuf(stc, "stg", [128, 2 * LC], F32)
        mxc2 = p.sbuf(stc, "mxc2", [128, NB * LC], BF16)
        xsqc = mxc
        ps_st = p.psum(stc, "ps_stc", [128, 1024], F32)
        ps_g = p.psum(stc, "ps_gc", [128, 1024], F32)
        ps_y = p.psum(stc, "ps_yc", [128, 1024], F32)

        h2e = h2b[:, :].rearrange("p (c t) -> p c t", c=NB)
        h23 = h2e[:, :, 1:LC + 1]
        hp23 = h2e[:, :, 0:LC]
        mxc3 = mxc[:, :].rearrange("p (c t) -> p c t", c=NB)
        mxc23 = mxc2[:, :].rearrange("p (c t) -> p c t", c=NB)

        for cb in range(NB):
            p.do("gpsimd", "dma_start", out=wcr[:, cb * D:(cb + 1) * D],
                 in_=wcr_d[cb * 128:(cb + 1) * 128, :], inc=s_ld, n=16)
        def cm_front(b, c):
            t0f = b * T + c * LC
            if c == 0:
                p.do("vector", "memset", h2e[:, :, 0:1], 0.0, inc=s_ve)
            else:
                p.do("vector", "tensor_copy", h2e[:, :, 0:1], h2e[:, :, LC:LC + 1], inc=s_ve)
            yield
            p.wait("vector", s_mm)
            p.do("vector", "tensor_mul",
                 xsqc[:, 0:NB * LC].rearrange("p (c t) -> p c t", c=NB),
                 zz3[:, :, t0f:t0f + LC], zz3[:, :, t0f:t0f + LC], inc=s_ve)
            yield
            p.wait("tensor", s_ve)
            for cb in range(NB):
                p.do("tensor", "matmul", ps_st[:, 0:512], onest[:, :],
                     zz3[:, cb, t0f:t0f + LC], start=(cb == 0), stop=(cb == NB - 1),
                     inc=(s_mm if cb == NB - 1 else None))
            for cb in range(NB):
                p.do("tensor", "matmul", ps_st[:, 512:1024], onest[:, :],
                     xsqc[:, cb * LC:(cb + 1) * LC],
                     start=(cb == 0), stop=(cb == NB - 1),
                     inc=(s_mm if cb == NB - 1 else None))
            smsnap = p.sem("mm").v
            yield
            p.wait("vector", s_mm, smsnap)
            p.do("vector", "tensor_scalar", mean[:, :], ps_st[:, 0:512], 1.0 / D,
                 None, ALU.mult, inc=s_ve)
            p.wait("vector", s_ve)
            p.do("vector", "tensor_mul", alpha[:, :], mean[:, :], mean[:, :], inc=s_ve)
            p.wait("vector", s_ve)
            p.do("vector", "scalar_tensor_tensor", alpha[:, :], ps_st[:, 512:1024],
                 1.0 / D, alpha[:, :], ALU.mult, ALU.subtract, inc=s_ve)
            p.wait("scalar", s_ve)
            p.do("scalar", "activation", alpha[:, :], alpha[:, :], AF.Ln, bias=epsc[:, :])
            p.do("scalar", "activation", alpha[:, :], alpha[:, :], AF.Exp, scale=-0.5, inc=s_se)
            yield
            ln_apply_bf16(zz3[:, :, t0f:t0f + LC], h23, "g2", "b2", Lx=LC)
            yield
            p.wait("vector", s_mm)
            p.do("vector", "tensor_tensor", mxc3[:, :, :], h23, hp23, ALU.subtract)
            for cb in range(NB):
                p.do("vector", "scalar_tensor_tensor", mxc3[:, cb, :], mxc3[:, cb, :],
                     vcol("cmr", cb), hp23[:, cb, :], ALU.mult, ALU.add,
                     inc=(s_ve if cb == NB - 1 else None))
            yield
            p.do("vector", "tensor_tensor", mxc23[:, :, :], h23, hp23, ALU.subtract)
            for cb in range(NB):
                p.do("vector", "scalar_tensor_tensor", mxc23[:, cb, :], mxc23[:, cb, :],
                     vcol("cmk", cb), hp23[:, cb, :], ALU.mult, ALU.add,
                     inc=(s_ve if cb == NB - 1 else None))
            yield

        weave = lambda fg: (next(fg, None) if fg is not None else None)
        chunks = [(half, b, c) for half in range(2) for b in range(BPC)
                  for c in range(T // LC)]
        for gi, (half, b, c) in enumerate(chunks):
            t0 = b * T + c * LC
            if gi == 0 or chunks[gi - 1][0] != half:
                p.wait("gpsimd", s_mm)
                for cb in range(NB):
                    p.do("gpsimd", "dma_start", out=wck[:, cb * HH:(cb + 1) * HH],
                         in_=wck_d[cb * 128:(cb + 1) * 128, half * HH:(half + 1) * HH], inc=s_ld, n=16)
                for ib in range(NHB):
                    p.do("gpsimd", "dma_start", out=wcv[:, ib * D:(ib + 1) * D],
                         in_=wcv_d[half * HH + ib * 128: half * HH + (ib + 1) * 128, :], inc=s_ld, n=16)
                p.wait("tensor", s_ld)
            if gi == 0:
                for _ in cm_front(b, c):
                    pass
            fg = None
            if gi + 1 < len(chunks):
                nh, nb_, nch = chunks[gi + 1]
                fg = cm_front(nb_, nch)
            p.wait("tensor", s_ve)
            p.wait("tensor", s_se)
            p.wait("scalar", s_ve)
            if gi == 0:
                _rsnap, _rit = [], [0]
            for ob in range(NB):
                bank = _rit[0] % 2
                if _rit[0] >= 2:
                    p.wait("tensor", s_se, _rsnap[_rit[0] - 2])
                for cb in range(NB):
                    p.do("tensor", "matmul", ps_g[:, bank * 512:(bank + 1) * 512],
                         wcr[:, cb * D + ob * 128: cb * D + ob * 128 + 128],
                         mxc[:, cb * LC:(cb + 1) * LC],
                         start=(cb == 0), stop=(cb == NB - 1))
                p.lastinc("tensor", s_mm)
                p.wait("scalar", s_mm)
                p.do("scalar", "activation", rrb[:, ob * LC:(ob + 1) * LC],
                     ps_g[:, bank * 512:(bank + 1) * 512], AF.Sigmoid, inc=s_se)
                _rsnap.append(s_se.v)
                _rit[0] += 1
            p.wait("tensor", s_ve)
            p.wait("tensor", s_se)
            if gi == 0:
                _ksnap_ve, _ksnap_se, _kit = [], [], [0]
            for hbk in range(NHB):
                bank = _kit[0] % 2
                if _kit[0] >= 2:
                    p.wait("tensor", s_ve, _ksnap_ve[_kit[0] - 2])
                    p.wait("tensor", s_se, _ksnap_se[_kit[0] - 2])
                for cb in range(NB):
                    p.do("tensor", "matmul", ps_g[:, bank * 512:(bank + 1) * 512],
                         wck[:, cb * HH + hbk * 128: cb * HH + hbk * 128 + 128],
                         mxc2[:, cb * LC:(cb + 1) * LC],
                         start=(cb == 0), stop=(cb == NB - 1))
                p.lastinc("tensor", s_mm)
                p.wait("scalar", s_mm)
                if _kit[0] >= 2:
                    p.wait("scalar", s_ve, _ksnap_ve[_kit[0] - 2])
                p.do("scalar", "activation", sgcq[bank][:, :], ps_g[:, bank * 512:(bank + 1) * 512],
                     AF.Relu, inc=s_se)
                p.wait("vector", s_se)
                p.do("vector", "tensor_mul", kkb[:, hbk * LC:(hbk + 1) * LC],
                     sgcq[bank][:, :], sgcq[bank][:, :], inc=s_ve)
                _ksnap_ve.append(s_ve.v)
                _ksnap_se.append(s_se.v)
                _kit[0] += 1
                if hbk in (1, 3, 9, 12):
                    weave(fg)
            p.wait("tensor", s_ve)
            if gi == 0:
                _vsnap, _vit = [], [0]
            for ob in range(NB):
                bank = _vit[0] % 2
                if _vit[0] >= 2:
                    p.wait("tensor", s_ve, _vsnap[_vit[0] - 2])
                for ib in range(NHB):
                    p.do("tensor", "matmul", ps_y[:, bank * 512:(bank + 1) * 512],
                         wcv[:, ib * D + ob * 128: ib * D + ob * 128 + 128],
                         kkb[:, ib * LC:(ib + 1) * LC],
                         start=(ib == 0), stop=(ib == NHB - 1))
                p.lastinc("tensor", s_mm)
                p.wait("vector", s_mm)
                p.wait("vector", s_st, max(0, s_st.v - 16))
                if half == 0:
                    p.do("vector", "tensor_mul", stg[:, bank * LC:(bank + 1) * LC],
                         rrb[:, ob * LC:(ob + 1) * LC], ps_y[:, bank * 512:(bank + 1) * 512])
                    p.do("vector", "tensor_add", stg[:, bank * LC:(bank + 1) * LC],
                         stg[:, bank * LC:(bank + 1) * LC],
                         z[:, ob * TC + t0: ob * TC + t0 + LC], inc=s_ve)
                else:
                    p.do("vector", "tensor_mul", stg[:, bank * LC:(bank + 1) * LC],
                         rrb[:, ob * LC:(ob + 1) * LC], ps_y[:, bank * 512:(bank + 1) * 512],
                         inc=s_ve)
                _vsnap.append(s_ve.v)
                _vit[0] += 1
                p.wait("gpsimd", s_ve)
                if half == 0:
                    p.do("gpsimd", "dma_start", out=ox_d[ob * 128:(ob + 1) * 128, t0:t0 + LC],
                         in_=stg[:, bank * LC:(bank + 1) * LC], inc=s_st, n=16)
                else:
                    p.do("gpsimd", "dma_start", out=ox_d[ob * 128:(ob + 1) * 128, t0:t0 + LC],
                         in_=stg[:, bank * LC:(bank + 1) * LC], accum_op=ALU.add, inc=s_st, n=16)
                if ob in (1, 2, 4, 6):
                    weave(fg)
            if fg is not None:
                for _ in fg:
                    pass
        p.wait("gpsimd", s_st)
        for cb in range(NB):
            p.do("gpsimd", "dma_start", out=ox_d[cb * 128:(cb + 1) * 128, :],
                 in_=x_d[cb * 128:(cb + 1) * 128, :], accum_op=ALU.add,
                 max_dma_last_dim=2048, inc=s_st, n=16)
        p.wait("gpsimd", s_st)
    stack.close()
    p.barrier()


def build_stage_c(trunc='full', debug=False):
    import contextlib
    nc = bass.Bass("TRN2", target_bir_lowering=False, detect_race_conditions=False)
    p = Prog(nc)
    cn = _mkconsts(nc, p)
    CH = 4096
    x_d = nc.dram_tensor("x", [D, TC], F32, kind="ExternalInput").ap()
    wtk_d = nc.dram_tensor("wtk", [D, D], F32, kind="ExternalInput").ap()
    wtv_d = nc.dram_tensor("wtv", [D, D], F32, kind="ExternalInput").ap()
    wtr_d = nc.dram_tensor("wtr", [D, D], F32, kind="ExternalInput").ap()
    wck_d = nc.dram_tensor("wck", [D, CH], F32, kind="ExternalInput").ap()
    wcv_d = nc.dram_tensor("wcv", [CH, D], F32, kind="ExternalInput").ap()
    wcr_d = nc.dram_tensor("wcr", [D, D], F32, kind="ExternalInput").ap()
    vc_d = nc.dram_tensor("vcols", [128, 88], F32, kind="ExternalInput").ap()
    ox_d = nc.dram_tensor("ox", [D, TC], F32, kind="ExternalOutput").ap()
    emit_stage_c(nc, p, cn, x_d, wtk_d, wtv_d, wtr_d, wck_d, wcv_d, wcr_d, vc_d, ox_d,
                 trunc=trunc, debug=debug)
    p.emit()
    return nc


def _prep_stage_c(inputs, core, x_full):
    xs = x_full[2 * core: 2 * core + 2].reshape(TC, D)
    def col8(name):
        return inputs[name].astype(np.float32).reshape(NB, 128).T
    vcols = np.concatenate([
        col8("L1_attn_time_decay"), col8("L1_attn_time_first"),
        col8("L1_attn_ln1_g"), col8("L1_attn_ln1_b"),
        col8("L1_attn_ln2_g"), col8("L1_attn_ln2_b"),
        inputs["L1_attn_mix_k"].astype(np.float32).reshape(NB, 128).T,
        inputs["L1_attn_mix_v"].astype(np.float32).reshape(NB, 128).T,
        inputs["L1_attn_mix_r"].astype(np.float32).reshape(NB, 128).T,
        inputs["L1_attn_cmix_k"].astype(np.float32).reshape(NB, 128).T,
        inputs["L1_attn_cmix_r"].astype(np.float32).reshape(NB, 128).T,
    ], axis=1)
    return {
        "x": np.ascontiguousarray(xs.T),
        "wtk": np.ascontiguousarray(inputs["L1_attn_w_tk"]),
        "wtv": np.ascontiguousarray(inputs["L1_attn_w_tv"]),
        "wtr": np.ascontiguousarray(inputs["L1_attn_w_tr"]),
        "wck": np.ascontiguousarray(inputs["L1_attn_w_ck"]),
        "wcv": np.ascontiguousarray(inputs["L1_attn_w_cv"]),
        "wcr": np.ascontiguousarray(inputs["L1_attn_w_cr"]),
        "vcols": np.ascontiguousarray(vcols),
    }


def run_stage_c(inputs, x_full, cores=None):
    cores = list(range(NCORES)) if cores is None else cores
    nc = build_stage_c()
    in_maps = [_prep_stage_c(inputs, c, x_full) for c in cores]
    res = run_bass_kernel_spmd(nc, in_maps, core_ids=list(range(len(cores))))
    return [r["ox"] for r in res.results]


def gen_prep_moe(nc, p, ring, w1_d, w3_d, w2_d, ws1_d, ws3_d, ws2_d,
                 w1b_d, w3b_d, w2b_d):
    E = 16
    s_pi = p.sem("pld"); s_po = p.sem("pst")
    jobs = []
    for e in range(E + 1):
        s1 = ws1_d if e == E else w1_d[e]
        s3 = ws3_d if e == E else w3_d[e]
        s2 = ws2_d if e == E else w2_d[e]
        jobs.append((s1.rearrange("(c p) i -> p c i", p=128), w1b_d[e, :, :], 768))
        jobs.append((s3.rearrange("(c p) i -> p c i", p=128), w3b_d[e, :, :], 768))
        jobs.append((s2.rearrange("(c p) d -> p c d", p=128), w2b_d[e, :, :], 1024))
    for src_ap, dst_ap, w in jobs:
        p.wait("gpsimd", s_po)
        p.do("gpsimd", "dma_start",
             out=ring[:, :].rearrange("p (c i) -> p c i", i=w), in_=src_ap, inc=s_pi, n=16)
        p.wait("sync", s_pi)
        p.do("sync", "dma_start", out=dst_ap, in_=ring[:, :], inc=s_po, n=16)
        yield
    p.wait("sync", s_po)
    p.do("sync", "sem_inc", p.sem("prepdone").h, 1)
    p.sem("prepdone").v += 1


def emit_stage_d(nc, p, cn, x_d, w1_d, w3_d, w2_d, ws1_d, ws3_d, ws2_d, gw_d, gb_d, ox_d,
                 w1b_d=None, w3b_d=None, w2b_d=None, prep_here=True):
    import contextlib
    stack = contextlib.ExitStack()
    E = 16
    I = 768
    NIB = I // 128
    PL = 1024
    NPASS = TC // PL
    NTI = PL // 128
    BIG = 1.0e30
    ident, onest, onesc, epsc, eps6c = cn["ident"], cn["onest"], cn["onesc"], cn["epsc"], cn["eps6c"]
    onesr = cn["onesr"]; ident32 = cn["ident32"]

    s_ld = p.sem("ld"); s_ldh = p.sem("ldh"); s_mm = p.sem("mm"); s_ve = p.sem("ve")
    s_se = p.sem("se"); s_st = p.sem("st"); s_rw = p.sem("rw")

    xln = p.sbuf(stack, "xlnp", [128, NB * PL], BF16)
    xch = p.sbuf(stack, "xch", [128, NB * 512], BF16)
    yab = p.sbuf(stack, "yab", [128, NB * PL], F32)
    w1s = [p.sbuf(stack, f"w1s{i}", [128, NB * I], BF16) for i in range(2)]
    w3s = [p.sbuf(stack, f"w3s{i}", [128, NB * I], BF16) for i in range(2)]
    w2s = [p.sbuf(stack, f"w2s{i}", [128, NIB * D], BF16) for i in range(2)]
    hbuf = p.sbuf(stack, "hbufd", [128, NIB * PL], BF16)
    sgm = [p.sbuf(stack, f"sgmd{i}", [128, 512], BF16) for i in range(2)]
    cmbq = [p.sbuf(stack, f"cmbq{i}", [128, PL], BF16) for i in range(2)]
    SELS = p.sbuf(stack, "SELS", [16, E * 128], F32)
    gws = p.sbuf(stack, "gws", [128, NB * E], BF16)
    gbb = p.sbuf(stack, "gbb", [128, E], F32)
    mean = p.sbuf(stack, "meand", [128, 512], F32)
    alpha = p.sbuf(stack, "alphad", [128, 512], F32)
    LG = p.sbuf(stack, "LG", [128, NTI * E], F32)
    SB = p.sbuf(stack, "SBr", [128, NTI * E], F32)
    SCR = p.sbuf(stack, "SCR", [128, NTI * E], F32)
    GA = p.sbuf(stack, "GA", [128, NTI * 4], F32)
    GB = p.sbuf(stack, "GB", [128, NTI * 4], F32)
    GC = p.sbuf(stack, "GC", [128, NTI * 4], F32)
    GD = p.sbuf(stack, "GD", [128, NTI * 4], F32)
    R1 = p.sbuf(stack, "R1", [128, NTI], F32)
    R2 = p.sbuf(stack, "R2", [128, NTI], F32)
    cmT = p.sbuf(stack, "cmT", [16, NTI * 128], F32)
    ps_misc = p.psum(stack, "ps_miscd", [128, 1024], F32)
    ps_g = [p.psum(stack, f"ps_gd{i}", [128, 512], F32) for i in range(2)]
    ps_u = [p.psum(stack, f"ps_ud{i}", [128, 512], F32) for i in range(2)]
    ps_y = p.psum(stack, "ps_yd", [128, 1024], F32)

    sI = p.sem("init")
    kcol = p.sbuf(stack, "selk", [16, 1], F32)
    p.do("gpsimd", "iota", SELS[:, :].rearrange("p (e m) -> p e m", e=E),
         [[1, E], [0, 128]], channel_multiplier=0,
         allow_small_or_imprecise_dtypes=True, inc=sI)
    p.do("gpsimd", "iota", kcol[:, :], [[0, 1]], channel_multiplier=1,
         allow_small_or_imprecise_dtypes=True, inc=sI)
    p.wait("vector", sI)
    p.do("vector", "tensor_scalar", SELS[:, :], SELS[:, :], kcol[:, :], None,
         ALU.is_equal, inc=sI)

    p.do("sync", "dma_start", out=gbb[0:1, :], in_=gb_d[:, :], inc=s_ldh, n=16)
    _gbv = p.sem("ldh").v
    for cb in range(NB):
        p.do("gpsimd", "dma_start", out=gws[:, cb * E:(cb + 1) * E],
             in_=gw_d[cb * 128:(cb + 1) * 128, :], inc=s_ld, n=16)
    _gwv = p.sem("ld").v

    if prep_here:
        ring = p.sbuf(stack, "prep_ring", [128, NB * I], BF16)
        for _ in gen_prep_moe(nc, p, ring, w1_d, w3_d, w2_d, ws1_d, ws3_d, ws2_d,
                              w1b_d, w3b_d, w2b_d):
            pass
    p.wait("sync", p.sem("prepdone"))

    def wload(e, buf):
        p.do("sync", "dma_start", out=w1s[buf][:, :], in_=w1b_d[e, :, :], inc=s_ldh, n=16)
        p.do("sync", "dma_start", out=w3s[buf][:, :], in_=w3b_d[e, :, :], inc=s_ldh, n=16)
        p.do("sync", "dma_start", out=w2s[buf][:, :], in_=w2b_d[e, :, :], inc=s_ldh, n=16)

    vsync = lambda: (p.lastinc("vector", s_ve), p.wait("vector", s_ve))

    _wsnap = {}
    _gsnap_se, _gsnap_ve, _ysnap = [], [], []
    _bcsnap, _bcit = [], [0]
    _git, _yit = [0], [0]
    wload(0, 0)
    _wsnap[0] = p.sem("ldh").v
    for cb in range(NB):
        p.do("gpsimd", "dma_start", out=ox_d[cb * 128:(cb + 1) * 128, :],
             in_=x_d[cb * 128:(cb + 1) * 128, :], inc=s_st, n=16)
    _oxv = p.sem("st").v
    xlnB = p.sbuf(stack, "xlnpB", [128, NB * PL], BF16)
    xln2 = [xln, xlnB]

    def head(pi):
        xl = xln2[pi % 2]
        pt0h = pi * PL
        x3c = xch[:, :].rearrange("p (c t) -> p c t", c=NB)
        for c in range(PL // 512):
            t0 = pt0h + c * 512
            p.wait("gpsimd", s_ve)
            p.wait("gpsimd", s_mm)
            for cb in range(NB):
                p.do("gpsimd", "dma_start", out=xch[:, cb * 512:(cb + 1) * 512],
                     in_=x_d[cb * 128:(cb + 1) * 128, t0:t0 + 512], inc=s_ld, n=16)
            yield
            xsq3 = xl[:, :].rearrange("p (c t) -> p c t", c=NB)[:, :, c * 512:(c + 1) * 512]
            p.wait("vector", s_ld)
            p.wait("vector", s_mm)
            p.do("vector", "tensor_mul", xsq3, x3c, x3c, inc=s_ve)
            yield
            p.wait("tensor", s_ld)
            p.wait("tensor", s_ve)
            p.wait("tensor", s_se)
            for cb in range(NB):
                p.do("tensor", "matmul", ps_misc[:, 0:512], onest[:, :], x3c[:, cb, :],
                     start=(cb == 0), stop=(cb == NB - 1), inc=(s_mm if cb == NB - 1 else None))
            for cb in range(NB):
                p.do("tensor", "matmul", ps_misc[:, 512:1024], onest[:, :], xsq3[:, cb, :],
                     start=(cb == 0), stop=(cb == NB - 1), inc=(s_mm if cb == NB - 1 else None))
            smsnap = p.sem("mm").v
            yield
            p.wait("vector", s_mm, smsnap)
            p.do("vector", "tensor_scalar", mean[:, :], ps_misc[:, 0:512], 1.0 / D,
                 None, ALU.mult, inc=s_ve)
            p.wait("vector", s_ve)
            p.do("vector", "tensor_mul", alpha[:, :], mean[:, :], mean[:, :], inc=s_ve)
            p.wait("vector", s_ve)
            p.do("vector", "scalar_tensor_tensor", alpha[:, :], ps_misc[:, 512:1024], 1.0 / D,
                 alpha[:, :], ALU.mult, ALU.subtract, inc=s_ve)
            p.wait("scalar", s_ve)
            p.do("scalar", "activation", alpha[:, :], alpha[:, :], AF.Ln, bias=epsc[:, :])
            p.do("scalar", "activation", alpha[:, :], alpha[:, :], AF.Exp, scale=-0.5, inc=s_se)
            yield
            mb = mean[:, :].unsqueeze(1).broadcast_to([128, NB, 512])
            ab = alpha[:, :].unsqueeze(1).broadcast_to([128, NB, 512])
            o3 = xl[:, :].rearrange("p (c t) -> p c t", c=NB)[:, :, c * 512:(c + 1) * 512]
            p.wait("vector", s_se)
            p.do("vector", "tensor_tensor", o3, x3c, mb, ALU.subtract)
            p.do("vector", "tensor_tensor", o3, o3, ab, ALU.mult, inc=s_ve)
            yield
        p.wait("tensor", s_ve)
        p.wait("tensor", s_ld, _gwv)
        p.wait("tensor", s_ldh, _gbv)
        for ti in range(NTI):
            for cb in range(NB):
                p.do("tensor", "matmul", ps_misc[:, ti * E:(ti + 1) * E],
                     xl[:, cb * PL + ti * 128: cb * PL + (ti + 1) * 128],
                     gws[:, cb * E:(cb + 1) * E],
                     start=(cb == 0), stop=(cb == NB - 1), inc=(s_mm if cb == NB - 1 else None))
        p.do("tensor", "matmul", ps_misc[:, 512:512 + E], onesr[:, :], gbb[0:1, :],
             start=True, stop=True, inc=s_mm)
        p.wait("scalar", s_mm)
        p.do("scalar", "activation", LG[:, :], ps_misc[:, 0:NTI * E], AF.Sigmoid, inc=s_se)
        yield
        p.wait("vector", s_se)
        gb3 = ps_misc[:, 512:512 + E].unsqueeze(1).broadcast_to([128, NTI, E])
        LG3 = LG[:, :].rearrange("p (t e) -> p t e", t=NTI)
        SB3 = SB[:, :].rearrange("p (t e) -> p t e", t=NTI)
        SC3 = SCR[:, :].rearrange("p (t e) -> p t e", t=NTI)
        p.do("vector", "tensor_tensor", SB3, LG3, gb3, ALU.add, inc=s_ve)
        p.wait("vector", s_ve)
        SBg = SB[:, :].rearrange("p (t g e) -> p (t g) e", g=4, e=4)
        a0, a1, a2, a3 = (SBg[:, :, j] for j in range(4))
        g = lambda t_: t_[:, :]
        p.do("vector", "tensor_max", GA[:, :], a0, a1)
        p.do("vector", "tensor_tensor", GB[:, :], a0, a1, ALU.min, inc=s_ve)
        p.wait("vector", s_ve)
        p.do("vector", "tensor_max", GC[:, :], a2, a3)
        p.do("vector", "tensor_tensor", GD[:, :], a2, a3, ALU.min, inc=s_ve)
        p.wait("vector", s_ve)
        yield
        p.do("vector", "tensor_sub", GB[:, :], GB[:, :], GD[:, :], inc=s_ve)
        p.wait("vector", s_ve)
        p.do("vector", "tensor_tensor", R1[:, 0:NTI * 4].bitcast(F32) if False else GB[:, :],
             GB[:, :], GB[:, :], ALU.bypass) if False else None
        p.do("vector", "tensor_tensor", SCR[:, 0:NTI * 4], GA[:, :], GC[:, :], ALU.is_gt, inc=s_ve)
        p.wait("vector", s_ve)
        p.do("vector", "tensor_mul", GB[:, :], GB[:, :], SCR[:, 0:NTI * 4], inc=s_ve)
        p.wait("vector", s_ve)
        yield
        p.do("vector", "tensor_add", GB[:, :], GB[:, :], GD[:, :], inc=s_ve)
        p.wait("vector", s_ve)
        p.do("vector", "tensor_tensor", GD[:, :], GA[:, :], GC[:, :], ALU.min, inc=s_ve)
        p.wait("vector", s_ve)
        p.do("vector", "tensor_max", GB[:, :], GB[:, :], GD[:, :], inc=s_ve)
        p.wait("vector", s_ve)
        yield
        p.do("vector", "tensor_max", GA[:, :], GA[:, :], GC[:, :], inc=s_ve)
        p.wait("vector", s_ve)
        p.do("vector", "tensor_add", GA[:, :], GA[:, :], GB[:, :], inc=s_ve)
        p.wait("vector", s_ve)
        GA4 = GA[:, :].rearrange("p (t g) -> p t g", t=NTI)
        p.do("vector", "reduce_max", R1[:, :], GA4, axis=AX.X, inc=s_ve)
        p.wait("vector", s_ve)
        yield
        r1b = R1[:, :].unsqueeze(2).broadcast_to([128, NTI, 4])
        p.do("vector", "tensor_tensor", GB[:, :].rearrange("p (t g) -> p t g", t=NTI), GA4, r1b,
             ALU.is_ge, inc=s_ve)
        p.wait("vector", s_ve)
        p.do("vector", "scalar_tensor_tensor", GC[:, :], GB[:, :], -BIG, GA[:, :],
             ALU.mult, ALU.add, inc=s_ve)
        p.wait("vector", s_ve)
        p.do("vector", "reduce_max", R2[:, :], GC[:, :].rearrange("p (t g) -> p t g", t=NTI),
             axis=AX.X, inc=s_ve)
        p.wait("vector", s_ve)
        yield
        r2b = R2[:, :].unsqueeze(2).broadcast_to([128, NTI, 4])
        p.do("vector", "tensor_tensor", GB[:, :].rearrange("p (t g) -> p t g", t=NTI), GA4, r2b,
             ALU.is_ge, inc=s_ve)
        p.wait("vector", s_ve)
        gm4 = GB[:, :].rearrange("p (t g) -> p t g 1", t=NTI) if False else None
        gmr = GB[:, :].rearrange("p (t g) -> p t g", t=NTI).unsqueeze(3).broadcast_to([128, NTI, 4, 4])
        SB4 = SB[:, :].rearrange("p (t g e) -> p t g e", g=4, e=4)
        SC4 = SCR[:, :].rearrange("p (t g e) -> p t g e", g=4, e=4)
        p.do("vector", "tensor_scalar", GD[:, :], GB[:, :], BIG, -BIG, ALU.mult, ALU.add, inc=s_ve)
        p.wait("vector", s_ve)
        gdr = GD[:, :].rearrange("p (t g) -> p t g", t=NTI).unsqueeze(3).broadcast_to([128, NTI, 4, 4])
        p.do("vector", "tensor_tensor", SC4, SB4, gmr, ALU.mult, inc=s_ve)
        p.wait("vector", s_ve)
        yield
        p.do("vector", "tensor_tensor", SC4, SC4, gdr, ALU.add, inc=s_ve)
        p.wait("vector", s_ve)
        for it in range(3):
            p.do("vector", "reduce_max", R1[:, :], SC3, axis=AX.X, inc=s_ve)
            p.wait("vector", s_ve)
            rb = R1[:, :].unsqueeze(2).broadcast_to([128, NTI, E])
            p.do("vector", "tensor_tensor", SB3 if False else LG3, SC3, rb, ALU.is_ge) if False else None
            p.do("vector", "tensor_tensor", GC[:, 0:NTI].rearrange("p t -> p t 1") if False else SC3,
                 SC3, rb, ALU.is_ge) if False else None
            p.do("vector", "tensor_tensor", GD[:, 0:NTI * 4], GD[:, 0:NTI * 4], GD[:, 0:NTI * 4], ALU.bypass) if False else None
            p.do("vector", "tensor_tensor", SC4, SC4, R1[:, :].unsqueeze(2).unsqueeze(3).broadcast_to([128, NTI, 4, 4]), ALU.is_ge, inc=s_ve) if False else None
            p.do("vector", "tensor_tensor", SB3, SC3, rb, ALU.is_ge, inc=s_ve)
            p.wait("vector", s_ve)
            yield
            p.do("vector", "scalar_tensor_tensor", SCR[:, :], SB[:, :], -BIG, SCR[:, :],
                 ALU.mult, ALU.add, inc=s_ve)
            p.wait("vector", s_ve)
        p.do("vector", "reduce_max", R1[:, :], SC3, axis=AX.X, inc=s_ve)
        p.wait("vector", s_ve)
        p.do("vector", "tensor_tensor", SB3, LG3, gb3, ALU.add, inc=s_ve)
        p.wait("vector", s_ve)
        yield
        p.do("vector", "tensor_tensor", SB4, SB4, gmr, ALU.mult, inc=s_ve)
        p.wait("vector", s_ve)
        p.do("vector", "tensor_tensor", SB4, SB4, gdr, ALU.add, inc=s_ve)
        p.wait("vector", s_ve)
        rb = R1[:, :].unsqueeze(2).broadcast_to([128, NTI, E])
        p.do("vector", "tensor_tensor", SB3, SB3, rb, ALU.is_ge, inc=s_ve)
        p.wait("vector", s_ve)
        yield
        p.do("vector", "tensor_mul", SB[:, :], SB[:, :], LG[:, :], inc=s_ve)
        p.wait("vector", s_ve)
        p.do("vector", "reduce_sum", R1[:, :], SB3, axis=AX.X, inc=s_ve)
        p.wait("vector", s_ve)
        p.do("vector", "tensor_scalar", R2[:, :], R1[:, :], 1e-20, None, ALU.add, inc=s_ve)
        p.wait("scalar", s_ve)
        p.do("scalar", "activation", R2[:, :], R2[:, :], AF.Ln)
        p.do("scalar", "activation", R2[:, :], R2[:, :], AF.Exp, scale=-1.0, inc=s_se)
        p.wait("vector", s_se)
        rwb = R2[:, :].unsqueeze(2).broadcast_to([128, NTI, E])
        p.do("vector", "tensor_tensor", SB3, SB3, rwb, ALU.mult, inc=s_ve)
        p.wait("vector", s_ve)
        yield
        p.do("vector", "tensor_scalar", SB[:, :], SB[:, :], 2.5, None, ALU.mult, inc=s_ve)
        p.wait("tensor", s_ve)
        for ti in range(NTI):
            p.do("tensor", "transpose", ps_misc[0:16, ti * 128:(ti + 1) * 128],
                 SB[:, ti * E:(ti + 1) * E], ident32[:, :], inc=s_mm)
        p.wait("vector", s_mm)
        p.do("vector", "tensor_copy", cmT[:, :], ps_misc[0:16, 0:NTI * 128], inc=s_ve)
    for _ in head(0):
        pass
    for pi in range(NPASS):
        pt0 = pi * PL
        xln_p = xln2[pi % 2]
        hg = head(pi + 1) if pi + 1 < NPASS else None
        for ei in range(E + 1):
            gctr = pi * (E + 1) + ei
            buf = gctr % 2
            if ei < E or pi + 1 < NPASS:
                p.wait("sync", s_mm)
                wload((ei + 1) % (E + 1), 1 - buf)
                _wsnap[(gctr + 1)] = p.sem("ldh").v
            p.wait("tensor", s_ldh, _wsnap.get(gctr, 0))
            if ei < E:
                cq = cmbq[ei % 2]
                p.wait("scalar", s_ve)
                if ei in (0, 15):
                    p.wait("tensor", s_ve)
                    p.wait("tensor", s_se)
                for h2 in range(2):
                    bank = ps_misc[:, (_bcit[0] % 2) * 512:(_bcit[0] % 2) * 512 + 512]
                    if _bcit[0] >= 2:
                        p.wait("tensor", s_se, _bcsnap[_bcit[0] - 2])
                    p.do("tensor", "matmul", bank, SELS[:, ei * 128:(ei + 1) * 128],
                         cmT[:, h2 * 512:(h2 + 1) * 512], start=True, stop=True, inc=s_mm)
                    p.wait("scalar", s_mm)
                    p.do("scalar", "copy", cq[:, h2 * 512:(h2 + 1) * 512], bank, inc=s_se)
                    _bcsnap.append(p.sem("se").v)
                    _bcit[0] += 1
            for ib in range(NIB):
                for h2 in range(2):
                    tc0 = h2 * 512
                    q = _git[0] % 2
                    if _git[0] >= 2:
                        p.wait("tensor", s_se, _gsnap_se[_git[0] - 2])
                        p.wait("tensor", s_ve, _gsnap_ve[_git[0] - 2])
                    for cb in range(NB):
                        p.do("tensor", "matmul", ps_g[q][:, :],
                             w1s[buf][:, cb * I + ib * 128: cb * I + (ib + 1) * 128],
                             xln_p[:, cb * PL + tc0: cb * PL + tc0 + 512],
                             start=(cb == 0), stop=(cb == NB - 1))
                    p.lastinc("tensor", s_mm)
                    for cb in range(NB):
                        p.do("tensor", "matmul", ps_u[q][:, :],
                             w3s[buf][:, cb * I + ib * 128: cb * I + (ib + 1) * 128],
                             xln_p[:, cb * PL + tc0: cb * PL + tc0 + 512],
                             start=(cb == 0), stop=(cb == NB - 1))
                    p.lastinc("tensor", s_mm)
                    p.wait("scalar", s_mm, s_mm.v - 1)
                    if _git[0] >= 2:
                        p.wait("scalar", s_ve, _gsnap_ve[_git[0] - 2])
                    p.do("scalar", "activation", sgm[q][:, :], ps_g[q][:, :], AF.Silu, inc=s_se)
                    p.wait("vector", s_mm)
                    p.wait("vector", s_se)
                    hslice = hbuf[:, ib * PL + tc0: ib * PL + tc0 + 512]
                    p.do("vector", "tensor_mul", hslice, sgm[q][:, :], ps_u[q][:, :], inc=s_ve)
                    if ei < E:
                        p.do("vector", "tensor_mul", hslice, hslice,
                             cmbq[ei % 2][:, tc0:tc0 + 512], inc=s_ve)
                    _gsnap_se.append(s_se.v)
                    _gsnap_ve.append(s_ve.v)
                    _git[0] += 1
                    if ei >= E - 2 and hg is not None:
                        next(hg, None)
            p.wait("tensor", s_ve)
            for ob in range(NB):
                for h2 in range(2):
                    bank = _yit[0] % 2
                    tc0 = h2 * 512
                    if _yit[0] >= 2:
                        p.wait("tensor", s_ve, _ysnap[_yit[0] - 2])
                    for ib in range(NIB):
                        p.do("tensor", "matmul", ps_y[:, bank * 512:(bank + 1) * 512],
                             w2s[buf][:, ib * D + ob * 128: ib * D + ob * 128 + 128],
                             hbuf[:, ib * PL + tc0: ib * PL + tc0 + 512],
                             start=(ib == 0), stop=(ib == NIB - 1))
                    p.lastinc("tensor", s_mm)
                    p.wait("vector", s_mm)
                    ysl = yab[:, ob * PL + tc0: ob * PL + tc0 + 512]
                    if ei == 0:
                        p.do("vector", "tensor_copy", ysl, ps_y[:, bank * 512:(bank + 1) * 512], inc=s_ve)
                    else:
                        p.do("vector", "tensor_add", ysl, ysl, ps_y[:, bank * 512:(bank + 1) * 512], inc=s_ve)
                    _ysnap.append(s_ve.v)
                    _yit[0] += 1
        if hg is not None:
            for _ in hg:
                pass
        p.wait("gpsimd", s_ve)
        p.wait("gpsimd", s_st, _oxv)
        for ob in range(NB):
            p.do("gpsimd", "dma_start", out=ox_d[ob * 128:(ob + 1) * 128, pt0:pt0 + PL],
                 in_=yab[:, ob * PL:(ob + 1) * PL], accum_op=ALU.add, inc=s_st, n=16)
        p.wait("vector", s_st)
    p.wait("gpsimd", s_st)
    stack.close()
    p.barrier()


def build_stage_d():
    nc = bass.Bass("TRN2", target_bir_lowering=False, detect_race_conditions=False)
    p = Prog(nc)
    cn = _mkconsts(nc, p)
    E, I = 16, 768
    x_d = nc.dram_tensor("x", [D, TC], F32, kind="ExternalInput").ap()
    w1_d = nc.dram_tensor("w1t", [E, D, I], F32, kind="ExternalInput").ap()
    w3_d = nc.dram_tensor("w3t", [E, D, I], F32, kind="ExternalInput").ap()
    w2_d = nc.dram_tensor("w2t", [E, I, D], F32, kind="ExternalInput").ap()
    ws1_d = nc.dram_tensor("ws1t", [D, I], F32, kind="ExternalInput").ap()
    ws3_d = nc.dram_tensor("ws3t", [D, I], F32, kind="ExternalInput").ap()
    ws2_d = nc.dram_tensor("ws2t", [I, D], F32, kind="ExternalInput").ap()
    gw_d = nc.dram_tensor("gwt", [D, E], F32, kind="ExternalInput").ap()
    gb_d = nc.dram_tensor("gbias", [1, E], F32, kind="ExternalInput").ap()
    ox_d = nc.dram_tensor("ox", [D, TC], F32, kind="ExternalOutput").ap()
    w1b_d = nc.dram_tensor("w1b", [E + 1, 128, NB * I], BF16).ap()
    w3b_d = nc.dram_tensor("w3b", [E + 1, 128, NB * I], BF16).ap()
    w2b_d = nc.dram_tensor("w2b", [E + 1, 128, (I // 128) * D], BF16).ap()
    emit_stage_d(nc, p, cn, x_d, w1_d, w3_d, w2_d, ws1_d, ws3_d, ws2_d, gw_d, gb_d, ox_d,
                 w1b_d=w1b_d, w3b_d=w3b_d, w2b_d=w2b_d, prep_here=True)
    p.emit()
    return nc


def _prep_stage_d(inputs, core, x_full):
    xs = x_full[2 * core: 2 * core + 2].reshape(TC, D)
    return {
        "x": np.ascontiguousarray(xs.T),
        "w1t": np.ascontiguousarray(inputs["L1_ffn_w1"].transpose(0, 2, 1)),
        "w3t": np.ascontiguousarray(inputs["L1_ffn_w3"].transpose(0, 2, 1)),
        "w2t": np.ascontiguousarray(inputs["L1_ffn_w2"].transpose(0, 2, 1)),
        "ws1t": np.ascontiguousarray(inputs["L1_ffn_ws1"].T),
        "ws3t": np.ascontiguousarray(inputs["L1_ffn_ws3"].T),
        "ws2t": np.ascontiguousarray(inputs["L1_ffn_ws2"].T),
        "gwt": np.ascontiguousarray(inputs["L1_ffn_gate_w"].T),
        "gbias": np.ascontiguousarray(inputs["L1_ffn_gate_bias"].reshape(1, 16)),
    }


def run_stage_d(inputs, x_full, cores=None):
    cores = list(range(NCORES)) if cores is None else cores
    nc = build_stage_d()
    in_maps = [_prep_stage_d(inputs, c, x_full) for c in cores]
    res = run_bass_kernel_spmd(nc, in_maps, core_ids=list(range(len(cores))))
    return [r["ox"] for r in res.results]



def build_fused():
    nc = bass.Bass("TRN2", target_bir_lowering=False, detect_race_conditions=False)
    p = Prog(nc)
    cn = _mkconsts(nc, p)
    E, I, CH, HID = 16, 768, 4096, 3072
    x_d = nc.dram_tensor("x", [D, TC], F32, kind="ExternalInput").ap()
    out_d = nc.dram_tensor("out", [D, TC], F32, kind="ExternalOutput").ap()
    act1 = nc.dram_tensor("act1", [D, TC], F32).ap()
    act2 = nc.dram_tensor("act2", [D, TC], F32).ap()
    wqkv_d = nc.dram_tensor("wqkv", [D, 3 * D], F32, kind="ExternalInput").ap()
    wout_d = nc.dram_tensor("wout", [D, D], F32, kind="ExternalInput").ap()
    bout_d = nc.dram_tensor("boutc", [128, NB], F32, kind="ExternalInput").ap()
    wg_d = nc.dram_tensor("wgt", [D, HID], I32, kind="ExternalInput").ap()
    wu_d = nc.dram_tensor("wut", [D, HID], I32, kind="ExternalInput").ap()
    wd_d = nc.dram_tensor("wdt", [HID, D], I32, kind="ExternalInput").ap()
    wtk_d = nc.dram_tensor("wtk", [D, D], F32, kind="ExternalInput").ap()
    wtv_d = nc.dram_tensor("wtv", [D, D], F32, kind="ExternalInput").ap()
    wtr_d = nc.dram_tensor("wtr", [D, D], F32, kind="ExternalInput").ap()
    wck_d = nc.dram_tensor("wck", [D, CH], F32, kind="ExternalInput").ap()
    wcv_d = nc.dram_tensor("wcv", [CH, D], F32, kind="ExternalInput").ap()
    wcr_d = nc.dram_tensor("wcr", [D, D], F32, kind="ExternalInput").ap()
    vc_d = nc.dram_tensor("vcols", [128, 88], F32, kind="ExternalInput").ap()
    w1_d = nc.dram_tensor("w1t", [E, D, I], F32, kind="ExternalInput").ap()
    w3_d = nc.dram_tensor("w3t", [E, D, I], F32, kind="ExternalInput").ap()
    w2_d = nc.dram_tensor("w2t", [E, I, D], F32, kind="ExternalInput").ap()
    ws1_d = nc.dram_tensor("ws1t", [D, I], F32, kind="ExternalInput").ap()
    ws3_d = nc.dram_tensor("ws3t", [D, I], F32, kind="ExternalInput").ap()
    ws2_d = nc.dram_tensor("ws2t", [I, D], F32, kind="ExternalInput").ap()
    gw_d = nc.dram_tensor("gwt", [D, E], F32, kind="ExternalInput").ap()
    gb_d = nc.dram_tensor("gbias", [1, E], F32, kind="ExternalInput").ap()
    w1b_d = nc.dram_tensor("w1b", [E + 1, 128, NB * I], BF16).ap()
    w3b_d = nc.dram_tensor("w3b", [E + 1, 128, NB * I], BF16).ap()
    w2b_d = nc.dram_tensor("w2b", [E + 1, 128, (I // 128) * D], BF16).ap()
    emit_stage_a(nc, p, cn, x_d, wqkv_d, wout_d, bout_d, act1)
    emit_stage_b(nc, p, cn, act1, wg_d, wu_d, wd_d, act2,
                 prep_args=(w1_d, w3_d, w2_d, ws1_d, ws3_d, ws2_d, w1b_d, w3b_d, w2b_d))
    emit_stage_c(nc, p, cn, act2, wtk_d, wtv_d, wtr_d, wck_d, wcv_d, wcr_d, vc_d, act1)
    emit_stage_d(nc, p, cn, act1, w1_d, w3_d, w2_d, ws1_d, ws3_d, ws2_d, gw_d, gb_d, out_d,
                 w1b_d=w1b_d, w3b_d=w3b_d, w2b_d=w2b_d, prep_here=False)
    p.emit()
    return nc


def _prep_fused(inputs, core):
    xs = inputs["L0_attn_x"][2 * core: 2 * core + 2].reshape(TC, D)
    m = {"x": np.ascontiguousarray(xs.T)}
    m.update({
        "wqkv": np.ascontiguousarray(inputs["L0_attn_Wqkv"]),
        "wout": np.ascontiguousarray(inputs["L0_attn_Wout"]),
        "boutc": np.ascontiguousarray(inputs["L0_attn_bout"].reshape(NB, 128).T),
        "wgt": np.ascontiguousarray(inputs["L0_ffn_w_gate"].T),
        "wut": np.ascontiguousarray(inputs["L0_ffn_w_up"].T),
        "wdt": np.ascontiguousarray(inputs["L0_ffn_w_down"].T),
    })
    pc = _prep_stage_c(inputs, 0, None) if False else None
    def col8(name):
        return inputs[name].astype(np.float32).reshape(NB, 128).T
    vcols = np.concatenate([
        col8("L1_attn_time_decay"), col8("L1_attn_time_first"),
        col8("L1_attn_ln1_g"), col8("L1_attn_ln1_b"),
        col8("L1_attn_ln2_g"), col8("L1_attn_ln2_b"),
        inputs["L1_attn_mix_k"].astype(np.float32).reshape(NB, 128).T,
        inputs["L1_attn_mix_v"].astype(np.float32).reshape(NB, 128).T,
        inputs["L1_attn_mix_r"].astype(np.float32).reshape(NB, 128).T,
        inputs["L1_attn_cmix_k"].astype(np.float32).reshape(NB, 128).T,
        inputs["L1_attn_cmix_r"].astype(np.float32).reshape(NB, 128).T,
    ], axis=1)
    m.update({
        "wtk": np.ascontiguousarray(inputs["L1_attn_w_tk"]),
        "wtv": np.ascontiguousarray(inputs["L1_attn_w_tv"]),
        "wtr": np.ascontiguousarray(inputs["L1_attn_w_tr"]),
        "wck": np.ascontiguousarray(inputs["L1_attn_w_ck"]),
        "wcv": np.ascontiguousarray(inputs["L1_attn_w_cv"]),
        "wcr": np.ascontiguousarray(inputs["L1_attn_w_cr"]),
        "vcols": np.ascontiguousarray(vcols),
        "w1t": np.ascontiguousarray(inputs["L1_ffn_w1"].transpose(0, 2, 1)),
        "w3t": np.ascontiguousarray(inputs["L1_ffn_w3"].transpose(0, 2, 1)),
        "w2t": np.ascontiguousarray(inputs["L1_ffn_w2"].transpose(0, 2, 1)),
        "ws1t": np.ascontiguousarray(inputs["L1_ffn_ws1"].T),
        "ws3t": np.ascontiguousarray(inputs["L1_ffn_ws3"].T),
        "ws2t": np.ascontiguousarray(inputs["L1_ffn_ws2"].T),
        "gwt": np.ascontiguousarray(inputs["L1_ffn_gate_w"].T),
        "gbias": np.ascontiguousarray(inputs["L1_ffn_gate_bias"].reshape(1, 16)),
    })
    return m


def kernel(**inputs):
    cores = list(range(NCORES))
    nc = build_fused()
    in_maps = [_prep_fused(inputs, c) for c in cores]
    res = run_bass_kernel_spmd(nc, in_maps, core_ids=cores)
    out = np.empty((16, T, D), dtype=np.float32)
    for c in cores:
        out[2 * c: 2 * c + 2] = np.asarray(res.results[c]["out"]).T.reshape(BPC, T, D)
    return out


# version 12
# speedup vs baseline: 1.0896x; 1.0172x over previous
import numpy as np
import concourse.bass as bass
import concourse.mybir as mybir
from concourse.bass_utils import run_bass_kernel_spmd

F32 = mybir.dt.float32
BF16 = mybir.dt.bfloat16
I32 = mybir.dt.int32
AF = mybir.ActivationFunctionType
ALU = mybir.AluOpType
AX = mybir.AxisListType

NCORES = 8
T = 2048
BPC = 2
TC = T * BPC
D = 1024
NB = D // 128
LNEPS = 1e-5


class Sem:
    def __init__(self, h):
        self.h = h
        self.v = 0


class Prog:
    ENGINES = ("sync", "tensor", "vector", "scalar", "gpsimd")

    def __init__(self, nc):
        self.nc = nc
        self.q = {e: [] for e in self.ENGINES}
        self._sems = {}
        self.plan = []
        self._semname = {}
        globals()["_LAST_PROG"] = self

    def sem(self, name):
        if name not in self._sems:
            self._sems[name] = Sem(self.nc.alloc_semaphore(name))
            self._semname[id(self._sems[name])] = name
        return self._sems[name]

    def do(self, eng, meth, *a, inc=None, n=1, **kw):
        rec = {"kind": "op", "eng": eng, "meth": meth, "args": a, "kw": dict(kw),
               "incs": [], "qi": len(self.q[eng])}
        if inc is not None:
            inc.v += n
            rec["incs"].append((self._semname[id(inc)], n, inc.v))
            def f(e, a=a, kw=kw, inc=inc, n=n, meth=meth):
                ins = getattr(e, meth)(*a, **kw)
                ins.then_inc(inc.h, n)
                return ins
        else:
            def f(e, a=a, kw=kw, meth=meth):
                return getattr(e, meth)(*a, **kw)
        self.q[eng].append(f)
        self.plan.append(rec)

    def lastinc(self, eng, sem, n=1):
        sem.v += n
        f0 = self.q[eng][-1]
        for rec in reversed(self.plan):
            if rec["kind"] == "op" and rec["eng"] == eng and rec["qi"] == len(self.q[eng]) - 1:
                rec["incs"].append((self._semname[id(sem)], n, sem.v))
                break
        def f(e, f0=f0, sem=sem, n=n):
            ins = f0(e)
            ins.then_inc(sem.h, n)
            return ins
        self.q[eng][-1] = f

    def wait(self, eng, sem, v=None):
        vv = sem.v if v is None else v
        self.plan.append({"kind": "wait", "eng": eng, "sem": self._semname[id(sem)],
                          "v": vv, "full": (v is None)})
        self.q[eng].append(lambda e, h=sem.h, vv=vv: e.wait_ge(h, vv))

    def sbuf(self, stack, name, shape, dtype):
        self._tn = getattr(self, "_tn", 0) + 1
        return stack.enter_context(self.nc.sbuf_tensor(f"{name}_t{self._tn}", shape, dtype))

    def psum(self, stack, name, shape, dtype):
        self._tn = getattr(self, "_tn", 0) + 1
        return stack.enter_context(self.nc.psum_tensor(f"{name}_t{self._tn}", shape, dtype))

    def barrier(self):
        sb = self.sem("bar")
        for e in self.ENGINES:
            self.do(e, "nop", inc=sb)
        for e in self.ENGINES:
            self.wait(e, sb)

    def emit(self):
        with self.nc.Block() as block:
            for e in self.ENGINES:
                if self.q[e]:
                    def body(eng, e=e):
                        for f in self.q[e]:
                            f(eng)
                    getattr(block, e)(body)


def _mkconsts(nc, p):
    ident = nc.alloc_sbuf_tensor("c_ident", [128, 128], BF16)
    ident32 = nc.alloc_sbuf_tensor("c_ident32", [128, 128], F32)
    onest = nc.alloc_sbuf_tensor("c_onest", [128, 128], BF16)
    onesc = nc.alloc_sbuf_tensor("c_onesc", [128, 1], BF16)
    onesr = nc.alloc_sbuf_tensor("c_onesr", [1, 128], F32)
    epsc = nc.alloc_sbuf_tensor("c_epsc", [128, 1], F32)
    eps6c = nc.alloc_sbuf_tensor("c_eps6c", [128, 1], F32)
    tinyc = nc.alloc_sbuf_tensor("c_tiny", [128, 1], F32)
    s = p.sem("init")
    p.do("gpsimd", "memset", ident[:, :], 1.0, inc=s)
    p.do("gpsimd", "memset", ident32[:, :], 1.0, inc=s)
    p.do("gpsimd", "memset", onest[:, :], 1.0, inc=s)
    p.do("gpsimd", "memset", onesc[:, :], 1.0, inc=s)
    p.do("gpsimd", "memset", onesr[:, :], 1.0, inc=s)
    p.do("gpsimd", "memset", epsc[:, :], LNEPS, inc=s)
    p.do("gpsimd", "memset", eps6c[:, :], 1e-6, inc=s)
    p.do("gpsimd", "memset", tinyc[:, :], 1e-30, inc=s)
    p.wait("gpsimd", s)
    p.do("gpsimd", "affine_select", ident[:, :], ident[:, :], [[1, 128]],
         ALU.is_equal, 0.0, base=0, channel_multiplier=-1, inc=s)
    p.wait("gpsimd", s)
    p.do("gpsimd", "affine_select", ident32[:, :], ident32[:, :], [[1, 128]],
         ALU.is_equal, 0.0, base=0, channel_multiplier=-1, inc=s)
    return dict(ident=ident, ident32=ident32, onest=onest, onesc=onesc,
                onesr=onesr, epsc=epsc, eps6c=eps6c, tinyc=tinyc)


def _ln_stats_chunk(p, onest, xap3, ps_sum, ps_ssq, xsq, mean, alpha, s_mm, s_ve, s_se, epsc=None):
    L = xap3.shape[2]
    p.do("vector", "tensor_mul", xsq[:, :, :], xap3, xap3, inc=s_ve)
    for cb in range(NB):
        p.do("tensor", "matmul", ps_sum, onest[:, :], xap3[:, cb, :],
             start=(cb == 0), stop=(cb == NB - 1), inc=(s_mm if cb == NB - 1 else None))
    p.wait("tensor", s_ve)
    for cb in range(NB):
        p.do("tensor", "matmul", ps_ssq, onest[:, :], xsq[:, cb, :],
             start=(cb == 0), stop=(cb == NB - 1), inc=(s_mm if cb == NB - 1 else None))
    p.wait("vector", s_mm)
    p.do("vector", "tensor_scalar", mean, ps_sum, 1.0 / D, None, ALU.mult, inc=s_ve)
    p.wait("vector", s_ve)
    p.do("vector", "tensor_mul", alpha, mean, mean, inc=s_ve)
    p.wait("vector", s_ve)
    p.do("vector", "scalar_tensor_tensor", alpha, ps_ssq, 1.0 / D, alpha,
         ALU.mult, ALU.subtract, inc=s_ve)
    p.wait("scalar", s_ve)
    p.do("scalar", "activation", alpha, alpha, AF.Ln, bias=epsc[:, :])
    p.do("scalar", "activation", alpha, alpha, AF.Exp, scale=-0.5, inc=s_se)


def _ln_apply_chunk(p, xap3, mean, alpha, out3, tmp, s_ve, s_se):
    L = xap3.shape[2]
    mb = mean.unsqueeze(1).broadcast_to([128, NB, L])
    ab = alpha.unsqueeze(1).broadcast_to([128, NB, L])
    p.wait("vector", s_se)
    p.do("vector", "tensor_tensor", tmp[:, :, :], xap3, mb, ALU.subtract)
    p.do("vector", "tensor_tensor", out3, tmp[:, :, :], ab, ALU.mult, inc=s_ve)


def emit_stage_a(nc, p, cn, x_d, wqkv_d, wout_d, bout_d, ox_d, debug=False):
    import contextlib
    stack = contextlib.ExitStack()
    _dbg_outs = []
    def dbg(p, name, ap, when):
        if not debug:
            return
        d = nc.dram_tensor("dbg_" + name, list(ap.shape), ap.dtype, kind="ExternalOutput").ap()
        _dbg_outs.append("dbg_" + name)
        sd = p.sem("dbgst"); sb = p.sem("blk")
        for s in when:
            p.wait("sync", s)
        p.do("sync", "dma_start", out=d, in_=ap, inc=sd, n=16)
        p.wait("sync", sd)
        p.do("sync", "sem_inc", sb.h, 1)
        sb.v += 1
        for e in ("tensor", "vector", "scalar", "gpsimd"):
            p.wait(e, sb)
    ident, onest, onesc, epsc, eps6c = cn["ident"], cn["onest"], cn["onesc"], cn["epsc"], cn["eps6c"]

    xsb = p.sbuf(stack, "xsb", [128, NB * TC], BF16)
    wq = p.sbuf(stack, "wq", [128, NB * 3 * D], BF16)
    wo = p.sbuf(stack, "wo", [128, NB * D], BF16)
    bo = p.sbuf(stack, "bo", [128, NB], F32)
    xln = p.sbuf(stack, "xln", [128, NB * T], BF16)
    qfm = p.sbuf(stack, "qfm", [128, NB * 512], BF16)
    ktm = p.sbuf(stack, "ktm", [128, D], BF16)
    VW = 130
    vtm = p.sbuf(stack, "vtm", [128, NB * VW], BF16)
    kvd = p.sbuf(stack, "kvd", [128, NB * 128], BF16)
    ks2 = p.sbuf(stack, "ks2", [128, NB * 2], BF16)
    att = ktm
    afm = p.sbuf(stack, "afm", [128, NB * 512], BF16)
    rden = p.sbuf(stack, "rden", [128, 32], F32)

    mean = p.sbuf(stack, "mean", [128, 512], F32)
    alpha = p.sbuf(stack, "alpha", [128, 512], F32)
    tA = p.sbuf(stack, "tA", [128, NB * 512], F32)
    tB = p.sbuf(stack, "tB", [128, 1024], F32)
    xsq = afm

    s_ld = p.sem("ld"); s_mm = p.sem("mm"); s_ve = p.sem("ve")
    s_se = p.sem("se"); s_gp = p.sem("gp"); s_st = p.sem("st")

    ps_big = p.psum(stack, "ps_big", [128, 2048], F32)
    ps_kv = p.psum(stack, "ps_kv", [128, NB * 128], F32)
    ps_ks = p.psum(stack, "ps_ks", [128, 32], F32)
    ps_tr = p.psum(stack, "ps_tr", [128, NB * 128], BF16)

    for cb in range(NB):
        p.do("gpsimd", "dma_start", out=xsb[:, cb * TC:(cb + 1) * TC],
             in_=x_d[cb * 128:(cb + 1) * 128, :], inc=s_ld, n=16)
        p.do("gpsimd", "dma_start", out=wq[:, cb * 3 * D:(cb + 1) * 3 * D],
             in_=wqkv_d[cb * 128:(cb + 1) * 128, :], inc=s_ld, n=16)
        p.do("gpsimd", "dma_start", out=wo[:, cb * D:(cb + 1) * D],
             in_=wout_d[cb * 128:(cb + 1) * 128, :], inc=s_ld, n=16)
    s_ldh = p.sem("ldh")
    p.do("sync", "dma_start", out=bo[:, :], in_=bout_d[:, :], inc=s_ldh, n=16)
    p.wait("vector", s_ld); p.wait("vector", s_ldh)
    p.wait("tensor", s_ld); p.wait("tensor", s_ldh)
    p.wait("scalar", s_ld); p.wait("scalar", s_ldh)
    p.wait("tensor", p.sem("init"))
    for cb in range(NB):
        p.do("vector", "memset", vtm[:, cb * VW + 128: cb * VW + 130], 1.0)
    p.lastinc("vector", s_ve)

    qfm_f32 = qfm[:, :].bitcast(F32)
    kvacc = qfm_f32[:, 0:1024]
    ksacc = qfm_f32[:, 1024:1032]
    x3 = lambda t0, L: xsb[:, :].rearrange("p (c t) -> p c t", c=NB)[:, :, t0:t0 + L]
    xln3 = lambda t0, L: xln[:, :].rearrange("p (c t) -> p c t", c=NB)[:, :, t0:t0 + L]

    def elu_from_psum(ps_ap, out_ap, w, off=0):
        a = tA[:, off:off + w]; b = tB[:, off:off + w]
        p.wait("scalar", s_mm)
        p.do("scalar", "activation", a, ps_ap, AF.Exp, inc=s_se)
        p.wait("vector", s_se)
        p.do("vector", "tensor_scalar", b, a, 1.0, None, ALU.min)
        p.do("vector", "scalar_tensor_tensor", out_ap, ps_ap, 0.0, b, ALU.max, ALU.add,
             inc=s_ve)

    NCH = T // 512

    for b in range(BPC):
        for c in range(NCH):
            t0 = b * T + c * 512
            tl0 = c * 512
            p.wait("tensor", s_ve)
            p.wait("tensor", s_se)
            p.wait("vector", s_st)
            p.wait("scalar", s_st)
            _ln_stats_chunk(p, onest, x3(t0, 512), ps_big[:, 0:512], ps_big[:, 512:1024],
                            xsq[:, :].rearrange("p (c t) -> p c t", c=NB), mean[:, :], alpha[:, :],
                            s_mm, s_ve, s_se, epsc=epsc)
            _ln_apply_chunk(p, x3(t0, 512), mean[:, :], alpha[:, :],
                            xln3(tl0, 512), tA[:, :].rearrange("p (c t) -> p c t", c=NB), s_ve, s_se)
            if b == 0 and c == 0:
                dbg(p, "xln0", xln[:, 0:512], [s_ve])
                dbg(p, "xln0b", xln[:, T:T + 512], [s_ve])
            for tt in range(4):
                tx = tl0 + tt * 128
                if b == 0 and c == 0 and tt == 0:
                    _asnap_ve, _asnap_se, _ait = [], [], [0]
                if tt == 0:
                    p.wait("tensor", s_ve)
                for item in range(4):
                    slot = _ait[0] % 4
                    sl = ps_big[:, slot * 512:(slot + 1) * 512]
                    if _ait[0] >= 4:
                        p.wait("tensor", s_ve, _asnap_ve[_ait[0] - 4])
                        p.wait("tensor", s_se, _asnap_se[_ait[0] - 4])
                    half = item % 2
                    cof = D + half * 512 if item < 2 else 2 * D + half * 512
                    for cb in range(NB):
                        p.do("tensor", "matmul", sl,
                             xln[:, cb * T + tx: cb * T + tx + 128],
                             wq[:, cb * 3 * D + cof: cb * 3 * D + cof + 512],
                             start=(cb == 0), stop=(cb == NB - 1))
                    p.lastinc("tensor", s_mm)
                    p.wait("vector", s_mm)
                    if item < 2:
                        elu_from_psum(sl, ktm[:, half * 512:(half + 1) * 512], 512, off=half * 512)
                    else:
                        v3 = vtm[:, :].rearrange("p (c w) -> p c w", c=NB)[:, half * 4:(half + 1) * 4, 0:128]
                        p.do("vector", "tensor_copy", v3,
                             sl.rearrange("p (c w) -> p c w", c=4), inc=s_ve)
                    _asnap_ve.append(s_ve.v)
                    _asnap_se.append(s_se.v)
                    _ait[0] += 1
                p.wait("tensor", s_ve)
                for cb in range(NB):
                    p.do("tensor", "matmul", ps_kv[:, cb * 128:(cb + 1) * 128],
                         ktm[:, cb * 128:(cb + 1) * 128],
                         vtm[:, cb * VW: cb * VW + 128],
                         start=True, stop=True)
                for cb in range(NB):
                    p.do("tensor", "matmul", ps_ks[:, cb: cb + 1],
                         ktm[:, cb * 128:(cb + 1) * 128], onesc[:, :],
                         start=True, stop=True, inc=s_mm)
                p.wait("vector", s_mm)
                if tt == 0 and c == 0:
                    p.do("vector", "tensor_copy", kvacc, ps_kv[:, :])
                    p.do("vector", "tensor_copy", ksacc, ps_ks[:, 0:8], inc=s_ve)
                else:
                    p.do("vector", "tensor_add", kvacc, kvacc, ps_kv[:, :])
                    p.do("vector", "tensor_add", ksacc, ksacc, ps_ks[:, 0:8], inc=s_ve)
                if b == 0 and c == 0 and tt == 0:
                    dbg(p, "ktm0", ktm[:, :], [s_ve])
                    dbg(p, "vtm0", vtm[:, :], [s_ve])
        p.do("vector", "memset", kvd[:, :], 0.0)
        p.do("vector", "memset", ks2[:, :], 0.0, inc=s_ve)
        p.wait("vector", s_ve)
        for cb in range(NB):
            p.do("vector", "tensor_copy", kvd[0:64, cb * 128: cb * 128 + 64],
                 kvacc[0:64, cb * 128: cb * 128 + 64])
            p.do("vector", "tensor_copy", kvd[64:128, cb * 128 + 64: cb * 128 + 128],
                 kvacc[64:128, cb * 128 + 64: cb * 128 + 128])
            p.do("vector", "tensor_copy", ks2[0:64, cb * 2: cb * 2 + 1], ksacc[0:64, cb: cb + 1])
            p.do("vector", "tensor_copy", ks2[64:128, cb * 2 + 1: cb * 2 + 2], ksacc[64:128, cb: cb + 1])
        p.lastinc("vector", s_ve)
        if b == 0:
            dbg(p, "kvacc", kvacc, [s_ve])
            dbg(p, "ksacc", ksacc, [s_ve])
            dbg(p, "kvd", kvd[:, :], [s_ve])
            dbg(p, "ks2", ks2[:, :], [s_ve])

        for c in range(NCH):
            t0 = b * T + c * 512
            tl0 = c * 512
            p.wait("vector", s_st)
            p.wait("scalar", s_st)
            if b == 0 and c == 0:
                _qsnap_ve, _qsnap_se, _qit = [], [], [0]
            for ob in range(NB):
                q = _qit[0] % 2
                if ob == 0:
                    p.wait("tensor", s_ve)
                    p.wait("tensor", s_se)
                if _qit[0] >= 2:
                    p.wait("tensor", s_ve, _qsnap_ve[_qit[0] - 2])
                    p.wait("tensor", s_se, _qsnap_se[_qit[0] - 2])
                for cb in range(NB):
                    p.do("tensor", "matmul", ps_big[:, q * 512:(q + 1) * 512],
                         wq[:, cb * 3 * D + ob * 128: cb * 3 * D + ob * 128 + 128],
                         xln[:, cb * T + tl0: cb * T + tl0 + 512],
                         start=(cb == 0), stop=(cb == NB - 1))
                p.lastinc("tensor", s_mm)
                p.wait("vector", s_mm)
                elu_from_psum(ps_big[:, q * 512:(q + 1) * 512], qfm[:, ob * 512:(ob + 1) * 512],
                              512, off=q * 512)
                _qsnap_ve.append(s_ve.v)
                _qsnap_se.append(s_se.v)
                _qit[0] += 1
            if b == 0 and c == 0:
                dbg(p, "qfm0", qfm[:, :], [s_ve])
            _tsnap = {}
            for tt in range(5):
                if tt < 4:
                    q3 = qfm[:, :].rearrange("p (c t) -> p c t", c=NB)[:, :, tt * 128:(tt + 1) * 128]
                    slot = ps_big[:, 1024:2048] if tt % 2 == 0 else ps_kv[:, :]
                    kslot = ps_ks[:, (tt % 2) * 16:(tt % 2) * 16 + 16]
                    if tt < 2:
                        p.wait("tensor", s_ve)
                    else:
                        p.wait("tensor", s_ve, _tsnap[("dv", tt - 2)])
                    for cb in range(NB):
                        p.do("tensor", "matmul", slot[:, cb * 128:(cb + 1) * 128],
                             q3[:, cb, :], kvd[:, cb * 128:(cb + 1) * 128],
                             start=True, stop=True)
                    for cb in range(NB):
                        p.do("tensor", "matmul", kslot[:, cb * 2:(cb + 1) * 2],
                             q3[:, cb, :], ks2[:, cb * 2:(cb + 1) * 2],
                             start=True, stop=True, inc=s_mm)
                    _tsnap[("mm", tt)] = p.sem("mm").v
                if tt >= 1:
                    te = tt - 1
                    slot = ps_big[:, 1024:2048] if te % 2 == 0 else ps_kv[:, :]
                    kslot = ps_ks[:, (te % 2) * 16:(te % 2) * 16 + 16]
                    rsl = rden[:, (te % 2) * 16:(te % 2) * 16 + 16]
                    p.wait("scalar", s_mm, _tsnap[("mm", te)])
                    p.do("scalar", "activation", rsl, kslot, AF.Ln, bias=eps6c[:, :])
                    p.do("scalar", "activation", rsl, rsl, AF.Exp, scale=-1.0, inc=s_se)
                    p.wait("vector", s_se)
                    p.wait("vector", s_mm, _tsnap.get(("tr", te - 1), 0))
                    rb = rsl.unsqueeze(2).broadcast_to([128, 16, 64])
                    p.do("vector", "tensor_tensor", att[:, :].rearrange("p (h f) -> p h f", h=16),
                         slot.rearrange("p (h f) -> p h f", h=16), rb, ALU.mult, inc=s_ve)
                    _tsnap[("dv", te)] = p.sem("ve").v
                    p.wait("tensor", s_ve)
                    p.wait("tensor", s_se)
                    for cb in range(NB):
                        p.do("tensor", "transpose", ps_tr[:, cb * 128:(cb + 1) * 128],
                             att[:, cb * 128:(cb + 1) * 128], ident[:, :])
                    p.lastinc("tensor", s_mm)
                    _tsnap[("tr", te)] = p.sem("mm").v
                    p.wait("scalar", s_mm)
                    p.do("scalar", "copy",
                         afm[:, :].rearrange("p (c t) -> p c t", c=NB)[:, :, te * 128:(te + 1) * 128],
                         ps_tr[:, :].rearrange("p (c t) -> p c t", c=NB), inc=s_se)
            p.wait("tensor", s_se)
            p.wait("tensor", s_ve)
            if b == 0 and c == 0:
                _wsnap2, _wit2 = [], [0]
            for ob in range(NB):
                if _wit2[0] >= 2:
                    p.wait("tensor", s_ve, _wsnap2[_wit2[0] - 2])
                for cb in range(NB):
                    p.do("tensor", "matmul", ps_big[:, (ob % 2) * 512:(ob % 2 + 1) * 512],
                         wo[:, cb * D + ob * 128: cb * D + ob * 128 + 128],
                         afm[:, cb * 512:(cb + 1) * 512],
                         start=(cb == 0), stop=(cb == NB - 1))
                p.lastinc("tensor", s_mm)
                p.wait("vector", s_mm)
                p.do("vector", "tensor_scalar", tB[:, (ob % 2) * 512:(ob % 2) * 512 + 512],
                     ps_big[:, (ob % 2) * 512:(ob % 2 + 1) * 512],
                     bo[:, ob:ob + 1], None, ALU.add)
                p.do("vector", "tensor_add", tA[:, ob * 512:(ob + 1) * 512],
                     tB[:, (ob % 2) * 512:(ob % 2) * 512 + 512],
                     xsb[:, ob * TC + t0: ob * TC + t0 + 512], inc=s_ve)
                _wsnap2.append(s_ve.v)
                _wit2[0] += 1
            p.wait("sync", s_ve)
            for ob in range(NB):
                p.do("sync", "dma_start", out=ox_d[ob * 128:(ob + 1) * 128, t0:t0 + 512],
                     in_=tA[:, ob * 512:(ob + 1) * 512], inc=s_st, n=16)

    p.wait("sync", s_st)
    stack.close()
    p.barrier()


def build_stage_a(debug=False):
    nc = bass.Bass("TRN2", target_bir_lowering=False, detect_race_conditions=False)
    p = Prog(nc)
    cn = _mkconsts(nc, p)
    x_d = nc.dram_tensor("x", [D, TC], F32, kind="ExternalInput").ap()
    wqkv_d = nc.dram_tensor("wqkv", [D, 3 * D], F32, kind="ExternalInput").ap()
    wout_d = nc.dram_tensor("wout", [D, D], F32, kind="ExternalInput").ap()
    bout_d = nc.dram_tensor("boutc", [128, NB], F32, kind="ExternalInput").ap()
    ox_d = nc.dram_tensor("ox", [D, TC], F32, kind="ExternalOutput").ap()
    emit_stage_a(nc, p, cn, x_d, wqkv_d, wout_d, bout_d, ox_d, debug=debug)
    p.emit()
    return nc



def _prep_stage_a(inputs, core):
    x = inputs["L0_attn_x"]
    xs = x[2 * core: 2 * core + 2].reshape(TC, D)
    return {
        "x": np.ascontiguousarray(xs.T),
        "wqkv": np.ascontiguousarray(inputs["L0_attn_Wqkv"]),
        "wout": np.ascontiguousarray(inputs["L0_attn_Wout"]),
        "boutc": np.ascontiguousarray(inputs["L0_attn_bout"].reshape(NB, 128).T),
    }


def run_stage_a(inputs, cores=None):
    cores = list(range(NCORES)) if cores is None else cores
    nc = build_stage_a()
    in_maps = [_prep_stage_a(inputs, c) for c in cores]
    res = run_bass_kernel_spmd(nc, in_maps, core_ids=list(range(len(cores))))
    return [r["ox"] for r in res.results]


def emit_stage_b(nc, p, cn, x_d, wg_d, wu_d, wd_d, ox_d, prep_args=None):
    import contextlib
    stack = contextlib.ExitStack()
    ident, onest, onesc, epsc, eps6c = cn["ident"], cn["onest"], cn["onesc"], cn["epsc"], cn["eps6c"]
    HID = 3072
    HH = HID // 2
    NIB = HH // 128
    S2 = float(1.0 / (127.0 * 127.0))

    xsb = p.sbuf(stack, "xsb", [128, NB * TC], BF16)
    wgh = p.sbuf(stack, "wgh", [128, NB * HH], BF16)
    wuh = p.sbuf(stack, "wuh", [128, NB * HH], BF16)
    wdh = p.sbuf(stack, "wdh", [128, NIB * D], BF16)
    hbuf = p.sbuf(stack, "hbuf", [128, NIB * 512], BF16)
    xln = p.sbuf(stack, "xlnb", [128, NB * 512], BF16)
    sg = p.sbuf(stack, "sg", [128, 512], BF16)
    sg2 = p.sbuf(stack, "sg2", [128, 512], BF16)
    mean = p.sbuf(stack, "mean", [128, 512], F32)
    alpha = p.sbuf(stack, "alpha", [128, 512], F32)
    tA = p.sbuf(stack, "tA", [128, NB * 512], F32)
    stg = p.sbuf(stack, "stg", [128, NB * 512], F32)
    xsq = hbuf[:, 0:4096].bitcast(BF16)

    s_ld = p.sem("ld"); s_mm = p.sem("mm"); s_ve = p.sem("ve")
    s_se = p.sem("se"); s_st = p.sem("st")

    prep_gen = None
    if prep_args is not None:
        ringb = p.sbuf(stack, "prep_ring", [128, 6144], BF16)
        prep_gen = gen_prep_moe(nc, p, ringb, *prep_args)
    ps_st = p.psum(stack, "ps_st", [128, 1024], F32)
    ps_g = [p.psum(stack, f"ps_g{i}", [128, 512], F32) for i in range(2)]
    ps_u = [p.psum(stack, f"ps_u{i}", [128, 512], F32) for i in range(2)]
    ps_y = p.psum(stack, "ps_y", [128, 1024], F32)
    sgq = [sg, sg2]

    for cb in range(NB):
        p.do("gpsimd", "dma_start", out=xsb[:, cb * TC:(cb + 1) * TC],
             in_=x_d[cb * 128:(cb + 1) * 128, :], inc=s_ld, n=16)

    x3 = lambda t0, L: xsb[:, :].rearrange("p (c t) -> p c t", c=NB)[:, :, t0:t0 + L]
    xln3 = xln[:, :].rearrange("p (c t) -> p c t", c=NB)

    for half in range(2):
        p.wait("gpsimd", s_mm)
        for cb in range(NB):
            p.do("gpsimd", "dma_start", out=wgh[:, cb * HH:(cb + 1) * HH],
                 in_=wg_d[cb * 128:(cb + 1) * 128, half * HH:(half + 1) * HH], inc=s_ld, n=16)
            p.do("gpsimd", "dma_start", out=wuh[:, cb * HH:(cb + 1) * HH],
                 in_=wu_d[cb * 128:(cb + 1) * 128, half * HH:(half + 1) * HH], inc=s_ld, n=16)
        for ib in range(NIB):
            p.do("gpsimd", "dma_start", out=wdh[:, ib * D:(ib + 1) * D],
                 in_=wd_d[half * HH + ib * 128: half * HH + (ib + 1) * 128, :], inc=s_ld, n=16)
        p.wait("tensor", s_ld)
        p.wait("vector", s_ld)

        for c in range(TC // 512):
            t0 = c * 512
            p.wait("vector", s_mm)
            p.wait("tensor", s_ve)
            p.wait("tensor", s_se)
            _ln_stats_chunk(p, onest, x3(t0, 512), ps_st[:, 0:512], ps_st[:, 512:1024],
                            xsq.rearrange("p (c t) -> p c t", c=NB), mean[:, :], alpha[:, :],
                            s_mm, s_ve, s_se, epsc=epsc)
            _ln_apply_chunk(p, x3(t0, 512), mean[:, :], alpha[:, :],
                            xln3[:, :, :], tA[:, :].rearrange("p (c t) -> p c t", c=NB), s_ve, s_se)
            if c == 0 and half == 0:
                _bsnap_se, _bsnap_ve, _bit = [], [], [0]
            for ib in range(NIB):
                q = _bit[0] % 2
                if ib == 0:
                    p.wait("tensor", s_ve)
                    p.wait("tensor", s_se)
                if _bit[0] >= 2:
                    p.wait("tensor", s_se, _bsnap_se[_bit[0] - 2])
                    p.wait("tensor", s_ve, _bsnap_ve[_bit[0] - 2])
                for cb in range(NB):
                    p.do("tensor", "matmul", ps_g[q][:, :],
                         wgh[:, cb * HH + ib * 128: cb * HH + (ib + 1) * 128],
                         xln[:, cb * 512:(cb + 1) * 512],
                         start=(cb == 0), stop=(cb == NB - 1))
                p.lastinc("tensor", s_mm)
                for cb in range(NB):
                    p.do("tensor", "matmul", ps_u[q][:, :],
                         wuh[:, cb * HH + ib * 128: cb * HH + (ib + 1) * 128],
                         xln[:, cb * 512:(cb + 1) * 512],
                         start=(cb == 0), stop=(cb == NB - 1))
                p.lastinc("tensor", s_mm)
                p.wait("scalar", s_mm, s_mm.v - 1)
                if _bit[0] >= 2:
                    p.wait("scalar", s_ve, _bsnap_ve[_bit[0] - 2])
                p.do("scalar", "activation", sgq[q][:, :], ps_g[q][:, :], AF.Silu,
                     scale=float(1.0 / 127.0), inc=s_se)
                p.wait("vector", s_mm)
                p.wait("vector", s_se)
                p.do("vector", "tensor_mul", hbuf[:, ib * 512:(ib + 1) * 512],
                     sgq[q][:, :], ps_u[q][:, :], inc=s_ve)
                _bsnap_se.append(s_se.v)
                _bsnap_ve.append(s_ve.v)
                _bit[0] += 1
            p.wait("vector", s_st)
            p.wait("tensor", s_ve)
            if c == 0 and half == 0:
                _dsnap, _dit = [], [0]
            for ob in range(NB):
                bank = ob % 2
                if _dit[0] >= 2:
                    p.wait("tensor", s_ve, _dsnap[_dit[0] - 2])
                for ib in range(NIB):
                    p.do("tensor", "matmul", ps_y[:, bank * 512:(bank + 1) * 512],
                         wdh[:, ib * D + ob * 128: ib * D + ob * 128 + 128],
                         hbuf[:, ib * 512:(ib + 1) * 512],
                         start=(ib == 0), stop=(ib == NIB - 1))
                p.lastinc("tensor", s_mm)
                p.wait("vector", s_mm)
                if half == 0:
                    p.do("vector", "scalar_tensor_tensor", stg[:, ob * 512:(ob + 1) * 512],
                         ps_y[:, bank * 512:(bank + 1) * 512], S2,
                         xsb[:, ob * TC + t0: ob * TC + t0 + 512],
                         ALU.mult, ALU.add, inc=s_ve)
                else:
                    p.do("vector", "tensor_scalar", stg[:, ob * 512:(ob + 1) * 512],
                         ps_y[:, bank * 512:(bank + 1) * 512], S2, None,
                         ALU.mult, inc=s_ve)
                _dsnap.append(s_ve.v)
                _dit[0] += 1
            p.wait("gpsimd", s_ve)
            for ob in range(NB):
                if half == 0:
                    p.do("gpsimd", "dma_start", out=ox_d[ob * 128:(ob + 1) * 128, t0:t0 + 512],
                         in_=stg[:, ob * 512:(ob + 1) * 512], inc=s_st, n=16)
                else:
                    p.do("gpsimd", "dma_start", out=ox_d[ob * 128:(ob + 1) * 128, t0:t0 + 512],
                         in_=stg[:, ob * 512:(ob + 1) * 512], accum_op=ALU.add, inc=s_st, n=16)
            if prep_gen is not None:
                for _ in range(4):
                    next(prep_gen, None)

    p.wait("gpsimd", s_st)
    if prep_gen is not None:
        for _ in prep_gen:
            pass
    stack.close()
    p.barrier()


def build_stage_b():
    nc = bass.Bass("TRN2", target_bir_lowering=False, detect_race_conditions=False)
    p = Prog(nc)
    cn = _mkconsts(nc, p)
    HID = 3072
    x_d = nc.dram_tensor("x", [D, TC], F32, kind="ExternalInput").ap()
    wg_d = nc.dram_tensor("wgt", [D, HID], I32, kind="ExternalInput").ap()
    wu_d = nc.dram_tensor("wut", [D, HID], I32, kind="ExternalInput").ap()
    wd_d = nc.dram_tensor("wdt", [HID, D], I32, kind="ExternalInput").ap()
    ox_d = nc.dram_tensor("ox", [D, TC], F32, kind="ExternalOutput").ap()
    emit_stage_b(nc, p, cn, x_d, wg_d, wu_d, wd_d, ox_d)
    p.emit()
    return nc


def _prep_stage_b(inputs, core, x_full):
    xs = x_full[2 * core: 2 * core + 2].reshape(TC, D)
    return {
        "x": np.ascontiguousarray(xs.T),
        "wgt": np.ascontiguousarray(inputs["L0_ffn_w_gate"].T),
        "wut": np.ascontiguousarray(inputs["L0_ffn_w_up"].T),
        "wdt": np.ascontiguousarray(inputs["L0_ffn_w_down"].T),
    }


def run_stage_b(inputs, x_full, cores=None):
    cores = list(range(NCORES)) if cores is None else cores
    nc = build_stage_b()
    in_maps = [_prep_stage_b(inputs, c, x_full) for c in cores]
    res = run_bass_kernel_spmd(nc, in_maps, core_ids=list(range(len(cores))))
    return [r["ox"] for r in res.results]


def _ln_stats_generic(p, onest, xap3, ps_sum, ps_ssq, xsq3, mean, alpha, s_mm, s_ve, s_se, epsc):
    _ln_stats_chunk(p, onest, xap3, ps_sum, ps_ssq, xsq3, mean, alpha, s_mm, s_ve, s_se, epsc=epsc)


def emit_stage_c(nc, p, cn, x_d, wtk_d, wtv_d, wtr_d, wck_d, wcv_d, wcr_d, vc_d, ox_d,
                 trunc='full', debug=False, prep_gen=None):
    import contextlib
    stack = contextlib.ExitStack()
    L = 256
    NCH_W = T // L
    LC = 512
    CH = 4096
    HH = CH // 2
    NHB = HH // 128

    _dbg_names = []
    def dbg(name, ap, when):
        if not debug:
            return
        d = nc.dram_tensor("dbg_" + name, list(ap.shape), ap.dtype, kind="ExternalOutput").ap()
        _dbg_names.append("dbg_" + name)
        sd = p.sem("dbgst"); sb = p.sem("blk")
        for s in when:
            p.wait("sync", s)
        p.do("sync", "dma_start", out=d, in_=ap, inc=sd, n=16)
        p.wait("sync", sd)
        p.do("sync", "sem_inc", sb.h, 1)
        sb.v += 1
        for e in ("tensor", "vector", "scalar", "gpsimd"):
            p.wait(e, sb)
    ident, onest, onesc, epsc, eps6c = cn["ident"], cn["onest"], cn["onesc"], cn["epsc"], cn["eps6c"]
    tinyc = cn["tinyc"]

    z = p.sbuf(stack, "z", [128, NB * TC], BF16)
    vc = p.sbuf(stack, "vc", [128, 88], F32)
    dcol = p.sbuf(stack, "dcol", [128, 24], F32)
    mean = p.sbuf(stack, "mean", [128, LC], F32)
    alpha = p.sbuf(stack, "alpha", [128, LC], F32)

    s_ld = p.sem("ld"); s_ldh = p.sem("ldh"); s_mm = p.sem("mm"); s_ve = p.sem("ve")
    s_se = p.sem("se"); s_st = p.sem("st"); s_bar = p.sem("bar")

    def barrier():
        for e in Prog.ENGINES:
            p.do(e, "nop", inc=s_bar)
        for e in Prog.ENGINES:
            p.wait(e, s_bar)

    VI = {"td": 0, "tf": 1, "g1": 2, "b1": 3, "g2": 4, "b2": 5,
          "mk": 6, "mv": 7, "mr": 8, "cmk": 9, "cmr": 10}
    def vcol(name, cb):
        return vc[:, VI[name] * 8 + cb: VI[name] * 8 + cb + 1]
    def vcol8(name):
        return vc[:, VI[name] * 8: VI[name] * 8 + 8]

    p.do("sync", "dma_start", out=vc[:, :], in_=vc_d[:, :], inc=s_ldh, n=16)
    p.wait("scalar", s_ldh)
    p.wait("vector", s_ldh)
    p.do("scalar", "activation", dcol[:, 0:8], vcol8("td"), AF.Exp, inc=s_se)
    p.do("vector", "tensor_scalar", dcol[:, 8:16], vcol8("td"), 0.0, float(L), ALU.max, ALU.mult, inc=s_ve)
    EW = dcol[:, 0:8]; WRL = dcol[:, 8:16]

    zz3 = z[:, :].rearrange("p (c t) -> p c t", c=NB)

    def ln_apply_bf16(xap3, out3, gname=None, bname=None, Lx=LC):
        mb = mean[:, 0:Lx].unsqueeze(1).broadcast_to([128, NB, Lx])
        ab = alpha[:, 0:Lx].unsqueeze(1).broadcast_to([128, NB, Lx])
        p.wait("vector", s_se)
        p.do("vector", "tensor_tensor", out3, xap3, mb, ALU.subtract)
        p.do("vector", "tensor_tensor", out3, out3, ab, ALU.mult, inc=s_ve)
        if gname is not None:
            for cb in range(NB):
                p.do("vector", "tensor_scalar", out3[:, cb, :], out3[:, cb, :],
                     vcol(gname, cb), vcol(bname, cb), ALU.mult, ALU.add,
                     inc=(s_ve if cb == NB - 1 else None))

    def dump_z_and_finish():
        p.wait("gpsimd", s_ve)
        for cb in range(NB):
            p.do("gpsimd", "dma_start", out=ox_d[cb * 128:(cb + 1) * 128, :],
                 in_=z[:, cb * TC:(cb + 1) * TC], inc=s_st, n=16)
        p.wait("gpsimd", s_st)
        stack.close()
        p.barrier()

    with contextlib.ExitStack() as st0:
        xch = p.sbuf(st0, "xch", [128, NB * LC], BF16)
        xsq0 = p.sbuf(st0, "xsq0", [128, NB * LC], BF16)
        ps_st0 = p.psum(st0, "ps_st0", [128, 1024], F32)
        for c in range(TC // LC):
            t0 = c * LC
            p.wait("gpsimd", s_ve)
            for cb in range(NB):
                p.do("gpsimd", "dma_start", out=xch[:, cb * LC:(cb + 1) * LC],
                     in_=x_d[cb * 128:(cb + 1) * 128, t0:t0 + LC], inc=s_ld, n=16)
            p.wait("vector", s_ld)
            p.wait("tensor", s_ld)
            p.wait("tensor", s_ve)
            p.wait("tensor", s_se)
            x3c = xch[:, :].rearrange("p (c t) -> p c t", c=NB)
            _ln_stats_chunk(p, onest, x3c, ps_st0[:, 0:512], ps_st0[:, 512:1024],
                            xsq0[:, :].rearrange("p (c t) -> p c t", c=NB),
                            mean[:, :], alpha[:, :], s_mm, s_ve, s_se, epsc=epsc)
            ln_apply_bf16(x3c, zz3[:, :, t0:t0 + LC])
    barrier()
    if trunc == 'p0':
        dump_z_and_finish()
        return

    with contextlib.ExitStack() as stw:
        wtk = p.sbuf(stw, "wtk_s", [128, NB * D], BF16)
        wtv = p.sbuf(stw, "wtv_s", [128, NB * D], BF16)
        wtr = p.sbuf(stw, "wtr_s", [128, NB * D], BF16)
        hb = p.sbuf(stw, "hb", [128, NB * L], BF16)
        hpb = p.sbuf(stw, "hpb", [128, NB * L], BF16)
        mx = p.sbuf(stw, "mx", [128, NB * L], BF16)
        kf = p.sbuf(stw, "kf", [128, NB * L], F32)
        vb = p.sbuf(stw, "vb", [128, NB * L], BF16)
        rb = p.sbuf(stw, "rb", [128, NB * L], BF16)
        kexp = p.sbuf(stw, "kexp", [128, NB * L], BF16)
        zbb = p.sbuf(stw, "zbb", [128, NB * L], BF16)
        Bbuf = p.sbuf(stw, "Bbuf", [128, NB * (L + 2)], BF16)
        Abuf = p.sbuf(stw, "Abuf", [128, NB * (L + 2)], BF16)
        ppp = p.sbuf(stw, "ppp", [128, NB * L], F32)
        t1f = p.sbuf(stw, "t1f", [128, NB * L], F32)
        t2f = p.sbuf(stw, "t2f", [128, NB * L], F32)

        e1u = hpb
        mxr = p.sbuf(stw, "mxr", [128, NB * L], BF16)
        rdn = t2f
        e1b = kexp
        e2b = mx
        dnb = zbb
        xsqw = kexp
        sm = p.sbuf(stw, "sm", [128, 64], F32)
        psA = p.psum(stw, "ps_kA", [128, NB * L], F32)
        psB = p.psum(stw, "ps_kB", [128, NB * L], F32)

        for cb in range(NB):
            p.do("gpsimd", "dma_start", out=wtk[:, cb * D:(cb + 1) * D],
                 in_=wtk_d[cb * 128:(cb + 1) * 128, :], inc=s_ld, n=16)
            p.do("gpsimd", "dma_start", out=wtv[:, cb * D:(cb + 1) * D],
                 in_=wtv_d[cb * 128:(cb + 1) * 128, :], inc=s_ld, n=16)
            p.do("gpsimd", "dma_start", out=wtr[:, cb * D:(cb + 1) * D],
                 in_=wtr_d[cb * 128:(cb + 1) * 128, :], inc=s_ld, n=16)
        p.wait("tensor", s_ld)

        h3 = hb[:, :].rearrange("p (c t) -> p c t", c=NB)
        hp3 = hpb[:, :].rearrange("p (c t) -> p c t", c=NB)
        mx3 = mx[:, :].rearrange("p (c t) -> p c t", c=NB)
        k3 = kf[:, :].rearrange("p (c t) -> p c t", c=NB)
        B3 = Bbuf[:, :].rearrange("p (c t) -> p c t", c=NB)
        A3 = Abuf[:, :].rearrange("p (c t) -> p c t", c=NB)
        aac = sm[:, 0:8]; btc = sm[:, 8:16]; mprev = sm[:, 16:24]
        kmax = sm[:, 24:32]; m2 = sm[:, 32:40]; negm2 = sm[:, 40:48]
        sc1 = sm[:, 48:56]; sc2 = sm[:, 56:64]

        def fm_matmul(wsb, rhs_sb, out_ps, Lx, evac):
            p.wait("tensor", s_ve)
            p.wait("tensor", s_se)
            for ob in range(NB):
                for cb in range(NB):
                    p.do("tensor", "matmul", out_ps[:, ob * Lx:(ob + 1) * Lx],
                         wsb[:, cb * D + ob * 128: cb * D + ob * 128 + 128],
                         rhs_sb[:, cb * Lx:(cb + 1) * Lx],
                         start=(cb == 0), stop=(cb == NB - 1))
            p.lastinc("tensor", s_mm)

        kf2 = p.sbuf(stw, "kf2", [128, NB * L], F32)
        vb2 = p.sbuf(stw, "vb2", [128, NB * L], BF16)
        mxv = p.sbuf(stw, "mxv", [128, NB * L], BF16)
        e2uv = mxv
        pcol = p.sbuf(stw, "pcol", [128, 8], BF16)
        kfq = [kf, kf2]; vbq = [vb, vb2]
        NCHT = BPC * NCH_W

        def front(gi):
            b, c = divmod(gi, NCH_W)
            t0 = b * T + c * L
            q = gi % 2
            p.wait("vector", s_mm)
            p.wait("tensor", s_ve)
            p.wait("tensor", s_se)
            _ln_stats_chunk(p, onest, zz3[:, :, t0:t0 + L], psA[:, 0:L], psA[:, L:2 * L],
                            xsqw[:, 0:NB * L].rearrange("p (c t) -> p c t", c=NB),
                            mean[:, 0:L], alpha[:, 0:L], s_mm, s_ve, s_se, epsc=epsc)
            _ssqsnap[gi] = p.sem("mm").v
            ln_apply_bf16(zz3[:, :, t0:t0 + L], h3, "g1", "b1", Lx=L)
            if c == 0:
                p.do("vector", "memset", pcol[:, :], 0.0, inc=s_ve)
                p.wait("vector", s_ve)
            p.do("vector", "tensor_copy", hp3[:, :, 0], pcol[:, :])
            p.do("vector", "tensor_copy", hp3[:, :, 1:L], h3[:, :, 0:L - 1], inc=s_ve)
            p.wait("vector", s_ve)
            p.do("vector", "tensor_copy", pcol[:, :], h3[:, :, L - 1], inc=s_ve)
            p.wait("vector", s_ve)
            p.do("vector", "tensor_tensor", mx3[:, :, :], h3, hp3, ALU.subtract, inc=s_ve)
            for mixn, mb_ in (("mr", mxr), ("mv", mxv), ("mk", mx)):
                mb3 = mb_[:, :].rearrange("p (c t) -> p c t", c=NB)
                for cb in range(NB):
                    p.do("vector", "scalar_tensor_tensor", mb3[:, cb, :], mx3[:, cb, :],
                         vcol(mixn, cb), hp3[:, cb, :], ALU.mult, ALU.add,
                         inc=(s_ve if cb == NB - 1 else None))
            for mat, (wsb, rhs_sb, pso) in enumerate(((wtk, mx, psA), (wtv, mxv, psB))):
                p.wait("tensor", s_ve)
                p.wait("tensor", s_se)
                for ob in range(NB):
                    for cb in range(NB):
                        p.do("tensor", "matmul", pso[:, ob * L:(ob + 1) * L],
                             wsb[:, cb * D + ob * 128: cb * D + ob * 128 + 128],
                             rhs_sb[:, cb * L:(cb + 1) * L],
                             start=(cb == 0), stop=(cb == NB - 1))
                p.lastinc("tensor", s_mm)
                if mat == 0:
                    _kmmsnap[gi] = p.sem("mm").v

        def wkv(gi):
            b, c = divmod(gi, NCH_W)
            t0 = b * T + c * L
            q = gi % 2
            kfc = kfq[q]; vbc = vbq[q]; rbc = rb
            k3c = kfc[:, :].rearrange("p (c t) -> p c t", c=NB)
            if c == 0:
                p.wait("vector", s_se)
                p.do("vector", "memset", aac, 0.0)
                p.do("vector", "memset", btc, 0.0)
                p.do("vector", "memset", mprev, -1e30, inc=s_ve)
                p.wait("vector", s_ve)
            p.wait("vector", s_se)
            p.do("vector", "reduce_max", kmax, k3c, axis=AX.X, inc=s_ve)
            p.wait("vector", s_ve)
            p.do("vector", "tensor_max", m2, kmax, mprev, inc=s_ve)
            p.wait("vector", s_ve)
            p.do("vector", "tensor_add", m2, m2, WRL, inc=s_ve)
            p.wait("vector", s_ve)
            p.do("vector", "tensor_scalar", negm2, m2, -1.0, None, ALU.mult)
            p.do("vector", "tensor_sub", sc1, mprev, m2, inc=s_ve)
            p.wait("scalar", s_ve)
            p.do("scalar", "activation", sc1, sc1, AF.Exp, inc=s_se)
            p.wait("vector", s_se)
            p.do("vector", "tensor_mul", sc2, btc, sc1, inc=s_ve)
            p.wait("vector", s_ve)
            p.do("vector", "tensor_copy", B3[:, :, L], sc2, inc=s_ve)
            p.wait("vector", s_ve)
            p.do("vector", "tensor_copy", A3[:, :, L], aac, inc=s_ve)
            p.wait("vector", s_ve)
            p.wait("scalar", s_mm, _ssqsnap.get(gi + 1, 0))
            p.wait("scalar", s_ve)
            for cb in range(NB):
                p.do("scalar", "activation", kexp[:, cb * L:(cb + 1) * L], k3c[:, cb, :],
                     AF.Exp, bias=negm2[:, cb:cb + 1],
                     inc=(s_se if cb == NB - 1 else None))
            p.wait("vector", s_se)
            for cb in range(NB):
                p.do("vector", "tensor_scalar", zbb[:, cb * L:(cb + 1) * L],
                     kexp[:, cb * L:(cb + 1) * L], EW[:, cb:cb + 1], None, ALU.mult,
                     inc=(s_ve if cb == NB - 1 else None))
            for cb in range(NB):
                d0 = EW[:, cb:cb + 1].broadcast_to([128, L])
                p.do("vector", "tensor_tensor_scan", B3[:, cb, 0:L], d0,
                     zbb[:, cb * L:(cb + 1) * L], B3[:, cb, L:L + 1],
                     ALU.mult, ALU.add, inc=(s_ve if cb == NB - 1 else None))
            p.wait("scalar", s_ve)
            ppp3 = ppp[:, :].rearrange("p (c t) -> p c t", c=NB)
            p.do("scalar", "activation", ppp3[:, :, 1:L], B3[:, :, 0:L - 1],
                 AF.Ln, bias=tinyc[:, :])
            p.do("scalar", "activation", ppp3[:, :, 0:1], B3[:, :, L:L + 1],
                 AF.Ln, bias=tinyc[:, :], inc=s_se)
            p.wait("vector", s_se)
            for cb in range(NB):
                p.do("vector", "tensor_scalar", ppp[:, cb * L:(cb + 1) * L],
                     ppp[:, cb * L:(cb + 1) * L], m2[:, cb:cb + 1], None, ALU.add,
                     inc=(s_ve if cb == NB - 1 else None))
            p.do("vector", "tensor_max", t1f[:, :], ppp[:, :], kfc[:, :])
            p.do("vector", "tensor_sub", t2f[:, :], ppp[:, :], t1f[:, :], inc=s_ve)
            p.wait("scalar", s_ve)
            p.do("scalar", "activation", e1u[:, :], t2f[:, :], AF.Exp, inc=s_se)
            p.wait("vector", s_se)
            p.do("vector", "tensor_sub", t2f[:, :], kfc[:, :], t1f[:, :], inc=s_ve)
            p.wait("scalar", s_ve)
            p.wait("scalar", s_mm)
            p.do("scalar", "activation", e2uv[:, :], t2f[:, :], AF.Exp, inc=s_se)
            p.wait("vector", s_se)
            p.do("vector", "tensor_mul", e2uv[:, :], e2uv[:, :], vbc[:, :], inc=s_ve)
            for cb in range(NB):
                p.do("vector", "tensor_tensor_scan", A3[:, cb, 0:L],
                     e1u[:, cb * L:(cb + 1) * L], e2uv[:, cb * L:(cb + 1) * L],
                     A3[:, cb, L:L + 1], ALU.mult, ALU.add,
                     inc=(s_ve if cb == NB - 1 else None))
            for cb in range(NB):
                p.do("vector", "tensor_scalar", t1f[:, cb * L:(cb + 1) * L],
                     k3c[:, cb, :], vcol("tf", cb), None, ALU.add,
                     inc=(s_ve if cb == NB - 1 else None))
            p.wait("vector", s_se)
            p.do("vector", "tensor_max", t2f[:, :], ppp[:, :], t1f[:, :])
            p.do("vector", "tensor_sub", ppp[:, :], ppp[:, :], t2f[:, :], inc=s_ve)
            p.wait("scalar", s_ve)
            p.do("scalar", "activation", e1b[:, :], ppp[:, :], AF.Exp, inc=s_se)
            p.wait("vector", s_se)
            p.do("vector", "tensor_sub", t1f[:, :], t1f[:, :], t2f[:, :], inc=s_ve)
            p.wait("scalar", s_ve)
            p.do("scalar", "activation", e2b[:, :], t1f[:, :], AF.Exp, inc=s_se)
            p.wait("vector", s_se)
            p.do("vector", "tensor_add", dnb[:, :], e1b[:, :], e2b[:, :], inc=s_ve)
            p.wait("scalar", s_ve)
            p.do("scalar", "activation", rdn[:, :], dnb[:, :], AF.Ln)
            p.do("scalar", "activation", rdn[:, :], rdn[:, :], AF.Exp, scale=-1.0, inc=s_se)
            e1b3 = e1b[:, :].rearrange("p (c t) -> p c t", c=NB)
            p.do("vector", "tensor_tensor", e1b3[:, :, 1:L], e1b3[:, :, 1:L],
                 A3[:, :, 0:L - 1], ALU.mult)
            p.do("vector", "tensor_tensor", e1b3[:, :, 0:1], e1b3[:, :, 0:1],
                 A3[:, :, L:L + 1], ALU.mult)
            p.do("vector", "tensor_mul", e2b[:, :], e2b[:, :], vbc[:, :])
            p.do("vector", "tensor_add", e1b[:, :], e1b[:, :], e2b[:, :])
            p.wait("vector", s_se)
            p.do("vector", "tensor_mul", e1b[:, :], e1b[:, :], rdn[:, :])
            p.do("vector", "tensor_mul", e1b[:, :], e1b[:, :], rbc[:, :])
            p.do("vector", "tensor_tensor", zz3[:, :, t0:t0 + L], zz3[:, :, t0:t0 + L],
                 e1b[:, :].rearrange("p (c t) -> p c t", c=NB), ALU.add, inc=s_ve)
            p.do("vector", "tensor_copy", aac, A3[:, :, L - 1])
            p.do("vector", "tensor_copy", btc, B3[:, :, L - 1])
            p.do("vector", "tensor_copy", mprev, m2, inc=s_ve)
            p.wait("vector", s_ve)

        def tail(gi):
            q = gi % 2
            b, c = divmod(gi, NCH_W)
            p.wait("scalar", s_mm, _kmmsnap[gi])
            p.do("scalar", "copy", kfq[q][:, :], psA[:, :], inc=s_se)
            p.wait("scalar", s_mm)
            p.do("scalar", "copy", vbq[q][:, :], psB[:, :], inc=s_se)
            kevac_v = p.sem("se").v - 1
            p.wait("tensor", s_se, kevac_v)
            for ob in range(NB):
                for cb in range(NB):
                    p.do("tensor", "matmul", psA[:, ob * L:(ob + 1) * L],
                         wtr[:, cb * D + ob * 128: cb * D + ob * 128 + 128],
                         mxr[:, cb * L:(cb + 1) * L],
                         start=(cb == 0), stop=(cb == NB - 1))
            p.lastinc("tensor", s_mm)
            p.wait("scalar", s_mm)
            p.wait("scalar", s_ve)
            p.do("scalar", "activation", rb[:, :], psA[:, :], AF.Sigmoid, inc=s_se)

        _ssqsnap, _kmmsnap = {}, {}
        for gi in range(NCHT + 1):
            if gi < NCHT:
                front(gi)
            if gi >= 1:
                wkv(gi - 1)
            if gi < NCHT:
                tail(gi)
            if prep_gen is not None:
                for _ in range(4):
                    next(prep_gen, None)
    barrier()
    if trunc == 'w':
        dump_z_and_finish()
        return

    with contextlib.ExitStack() as stc:
        wck = p.sbuf(stc, "wck_s", [128, NB * HH], BF16)
        wcv = p.sbuf(stc, "wcv_s", [128, NHB * D], BF16)
        wcr = p.sbuf(stc, "wcr_s", [128, NB * D], BF16)
        h2b = p.sbuf(stc, "h2b", [128, NB * (LC + 1)], BF16)
        mxc = p.sbuf(stc, "mxc", [128, NB * LC], BF16)
        rrb = p.sbuf(stc, "rrb", [128, NB * LC], BF16)
        kkb = p.sbuf(stc, "kkb", [128, NHB * LC], BF16)
        sgc = p.sbuf(stc, "sgc", [128, LC], BF16)
        sgc2 = p.sbuf(stc, "sgc2", [128, LC], BF16)
        sgcq = [sgc, sgc2]
        stg = p.sbuf(stc, "stg", [128, 2 * LC], F32)
        mxc2 = p.sbuf(stc, "mxc2", [128, NB * LC], BF16)
        xsqc = mxc
        ps_st = p.psum(stc, "ps_stc", [128, 1024], F32)
        ps_g = p.psum(stc, "ps_gc", [128, 1024], F32)
        ps_y = p.psum(stc, "ps_yc", [128, 1024], F32)

        h2e = h2b[:, :].rearrange("p (c t) -> p c t", c=NB)
        h23 = h2e[:, :, 1:LC + 1]
        hp23 = h2e[:, :, 0:LC]
        mxc3 = mxc[:, :].rearrange("p (c t) -> p c t", c=NB)
        mxc23 = mxc2[:, :].rearrange("p (c t) -> p c t", c=NB)

        for cb in range(NB):
            p.do("gpsimd", "dma_start", out=wcr[:, cb * D:(cb + 1) * D],
                 in_=wcr_d[cb * 128:(cb + 1) * 128, :], inc=s_ld, n=16)
        def cm_front(b, c):
            t0f = b * T + c * LC
            if c == 0:
                p.do("vector", "memset", h2e[:, :, 0:1], 0.0, inc=s_ve)
            else:
                p.do("vector", "tensor_copy", h2e[:, :, 0:1], h2e[:, :, LC:LC + 1], inc=s_ve)
            yield
            p.wait("vector", s_mm)
            p.do("vector", "tensor_mul",
                 xsqc[:, 0:NB * LC].rearrange("p (c t) -> p c t", c=NB),
                 zz3[:, :, t0f:t0f + LC], zz3[:, :, t0f:t0f + LC], inc=s_ve)
            yield
            p.wait("tensor", s_ve)
            for cb in range(NB):
                p.do("tensor", "matmul", ps_st[:, 0:512], onest[:, :],
                     zz3[:, cb, t0f:t0f + LC], start=(cb == 0), stop=(cb == NB - 1),
                     inc=(s_mm if cb == NB - 1 else None))
            for cb in range(NB):
                p.do("tensor", "matmul", ps_st[:, 512:1024], onest[:, :],
                     xsqc[:, cb * LC:(cb + 1) * LC],
                     start=(cb == 0), stop=(cb == NB - 1),
                     inc=(s_mm if cb == NB - 1 else None))
            smsnap = p.sem("mm").v
            yield
            p.wait("vector", s_mm, smsnap)
            p.do("vector", "tensor_scalar", mean[:, :], ps_st[:, 0:512], 1.0 / D,
                 None, ALU.mult, inc=s_ve)
            p.wait("vector", s_ve)
            p.do("vector", "tensor_mul", alpha[:, :], mean[:, :], mean[:, :], inc=s_ve)
            p.wait("vector", s_ve)
            p.do("vector", "scalar_tensor_tensor", alpha[:, :], ps_st[:, 512:1024],
                 1.0 / D, alpha[:, :], ALU.mult, ALU.subtract, inc=s_ve)
            p.wait("scalar", s_ve)
            p.do("scalar", "activation", alpha[:, :], alpha[:, :], AF.Ln, bias=epsc[:, :])
            p.do("scalar", "activation", alpha[:, :], alpha[:, :], AF.Exp, scale=-0.5, inc=s_se)
            yield
            ln_apply_bf16(zz3[:, :, t0f:t0f + LC], h23, "g2", "b2", Lx=LC)
            yield
            p.wait("vector", s_mm)
            p.do("vector", "tensor_tensor", mxc3[:, :, :], h23, hp23, ALU.subtract)
            for cb in range(NB):
                p.do("vector", "scalar_tensor_tensor", mxc3[:, cb, :], mxc3[:, cb, :],
                     vcol("cmr", cb), hp23[:, cb, :], ALU.mult, ALU.add,
                     inc=(s_ve if cb == NB - 1 else None))
            yield
            p.do("vector", "tensor_tensor", mxc23[:, :, :], h23, hp23, ALU.subtract)
            for cb in range(NB):
                p.do("vector", "scalar_tensor_tensor", mxc23[:, cb, :], mxc23[:, cb, :],
                     vcol("cmk", cb), hp23[:, cb, :], ALU.mult, ALU.add,
                     inc=(s_ve if cb == NB - 1 else None))
            yield

        weave = lambda fg: (next(fg, None) if fg is not None else None)
        chunks = [(half, b, c) for half in range(2) for b in range(BPC)
                  for c in range(T // LC)]
        for gi, (half, b, c) in enumerate(chunks):
            t0 = b * T + c * LC
            if gi == 0 or chunks[gi - 1][0] != half:
                p.wait("gpsimd", s_mm)
                for cb in range(NB):
                    p.do("gpsimd", "dma_start", out=wck[:, cb * HH:(cb + 1) * HH],
                         in_=wck_d[cb * 128:(cb + 1) * 128, half * HH:(half + 1) * HH], inc=s_ld, n=16)
                for ib in range(NHB):
                    p.do("gpsimd", "dma_start", out=wcv[:, ib * D:(ib + 1) * D],
                         in_=wcv_d[half * HH + ib * 128: half * HH + (ib + 1) * 128, :], inc=s_ld, n=16)
                p.wait("tensor", s_ld)
            if gi == 0:
                for _ in cm_front(b, c):
                    pass
            fg = None
            if gi + 1 < len(chunks):
                nh, nb_, nch = chunks[gi + 1]
                fg = cm_front(nb_, nch)
            p.wait("tensor", s_ve)
            p.wait("tensor", s_se)
            p.wait("scalar", s_ve)
            if gi == 0:
                _rsnap, _rit = [], [0]
            for ob in range(NB):
                bank = _rit[0] % 2
                if _rit[0] >= 2:
                    p.wait("tensor", s_se, _rsnap[_rit[0] - 2])
                for cb in range(NB):
                    p.do("tensor", "matmul", ps_g[:, bank * 512:(bank + 1) * 512],
                         wcr[:, cb * D + ob * 128: cb * D + ob * 128 + 128],
                         mxc[:, cb * LC:(cb + 1) * LC],
                         start=(cb == 0), stop=(cb == NB - 1))
                p.lastinc("tensor", s_mm)
                p.wait("scalar", s_mm)
                p.do("scalar", "activation", rrb[:, ob * LC:(ob + 1) * LC],
                     ps_g[:, bank * 512:(bank + 1) * 512], AF.Sigmoid, inc=s_se)
                _rsnap.append(s_se.v)
                _rit[0] += 1
            p.wait("tensor", s_ve)
            p.wait("tensor", s_se)
            if gi == 0:
                _ksnap_ve, _ksnap_se, _kit = [], [], [0]
            for hbk in range(NHB):
                bank = _kit[0] % 2
                if _kit[0] >= 2:
                    p.wait("tensor", s_ve, _ksnap_ve[_kit[0] - 2])
                    p.wait("tensor", s_se, _ksnap_se[_kit[0] - 2])
                for cb in range(NB):
                    p.do("tensor", "matmul", ps_g[:, bank * 512:(bank + 1) * 512],
                         wck[:, cb * HH + hbk * 128: cb * HH + hbk * 128 + 128],
                         mxc2[:, cb * LC:(cb + 1) * LC],
                         start=(cb == 0), stop=(cb == NB - 1))
                p.lastinc("tensor", s_mm)
                p.wait("scalar", s_mm)
                if _kit[0] >= 2:
                    p.wait("scalar", s_ve, _ksnap_ve[_kit[0] - 2])
                p.do("scalar", "activation", sgcq[bank][:, :], ps_g[:, bank * 512:(bank + 1) * 512],
                     AF.Relu, inc=s_se)
                p.wait("vector", s_se)
                p.do("vector", "tensor_mul", kkb[:, hbk * LC:(hbk + 1) * LC],
                     sgcq[bank][:, :], sgcq[bank][:, :], inc=s_ve)
                _ksnap_ve.append(s_ve.v)
                _ksnap_se.append(s_se.v)
                _kit[0] += 1
                if hbk in (1, 3, 9, 12):
                    weave(fg)
            p.wait("tensor", s_ve)
            if gi == 0:
                _vsnap, _vit = [], [0]
            for ob in range(NB):
                bank = _vit[0] % 2
                if _vit[0] >= 2:
                    p.wait("tensor", s_ve, _vsnap[_vit[0] - 2])
                for ib in range(NHB):
                    p.do("tensor", "matmul", ps_y[:, bank * 512:(bank + 1) * 512],
                         wcv[:, ib * D + ob * 128: ib * D + ob * 128 + 128],
                         kkb[:, ib * LC:(ib + 1) * LC],
                         start=(ib == 0), stop=(ib == NHB - 1))
                p.lastinc("tensor", s_mm)
                p.wait("vector", s_mm)
                p.wait("vector", s_st, max(0, s_st.v - 16))
                if half == 0:
                    p.do("vector", "tensor_mul", stg[:, bank * LC:(bank + 1) * LC],
                         rrb[:, ob * LC:(ob + 1) * LC], ps_y[:, bank * 512:(bank + 1) * 512])
                    p.do("vector", "tensor_add", stg[:, bank * LC:(bank + 1) * LC],
                         stg[:, bank * LC:(bank + 1) * LC],
                         z[:, ob * TC + t0: ob * TC + t0 + LC], inc=s_ve)
                else:
                    p.do("vector", "tensor_mul", stg[:, bank * LC:(bank + 1) * LC],
                         rrb[:, ob * LC:(ob + 1) * LC], ps_y[:, bank * 512:(bank + 1) * 512],
                         inc=s_ve)
                _vsnap.append(s_ve.v)
                _vit[0] += 1
                p.wait("gpsimd", s_ve)
                if half == 0:
                    p.do("gpsimd", "dma_start", out=ox_d[ob * 128:(ob + 1) * 128, t0:t0 + LC],
                         in_=stg[:, bank * LC:(bank + 1) * LC], inc=s_st, n=16)
                else:
                    p.do("gpsimd", "dma_start", out=ox_d[ob * 128:(ob + 1) * 128, t0:t0 + LC],
                         in_=stg[:, bank * LC:(bank + 1) * LC], accum_op=ALU.add, inc=s_st, n=16)
                if ob in (1, 2, 4, 6):
                    weave(fg)
            if fg is not None:
                for _ in fg:
                    pass
        p.wait("gpsimd", s_st)
        for cb in range(NB):
            p.do("gpsimd", "dma_start", out=ox_d[cb * 128:(cb + 1) * 128, :],
                 in_=x_d[cb * 128:(cb + 1) * 128, :], accum_op=ALU.add,
                 max_dma_last_dim=2048, inc=s_st, n=16)
        p.wait("gpsimd", s_st)
    stack.close()
    p.barrier()


def build_stage_c(trunc='full', debug=False):
    import contextlib
    nc = bass.Bass("TRN2", target_bir_lowering=False, detect_race_conditions=False)
    p = Prog(nc)
    cn = _mkconsts(nc, p)
    CH = 4096
    x_d = nc.dram_tensor("x", [D, TC], F32, kind="ExternalInput").ap()
    wtk_d = nc.dram_tensor("wtk", [D, D], F32, kind="ExternalInput").ap()
    wtv_d = nc.dram_tensor("wtv", [D, D], F32, kind="ExternalInput").ap()
    wtr_d = nc.dram_tensor("wtr", [D, D], F32, kind="ExternalInput").ap()
    wck_d = nc.dram_tensor("wck", [D, CH], F32, kind="ExternalInput").ap()
    wcv_d = nc.dram_tensor("wcv", [CH, D], F32, kind="ExternalInput").ap()
    wcr_d = nc.dram_tensor("wcr", [D, D], F32, kind="ExternalInput").ap()
    vc_d = nc.dram_tensor("vcols", [128, 88], F32, kind="ExternalInput").ap()
    ox_d = nc.dram_tensor("ox", [D, TC], F32, kind="ExternalOutput").ap()
    emit_stage_c(nc, p, cn, x_d, wtk_d, wtv_d, wtr_d, wck_d, wcv_d, wcr_d, vc_d, ox_d,
                 trunc=trunc, debug=debug)
    p.emit()
    return nc


def _prep_stage_c(inputs, core, x_full):
    xs = x_full[2 * core: 2 * core + 2].reshape(TC, D)
    def col8(name):
        return inputs[name].astype(np.float32).reshape(NB, 128).T
    vcols = np.concatenate([
        col8("L1_attn_time_decay"), col8("L1_attn_time_first"),
        col8("L1_attn_ln1_g"), col8("L1_attn_ln1_b"),
        col8("L1_attn_ln2_g"), col8("L1_attn_ln2_b"),
        inputs["L1_attn_mix_k"].astype(np.float32).reshape(NB, 128).T,
        inputs["L1_attn_mix_v"].astype(np.float32).reshape(NB, 128).T,
        inputs["L1_attn_mix_r"].astype(np.float32).reshape(NB, 128).T,
        inputs["L1_attn_cmix_k"].astype(np.float32).reshape(NB, 128).T,
        inputs["L1_attn_cmix_r"].astype(np.float32).reshape(NB, 128).T,
    ], axis=1)
    return {
        "x": np.ascontiguousarray(xs.T),
        "wtk": np.ascontiguousarray(inputs["L1_attn_w_tk"]),
        "wtv": np.ascontiguousarray(inputs["L1_attn_w_tv"]),
        "wtr": np.ascontiguousarray(inputs["L1_attn_w_tr"]),
        "wck": np.ascontiguousarray(inputs["L1_attn_w_ck"]),
        "wcv": np.ascontiguousarray(inputs["L1_attn_w_cv"]),
        "wcr": np.ascontiguousarray(inputs["L1_attn_w_cr"]),
        "vcols": np.ascontiguousarray(vcols),
    }


def run_stage_c(inputs, x_full, cores=None):
    cores = list(range(NCORES)) if cores is None else cores
    nc = build_stage_c()
    in_maps = [_prep_stage_c(inputs, c, x_full) for c in cores]
    res = run_bass_kernel_spmd(nc, in_maps, core_ids=list(range(len(cores))))
    return [r["ox"] for r in res.results]


def gen_prep_moe(nc, p, ring, w1_d, w3_d, w2_d, ws1_d, ws3_d, ws2_d,
                 w1b_d, w3b_d, w2b_d):
    E = 16
    s_pi = p.sem("pld"); s_po = p.sem("pst")
    jobs = []
    for e in range(E + 1):
        s1 = ws1_d if e == E else w1_d[e]
        s3 = ws3_d if e == E else w3_d[e]
        s2 = ws2_d if e == E else w2_d[e]
        jobs.append((s1.rearrange("(c p) i -> p c i", p=128), w1b_d[e, :, :], 768))
        jobs.append((s3.rearrange("(c p) i -> p c i", p=128), w3b_d[e, :, :], 768))
        jobs.append((s2.rearrange("(c p) d -> p c d", p=128), w2b_d[e, :, :], 1024))
    for src_ap, dst_ap, w in jobs:
        p.wait("gpsimd", s_po)
        p.do("gpsimd", "dma_start",
             out=ring[:, :].rearrange("p (c i) -> p c i", i=w), in_=src_ap, inc=s_pi, n=16)
        p.wait("sync", s_pi)
        p.do("sync", "dma_start", out=dst_ap, in_=ring[:, :], inc=s_po, n=16)
        yield
    p.wait("sync", s_po)
    p.do("sync", "sem_inc", p.sem("prepdone").h, 1)
    p.sem("prepdone").v += 1


def emit_stage_d(nc, p, cn, x_d, w1_d, w3_d, w2_d, ws1_d, ws3_d, ws2_d, gw_d, gb_d, ox_d,
                 w1b_d=None, w3b_d=None, w2b_d=None, prep_here=True):
    import contextlib
    stack = contextlib.ExitStack()
    E = 16
    I = 768
    NIB = I // 128
    PL = 1024
    NPASS = TC // PL
    NTI = PL // 128
    BIG = 1.0e30
    ident, onest, onesc, epsc, eps6c = cn["ident"], cn["onest"], cn["onesc"], cn["epsc"], cn["eps6c"]
    onesr = cn["onesr"]; ident32 = cn["ident32"]

    s_ld = p.sem("ld"); s_ldh = p.sem("ldh"); s_mm = p.sem("mm"); s_ve = p.sem("ve")
    s_se = p.sem("se"); s_st = p.sem("st"); s_rw = p.sem("rw")

    xln = p.sbuf(stack, "xlnp", [128, NB * PL], BF16)
    xch = p.sbuf(stack, "xch", [128, NB * 512], BF16)
    yab = p.sbuf(stack, "yab", [128, NB * PL], F32)
    w1s = [p.sbuf(stack, f"w1s{i}", [128, NB * I], BF16) for i in range(2)]
    w3s = [p.sbuf(stack, f"w3s{i}", [128, NB * I], BF16) for i in range(2)]
    w2s = [p.sbuf(stack, f"w2s{i}", [128, NIB * D], BF16) for i in range(2)]
    hbuf = p.sbuf(stack, "hbufd", [128, NIB * PL], BF16)
    sgm = [p.sbuf(stack, f"sgmd{i}", [128, 512], BF16) for i in range(2)]
    cmbq = [p.sbuf(stack, f"cmbq{i}", [128, PL], BF16) for i in range(2)]
    SELS = p.sbuf(stack, "SELS", [16, E * 128], F32)
    gws = p.sbuf(stack, "gws", [128, NB * E], BF16)
    gbb = p.sbuf(stack, "gbb", [128, E], F32)
    mean = p.sbuf(stack, "meand", [128, 512], F32)
    alpha = p.sbuf(stack, "alphad", [128, 512], F32)
    LG = p.sbuf(stack, "LG", [128, NTI * E], F32)
    SB = p.sbuf(stack, "SBr", [128, NTI * E], F32)
    SCR = p.sbuf(stack, "SCR", [128, NTI * E], F32)
    GA = p.sbuf(stack, "GA", [128, NTI * 4], F32)
    GB = p.sbuf(stack, "GB", [128, NTI * 4], F32)
    GC = p.sbuf(stack, "GC", [128, NTI * 4], F32)
    GD = p.sbuf(stack, "GD", [128, NTI * 4], F32)
    R1 = p.sbuf(stack, "R1", [128, NTI], F32)
    R2 = p.sbuf(stack, "R2", [128, NTI], F32)
    cmT = p.sbuf(stack, "cmT", [16, NTI * 128], F32)
    ps_misc = p.psum(stack, "ps_miscd", [128, 1024], F32)
    ps_g = [p.psum(stack, f"ps_gd{i}", [128, 512], F32) for i in range(2)]
    ps_u = [p.psum(stack, f"ps_ud{i}", [128, 512], F32) for i in range(2)]
    ps_y = p.psum(stack, "ps_yd", [128, 1024], F32)

    sI = p.sem("init")
    kcol = p.sbuf(stack, "selk", [16, 1], F32)
    p.do("gpsimd", "iota", SELS[:, :].rearrange("p (e m) -> p e m", e=E),
         [[1, E], [0, 128]], channel_multiplier=0,
         allow_small_or_imprecise_dtypes=True, inc=sI)
    p.do("gpsimd", "iota", kcol[:, :], [[0, 1]], channel_multiplier=1,
         allow_small_or_imprecise_dtypes=True, inc=sI)
    p.wait("vector", sI)
    p.do("vector", "tensor_scalar", SELS[:, :], SELS[:, :], kcol[:, :], None,
         ALU.is_equal, inc=sI)

    p.do("sync", "dma_start", out=gbb[0:1, :], in_=gb_d[:, :], inc=s_ldh, n=16)
    _gbv = p.sem("ldh").v
    for cb in range(NB):
        p.do("gpsimd", "dma_start", out=gws[:, cb * E:(cb + 1) * E],
             in_=gw_d[cb * 128:(cb + 1) * 128, :], inc=s_ld, n=16)
    _gwv = p.sem("ld").v

    if prep_here:
        ring = p.sbuf(stack, "prep_ring", [128, NB * I], BF16)
        for _ in gen_prep_moe(nc, p, ring, w1_d, w3_d, w2_d, ws1_d, ws3_d, ws2_d,
                              w1b_d, w3b_d, w2b_d):
            pass
    p.wait("sync", p.sem("prepdone"))

    def wload(e, buf):
        p.do("sync", "dma_start", out=w1s[buf][:, :], in_=w1b_d[e, :, :], inc=s_ldh, n=16)
        p.do("sync", "dma_start", out=w3s[buf][:, :], in_=w3b_d[e, :, :], inc=s_ldh, n=16)
        p.do("sync", "dma_start", out=w2s[buf][:, :], in_=w2b_d[e, :, :], inc=s_ldh, n=16)

    vsync = lambda: (p.lastinc("vector", s_ve), p.wait("vector", s_ve))

    _wsnap = {}
    _gsnap_se, _gsnap_ve, _ysnap = [], [], []
    _bcsnap, _bcit = [], [0]
    _git, _yit = [0], [0]
    wload(0, 0)
    _wsnap[0] = p.sem("ldh").v
    for cb in range(NB):
        p.do("gpsimd", "dma_start", out=ox_d[cb * 128:(cb + 1) * 128, :],
             in_=x_d[cb * 128:(cb + 1) * 128, :], inc=s_st, n=16)
    _oxv = p.sem("st").v
    xlnB = p.sbuf(stack, "xlnpB", [128, NB * PL], BF16)
    xln2 = [xln, xlnB]

    def head(pi):
        xl = xln2[pi % 2]
        pt0h = pi * PL
        x3c = xch[:, :].rearrange("p (c t) -> p c t", c=NB)
        for c in range(PL // 512):
            t0 = pt0h + c * 512
            p.wait("gpsimd", s_ve)
            p.wait("gpsimd", s_mm)
            for cb in range(NB):
                p.do("gpsimd", "dma_start", out=xch[:, cb * 512:(cb + 1) * 512],
                     in_=x_d[cb * 128:(cb + 1) * 128, t0:t0 + 512], inc=s_ld, n=16)
            yield
            xsq3 = xl[:, :].rearrange("p (c t) -> p c t", c=NB)[:, :, c * 512:(c + 1) * 512]
            p.wait("vector", s_ld)
            p.wait("vector", s_mm)
            p.do("vector", "tensor_mul", xsq3, x3c, x3c, inc=s_ve)
            yield
            p.wait("tensor", s_ld)
            p.wait("tensor", s_ve)
            p.wait("tensor", s_se)
            for cb in range(NB):
                p.do("tensor", "matmul", ps_misc[:, 0:512], onest[:, :], x3c[:, cb, :],
                     start=(cb == 0), stop=(cb == NB - 1), inc=(s_mm if cb == NB - 1 else None))
            for cb in range(NB):
                p.do("tensor", "matmul", ps_misc[:, 512:1024], onest[:, :], xsq3[:, cb, :],
                     start=(cb == 0), stop=(cb == NB - 1), inc=(s_mm if cb == NB - 1 else None))
            smsnap = p.sem("mm").v
            yield
            p.wait("vector", s_mm, smsnap)
            p.do("vector", "tensor_scalar", mean[:, :], ps_misc[:, 0:512], 1.0 / D,
                 None, ALU.mult, inc=s_ve)
            p.wait("vector", s_ve)
            p.do("vector", "tensor_mul", alpha[:, :], mean[:, :], mean[:, :], inc=s_ve)
            p.wait("vector", s_ve)
            p.do("vector", "scalar_tensor_tensor", alpha[:, :], ps_misc[:, 512:1024], 1.0 / D,
                 alpha[:, :], ALU.mult, ALU.subtract, inc=s_ve)
            p.wait("scalar", s_ve)
            p.do("scalar", "activation", alpha[:, :], alpha[:, :], AF.Ln, bias=epsc[:, :])
            p.do("scalar", "activation", alpha[:, :], alpha[:, :], AF.Exp, scale=-0.5, inc=s_se)
            yield
            mb = mean[:, :].unsqueeze(1).broadcast_to([128, NB, 512])
            ab = alpha[:, :].unsqueeze(1).broadcast_to([128, NB, 512])
            o3 = xl[:, :].rearrange("p (c t) -> p c t", c=NB)[:, :, c * 512:(c + 1) * 512]
            p.wait("vector", s_se)
            p.do("vector", "tensor_tensor", o3, x3c, mb, ALU.subtract)
            p.do("vector", "tensor_tensor", o3, o3, ab, ALU.mult, inc=s_ve)
            yield
        p.wait("tensor", s_ve)
        p.wait("tensor", s_ld, _gwv)
        p.wait("tensor", s_ldh, _gbv)
        for ti in range(NTI):
            for cb in range(NB):
                p.do("tensor", "matmul", ps_misc[:, ti * E:(ti + 1) * E],
                     xl[:, cb * PL + ti * 128: cb * PL + (ti + 1) * 128],
                     gws[:, cb * E:(cb + 1) * E],
                     start=(cb == 0), stop=(cb == NB - 1), inc=(s_mm if cb == NB - 1 else None))
        p.do("tensor", "matmul", ps_misc[:, 512:512 + E], onesr[:, :], gbb[0:1, :],
             start=True, stop=True, inc=s_mm)
        p.wait("scalar", s_mm)
        p.do("scalar", "activation", LG[:, :], ps_misc[:, 0:NTI * E], AF.Sigmoid, inc=s_se)
        yield
        p.wait("vector", s_se)
        gb3 = ps_misc[:, 512:512 + E].unsqueeze(1).broadcast_to([128, NTI, E])
        LG3 = LG[:, :].rearrange("p (t e) -> p t e", t=NTI)
        SB3 = SB[:, :].rearrange("p (t e) -> p t e", t=NTI)
        SC3 = SCR[:, :].rearrange("p (t e) -> p t e", t=NTI)
        p.do("vector", "tensor_tensor", SB3, LG3, gb3, ALU.add, inc=s_ve)
        p.wait("vector", s_ve)
        SBg = SB[:, :].rearrange("p (t g e) -> p (t g) e", g=4, e=4)
        a0, a1, a2, a3 = (SBg[:, :, j] for j in range(4))
        g = lambda t_: t_[:, :]
        p.do("vector", "tensor_max", GA[:, :], a0, a1)
        p.do("vector", "tensor_tensor", GB[:, :], a0, a1, ALU.min, inc=s_ve)
        p.wait("vector", s_ve)
        p.do("vector", "tensor_max", GC[:, :], a2, a3)
        p.do("vector", "tensor_tensor", GD[:, :], a2, a3, ALU.min, inc=s_ve)
        p.wait("vector", s_ve)
        yield
        p.do("vector", "tensor_sub", GB[:, :], GB[:, :], GD[:, :], inc=s_ve)
        p.wait("vector", s_ve)
        p.do("vector", "tensor_tensor", R1[:, 0:NTI * 4].bitcast(F32) if False else GB[:, :],
             GB[:, :], GB[:, :], ALU.bypass) if False else None
        p.do("vector", "tensor_tensor", SCR[:, 0:NTI * 4], GA[:, :], GC[:, :], ALU.is_gt, inc=s_ve)
        p.wait("vector", s_ve)
        p.do("vector", "tensor_mul", GB[:, :], GB[:, :], SCR[:, 0:NTI * 4], inc=s_ve)
        p.wait("vector", s_ve)
        yield
        p.do("vector", "tensor_add", GB[:, :], GB[:, :], GD[:, :], inc=s_ve)
        p.wait("vector", s_ve)
        p.do("vector", "tensor_tensor", GD[:, :], GA[:, :], GC[:, :], ALU.min, inc=s_ve)
        p.wait("vector", s_ve)
        p.do("vector", "tensor_max", GB[:, :], GB[:, :], GD[:, :], inc=s_ve)
        p.wait("vector", s_ve)
        yield
        p.do("vector", "tensor_max", GA[:, :], GA[:, :], GC[:, :], inc=s_ve)
        p.wait("vector", s_ve)
        p.do("vector", "tensor_add", GA[:, :], GA[:, :], GB[:, :], inc=s_ve)
        p.wait("vector", s_ve)
        GA4 = GA[:, :].rearrange("p (t g) -> p t g", t=NTI)
        p.do("vector", "reduce_max", R1[:, :], GA4, axis=AX.X, inc=s_ve)
        p.wait("vector", s_ve)
        yield
        r1b = R1[:, :].unsqueeze(2).broadcast_to([128, NTI, 4])
        p.do("vector", "tensor_tensor", GB[:, :].rearrange("p (t g) -> p t g", t=NTI), GA4, r1b,
             ALU.is_ge, inc=s_ve)
        p.wait("vector", s_ve)
        p.do("vector", "scalar_tensor_tensor", GC[:, :], GB[:, :], -BIG, GA[:, :],
             ALU.mult, ALU.add, inc=s_ve)
        p.wait("vector", s_ve)
        p.do("vector", "reduce_max", R2[:, :], GC[:, :].rearrange("p (t g) -> p t g", t=NTI),
             axis=AX.X, inc=s_ve)
        p.wait("vector", s_ve)
        yield
        r2b = R2[:, :].unsqueeze(2).broadcast_to([128, NTI, 4])
        p.do("vector", "tensor_tensor", GB[:, :].rearrange("p (t g) -> p t g", t=NTI), GA4, r2b,
             ALU.is_ge, inc=s_ve)
        p.wait("vector", s_ve)
        gm4 = GB[:, :].rearrange("p (t g) -> p t g 1", t=NTI) if False else None
        gmr = GB[:, :].rearrange("p (t g) -> p t g", t=NTI).unsqueeze(3).broadcast_to([128, NTI, 4, 4])
        SB4 = SB[:, :].rearrange("p (t g e) -> p t g e", g=4, e=4)
        SC4 = SCR[:, :].rearrange("p (t g e) -> p t g e", g=4, e=4)
        p.do("vector", "tensor_scalar", GD[:, :], GB[:, :], BIG, -BIG, ALU.mult, ALU.add, inc=s_ve)
        p.wait("vector", s_ve)
        gdr = GD[:, :].rearrange("p (t g) -> p t g", t=NTI).unsqueeze(3).broadcast_to([128, NTI, 4, 4])
        p.do("vector", "tensor_tensor", SC4, SB4, gmr, ALU.mult, inc=s_ve)
        p.wait("vector", s_ve)
        yield
        p.do("vector", "tensor_tensor", SC4, SC4, gdr, ALU.add, inc=s_ve)
        p.wait("vector", s_ve)
        for it in range(3):
            p.do("vector", "reduce_max", R1[:, :], SC3, axis=AX.X, inc=s_ve)
            p.wait("vector", s_ve)
            rb = R1[:, :].unsqueeze(2).broadcast_to([128, NTI, E])
            p.do("vector", "tensor_tensor", SB3 if False else LG3, SC3, rb, ALU.is_ge) if False else None
            p.do("vector", "tensor_tensor", GC[:, 0:NTI].rearrange("p t -> p t 1") if False else SC3,
                 SC3, rb, ALU.is_ge) if False else None
            p.do("vector", "tensor_tensor", GD[:, 0:NTI * 4], GD[:, 0:NTI * 4], GD[:, 0:NTI * 4], ALU.bypass) if False else None
            p.do("vector", "tensor_tensor", SC4, SC4, R1[:, :].unsqueeze(2).unsqueeze(3).broadcast_to([128, NTI, 4, 4]), ALU.is_ge, inc=s_ve) if False else None
            p.do("vector", "tensor_tensor", SB3, SC3, rb, ALU.is_ge, inc=s_ve)
            p.wait("vector", s_ve)
            yield
            p.do("vector", "scalar_tensor_tensor", SCR[:, :], SB[:, :], -BIG, SCR[:, :],
                 ALU.mult, ALU.add, inc=s_ve)
            p.wait("vector", s_ve)
        p.do("vector", "reduce_max", R1[:, :], SC3, axis=AX.X, inc=s_ve)
        p.wait("vector", s_ve)
        p.do("vector", "tensor_tensor", SB3, LG3, gb3, ALU.add, inc=s_ve)
        p.wait("vector", s_ve)
        yield
        p.do("vector", "tensor_tensor", SB4, SB4, gmr, ALU.mult, inc=s_ve)
        p.wait("vector", s_ve)
        p.do("vector", "tensor_tensor", SB4, SB4, gdr, ALU.add, inc=s_ve)
        p.wait("vector", s_ve)
        rb = R1[:, :].unsqueeze(2).broadcast_to([128, NTI, E])
        p.do("vector", "tensor_tensor", SB3, SB3, rb, ALU.is_ge, inc=s_ve)
        p.wait("vector", s_ve)
        yield
        p.do("vector", "tensor_mul", SB[:, :], SB[:, :], LG[:, :], inc=s_ve)
        p.wait("vector", s_ve)
        p.do("vector", "reduce_sum", R1[:, :], SB3, axis=AX.X, inc=s_ve)
        p.wait("vector", s_ve)
        p.do("vector", "tensor_scalar", R2[:, :], R1[:, :], 1e-20, None, ALU.add, inc=s_ve)
        p.wait("scalar", s_ve)
        p.do("scalar", "activation", R2[:, :], R2[:, :], AF.Ln)
        p.do("scalar", "activation", R2[:, :], R2[:, :], AF.Exp, scale=-1.0, inc=s_se)
        p.wait("vector", s_se)
        rwb = R2[:, :].unsqueeze(2).broadcast_to([128, NTI, E])
        p.do("vector", "tensor_tensor", SB3, SB3, rwb, ALU.mult, inc=s_ve)
        p.wait("vector", s_ve)
        yield
        p.do("vector", "tensor_scalar", SB[:, :], SB[:, :], 2.5, None, ALU.mult, inc=s_ve)
        p.wait("tensor", s_ve)
        for ti in range(NTI):
            p.do("tensor", "transpose", ps_misc[0:16, ti * 128:(ti + 1) * 128],
                 SB[:, ti * E:(ti + 1) * E], ident32[:, :], inc=s_mm)
        p.wait("vector", s_mm)
        p.do("vector", "tensor_copy", cmT[:, :], ps_misc[0:16, 0:NTI * 128], inc=s_ve)
    for _ in head(0):
        pass
    for pi in range(NPASS):
        pt0 = pi * PL
        xln_p = xln2[pi % 2]
        hg = head(pi + 1) if pi + 1 < NPASS else None
        for ei in range(E + 1):
            gctr = pi * (E + 1) + ei
            buf = gctr % 2
            if ei < E or pi + 1 < NPASS:
                p.wait("sync", s_mm)
                wload((ei + 1) % (E + 1), 1 - buf)
                _wsnap[(gctr + 1)] = p.sem("ldh").v
            p.wait("tensor", s_ldh, _wsnap.get(gctr, 0))
            if ei < E:
                cq = cmbq[ei % 2]
                p.wait("scalar", s_ve)
                if ei in (0, 15):
                    p.wait("tensor", s_ve)
                    p.wait("tensor", s_se)
                for h2 in range(2):
                    bank = ps_misc[:, (_bcit[0] % 2) * 512:(_bcit[0] % 2) * 512 + 512]
                    if _bcit[0] >= 2:
                        p.wait("tensor", s_se, _bcsnap[_bcit[0] - 2])
                    p.do("tensor", "matmul", bank, SELS[:, ei * 128:(ei + 1) * 128],
                         cmT[:, h2 * 512:(h2 + 1) * 512], start=True, stop=True, inc=s_mm)
                    p.wait("scalar", s_mm)
                    p.do("scalar", "copy", cq[:, h2 * 512:(h2 + 1) * 512], bank, inc=s_se)
                    _bcsnap.append(p.sem("se").v)
                    _bcit[0] += 1
            for ib in range(NIB):
                for h2 in range(2):
                    tc0 = h2 * 512
                    q = _git[0] % 2
                    if _git[0] >= 2:
                        p.wait("tensor", s_se, _gsnap_se[_git[0] - 2])
                        p.wait("tensor", s_ve, _gsnap_ve[_git[0] - 2])
                    for cb in range(NB):
                        p.do("tensor", "matmul", ps_g[q][:, :],
                             w1s[buf][:, cb * I + ib * 128: cb * I + (ib + 1) * 128],
                             xln_p[:, cb * PL + tc0: cb * PL + tc0 + 512],
                             start=(cb == 0), stop=(cb == NB - 1))
                    p.lastinc("tensor", s_mm)
                    for cb in range(NB):
                        p.do("tensor", "matmul", ps_u[q][:, :],
                             w3s[buf][:, cb * I + ib * 128: cb * I + (ib + 1) * 128],
                             xln_p[:, cb * PL + tc0: cb * PL + tc0 + 512],
                             start=(cb == 0), stop=(cb == NB - 1))
                    p.lastinc("tensor", s_mm)
                    p.wait("scalar", s_mm, s_mm.v - 1)
                    if _git[0] >= 2:
                        p.wait("scalar", s_ve, _gsnap_ve[_git[0] - 2])
                    p.do("scalar", "activation", sgm[q][:, :], ps_g[q][:, :], AF.Silu, inc=s_se)
                    p.wait("vector", s_mm)
                    p.wait("vector", s_se)
                    hslice = hbuf[:, ib * PL + tc0: ib * PL + tc0 + 512]
                    p.do("vector", "tensor_mul", hslice, sgm[q][:, :], ps_u[q][:, :], inc=s_ve)
                    if ei < E:
                        p.do("vector", "tensor_mul", hslice, hslice,
                             cmbq[ei % 2][:, tc0:tc0 + 512], inc=s_ve)
                    _gsnap_se.append(s_se.v)
                    _gsnap_ve.append(s_ve.v)
                    _git[0] += 1
                    if ei >= E - 2 and hg is not None:
                        next(hg, None)
            p.wait("tensor", s_ve)
            for ob in range(NB):
                for h2 in range(2):
                    bank = _yit[0] % 2
                    tc0 = h2 * 512
                    if _yit[0] >= 2:
                        p.wait("tensor", s_ve, _ysnap[_yit[0] - 2])
                    for ib in range(NIB):
                        p.do("tensor", "matmul", ps_y[:, bank * 512:(bank + 1) * 512],
                             w2s[buf][:, ib * D + ob * 128: ib * D + ob * 128 + 128],
                             hbuf[:, ib * PL + tc0: ib * PL + tc0 + 512],
                             start=(ib == 0), stop=(ib == NIB - 1))
                    p.lastinc("tensor", s_mm)
                    p.wait("vector", s_mm)
                    ysl = yab[:, ob * PL + tc0: ob * PL + tc0 + 512]
                    if ei == 0:
                        p.do("vector", "tensor_copy", ysl, ps_y[:, bank * 512:(bank + 1) * 512], inc=s_ve)
                    else:
                        p.do("vector", "tensor_add", ysl, ysl, ps_y[:, bank * 512:(bank + 1) * 512], inc=s_ve)
                    _ysnap.append(s_ve.v)
                    _yit[0] += 1
        if hg is not None:
            for _ in hg:
                pass
        p.wait("gpsimd", s_ve)
        p.wait("gpsimd", s_st, _oxv)
        for ob in range(NB):
            p.do("gpsimd", "dma_start", out=ox_d[ob * 128:(ob + 1) * 128, pt0:pt0 + PL],
                 in_=yab[:, ob * PL:(ob + 1) * PL], accum_op=ALU.add, inc=s_st, n=16)
        p.wait("vector", s_st)
    p.wait("gpsimd", s_st)
    stack.close()
    p.barrier()


def build_stage_d():
    nc = bass.Bass("TRN2", target_bir_lowering=False, detect_race_conditions=False)
    p = Prog(nc)
    cn = _mkconsts(nc, p)
    E, I = 16, 768
    x_d = nc.dram_tensor("x", [D, TC], F32, kind="ExternalInput").ap()
    w1_d = nc.dram_tensor("w1t", [E, D, I], F32, kind="ExternalInput").ap()
    w3_d = nc.dram_tensor("w3t", [E, D, I], F32, kind="ExternalInput").ap()
    w2_d = nc.dram_tensor("w2t", [E, I, D], F32, kind="ExternalInput").ap()
    ws1_d = nc.dram_tensor("ws1t", [D, I], F32, kind="ExternalInput").ap()
    ws3_d = nc.dram_tensor("ws3t", [D, I], F32, kind="ExternalInput").ap()
    ws2_d = nc.dram_tensor("ws2t", [I, D], F32, kind="ExternalInput").ap()
    gw_d = nc.dram_tensor("gwt", [D, E], F32, kind="ExternalInput").ap()
    gb_d = nc.dram_tensor("gbias", [1, E], F32, kind="ExternalInput").ap()
    ox_d = nc.dram_tensor("ox", [D, TC], F32, kind="ExternalOutput").ap()
    w1b_d = nc.dram_tensor("w1b", [E + 1, 128, NB * I], BF16).ap()
    w3b_d = nc.dram_tensor("w3b", [E + 1, 128, NB * I], BF16).ap()
    w2b_d = nc.dram_tensor("w2b", [E + 1, 128, (I // 128) * D], BF16).ap()
    emit_stage_d(nc, p, cn, x_d, w1_d, w3_d, w2_d, ws1_d, ws3_d, ws2_d, gw_d, gb_d, ox_d,
                 w1b_d=w1b_d, w3b_d=w3b_d, w2b_d=w2b_d, prep_here=True)
    p.emit()
    return nc


def _prep_stage_d(inputs, core, x_full):
    xs = x_full[2 * core: 2 * core + 2].reshape(TC, D)
    return {
        "x": np.ascontiguousarray(xs.T),
        "w1t": np.ascontiguousarray(inputs["L1_ffn_w1"].transpose(0, 2, 1)),
        "w3t": np.ascontiguousarray(inputs["L1_ffn_w3"].transpose(0, 2, 1)),
        "w2t": np.ascontiguousarray(inputs["L1_ffn_w2"].transpose(0, 2, 1)),
        "ws1t": np.ascontiguousarray(inputs["L1_ffn_ws1"].T),
        "ws3t": np.ascontiguousarray(inputs["L1_ffn_ws3"].T),
        "ws2t": np.ascontiguousarray(inputs["L1_ffn_ws2"].T),
        "gwt": np.ascontiguousarray(inputs["L1_ffn_gate_w"].T),
        "gbias": np.ascontiguousarray(inputs["L1_ffn_gate_bias"].reshape(1, 16)),
    }


def run_stage_d(inputs, x_full, cores=None):
    cores = list(range(NCORES)) if cores is None else cores
    nc = build_stage_d()
    in_maps = [_prep_stage_d(inputs, c, x_full) for c in cores]
    res = run_bass_kernel_spmd(nc, in_maps, core_ids=list(range(len(cores))))
    return [r["ox"] for r in res.results]



def build_fused():
    nc = bass.Bass("TRN2", target_bir_lowering=False, detect_race_conditions=False)
    p = Prog(nc)
    cn = _mkconsts(nc, p)
    E, I, CH, HID = 16, 768, 4096, 3072
    x_d = nc.dram_tensor("x", [D, TC], F32, kind="ExternalInput").ap()
    out_d = nc.dram_tensor("out", [D, TC], F32, kind="ExternalOutput").ap()
    act1 = nc.dram_tensor("act1", [D, TC], F32).ap()
    act2 = nc.dram_tensor("act2", [D, TC], F32).ap()
    wqkv_d = nc.dram_tensor("wqkv", [D, 3 * D], F32, kind="ExternalInput").ap()
    wout_d = nc.dram_tensor("wout", [D, D], F32, kind="ExternalInput").ap()
    bout_d = nc.dram_tensor("boutc", [128, NB], F32, kind="ExternalInput").ap()
    wg_d = nc.dram_tensor("wgt", [D, HID], I32, kind="ExternalInput").ap()
    wu_d = nc.dram_tensor("wut", [D, HID], I32, kind="ExternalInput").ap()
    wd_d = nc.dram_tensor("wdt", [HID, D], I32, kind="ExternalInput").ap()
    wtk_d = nc.dram_tensor("wtk", [D, D], F32, kind="ExternalInput").ap()
    wtv_d = nc.dram_tensor("wtv", [D, D], F32, kind="ExternalInput").ap()
    wtr_d = nc.dram_tensor("wtr", [D, D], F32, kind="ExternalInput").ap()
    wck_d = nc.dram_tensor("wck", [D, CH], F32, kind="ExternalInput").ap()
    wcv_d = nc.dram_tensor("wcv", [CH, D], F32, kind="ExternalInput").ap()
    wcr_d = nc.dram_tensor("wcr", [D, D], F32, kind="ExternalInput").ap()
    vc_d = nc.dram_tensor("vcols", [128, 88], F32, kind="ExternalInput").ap()
    w1_d = nc.dram_tensor("w1t", [E, D, I], F32, kind="ExternalInput").ap()
    w3_d = nc.dram_tensor("w3t", [E, D, I], F32, kind="ExternalInput").ap()
    w2_d = nc.dram_tensor("w2t", [E, I, D], F32, kind="ExternalInput").ap()
    ws1_d = nc.dram_tensor("ws1t", [D, I], F32, kind="ExternalInput").ap()
    ws3_d = nc.dram_tensor("ws3t", [D, I], F32, kind="ExternalInput").ap()
    ws2_d = nc.dram_tensor("ws2t", [I, D], F32, kind="ExternalInput").ap()
    gw_d = nc.dram_tensor("gwt", [D, E], F32, kind="ExternalInput").ap()
    gb_d = nc.dram_tensor("gbias", [1, E], F32, kind="ExternalInput").ap()
    w1b_d = nc.dram_tensor("w1b", [E + 1, 128, NB * I], BF16).ap()
    w3b_d = nc.dram_tensor("w3b", [E + 1, 128, NB * I], BF16).ap()
    w2b_d = nc.dram_tensor("w2b", [E + 1, 128, (I // 128) * D], BF16).ap()
    emit_stage_a(nc, p, cn, x_d, wqkv_d, wout_d, bout_d, act1)
    emit_stage_b(nc, p, cn, act1, wg_d, wu_d, wd_d, act2,
                 prep_args=(w1_d, w3_d, w2_d, ws1_d, ws3_d, ws2_d, w1b_d, w3b_d, w2b_d))
    emit_stage_c(nc, p, cn, act2, wtk_d, wtv_d, wtr_d, wck_d, wcv_d, wcr_d, vc_d, act1)
    emit_stage_d(nc, p, cn, act1, w1_d, w3_d, w2_d, ws1_d, ws3_d, ws2_d, gw_d, gb_d, out_d,
                 w1b_d=w1b_d, w3b_d=w3b_d, w2b_d=w2b_d, prep_here=False)
    p.emit()
    return nc


def _prep_fused(inputs, core):
    xs = inputs["L0_attn_x"][2 * core: 2 * core + 2].reshape(TC, D)
    m = {"x": np.ascontiguousarray(xs.T)}
    m.update({
        "wqkv": np.ascontiguousarray(inputs["L0_attn_Wqkv"]),
        "wout": np.ascontiguousarray(inputs["L0_attn_Wout"]),
        "boutc": np.ascontiguousarray(inputs["L0_attn_bout"].reshape(NB, 128).T),
        "wgt": np.ascontiguousarray(inputs["L0_ffn_w_gate"].T),
        "wut": np.ascontiguousarray(inputs["L0_ffn_w_up"].T),
        "wdt": np.ascontiguousarray(inputs["L0_ffn_w_down"].T),
    })
    pc = _prep_stage_c(inputs, 0, None) if False else None
    def col8(name):
        return inputs[name].astype(np.float32).reshape(NB, 128).T
    vcols = np.concatenate([
        col8("L1_attn_time_decay"), col8("L1_attn_time_first"),
        col8("L1_attn_ln1_g"), col8("L1_attn_ln1_b"),
        col8("L1_attn_ln2_g"), col8("L1_attn_ln2_b"),
        inputs["L1_attn_mix_k"].astype(np.float32).reshape(NB, 128).T,
        inputs["L1_attn_mix_v"].astype(np.float32).reshape(NB, 128).T,
        inputs["L1_attn_mix_r"].astype(np.float32).reshape(NB, 128).T,
        inputs["L1_attn_cmix_k"].astype(np.float32).reshape(NB, 128).T,
        inputs["L1_attn_cmix_r"].astype(np.float32).reshape(NB, 128).T,
    ], axis=1)
    m.update({
        "wtk": np.ascontiguousarray(inputs["L1_attn_w_tk"]),
        "wtv": np.ascontiguousarray(inputs["L1_attn_w_tv"]),
        "wtr": np.ascontiguousarray(inputs["L1_attn_w_tr"]),
        "wck": np.ascontiguousarray(inputs["L1_attn_w_ck"]),
        "wcv": np.ascontiguousarray(inputs["L1_attn_w_cv"]),
        "wcr": np.ascontiguousarray(inputs["L1_attn_w_cr"]),
        "vcols": np.ascontiguousarray(vcols),
        "w1t": np.ascontiguousarray(inputs["L1_ffn_w1"].transpose(0, 2, 1)),
        "w3t": np.ascontiguousarray(inputs["L1_ffn_w3"].transpose(0, 2, 1)),
        "w2t": np.ascontiguousarray(inputs["L1_ffn_w2"].transpose(0, 2, 1)),
        "ws1t": np.ascontiguousarray(inputs["L1_ffn_ws1"].T),
        "ws3t": np.ascontiguousarray(inputs["L1_ffn_ws3"].T),
        "ws2t": np.ascontiguousarray(inputs["L1_ffn_ws2"].T),
        "gwt": np.ascontiguousarray(inputs["L1_ffn_gate_w"].T),
        "gbias": np.ascontiguousarray(inputs["L1_ffn_gate_bias"].reshape(1, 16)),
    })
    return m


def kernel(**inputs):
    cores = list(range(NCORES))
    nc = build_fused()
    in_maps = [_prep_fused(inputs, c) for c in cores]
    res = run_bass_kernel_spmd(nc, in_maps, core_ids=cores)
    out = np.empty((16, T, D), dtype=np.float32)
    for c in cores:
        out[2 * c: 2 * c + 2] = np.asarray(res.results[c]["out"]).T.reshape(BPC, T, D)
    return out
